# version 01
# baseline (speedup 1.0000x reference)
import contextlib
import numpy as np

import concourse.bass as bass
import concourse.mybir as mybir
from concourse.bass_utils import run_bass_kernel_spmd

F32 = mybir.dt.float32
U32 = mybir.dt.uint32
AF = mybir.ActivationFunctionType
OP = mybir.AluOpType
AX = mybir.AxisListType

D, T, FF, NH, DH, E = 1024, 4096, 4096, 16, 64, 8
DC = D // 128
TBN = T // 512
NTT = T // 128
EPS = 1e-5
N_CORES = 8

ENGINES = ('sync', 'gpsimd', 'tensor', 'scalar', 'vector')


class Prog:

    NPOOL = 6

    def __init__(self):
        self.streams = {e: [] for e in ENGINES}
        self.cnt = dict(st=0, ss=0, sv=0, sg=0)
        for q in ('dl', 'ds', 'dg'):
            for i in range(self.NPOOL):
                self.cnt[f'{q}{i}'] = 0
        self.rr = dict(dl=0, ds=0, dg=0)
        self.base = {}

    def _dma(self, eng, q, deps, fn):
        i = self.rr[q]
        self.rr[q] = (i + 1) % self.NPOOL
        sem = f'{q}{i}'
        prev = self.cnt[sem]
        dd = list(deps)
        if prev:
            dd.append((sem, prev))
        return self._op(eng, sem, True, dd, fn)

    def set_base(self, toks):
        self.base = {}
        for t in toks:
            if t is None:
                continue
            s, c = t
            if c > self.base.get(s, 0):
                self.base[s] = c

    def _op(self, eng, sem, dma, deps, fn):
        tok = None
        if sem is not None:
            self.cnt[sem] += 16 if dma else 1
            tok = (sem, self.cnt[sem])
        dd = dict(self.base)
        for d in deps:
            if d is None:
                continue
            s, c = d
            if c > dd.get(s, 0):
                dd[s] = c
        self.streams[eng].append((fn, dd, sem, 16 if dma else 1))
        return tok

    def load(self, dst, src, deps=(), slow=False):
        if slow:
            return self._dma('sync', 'dl', deps,
                             lambda h, a=dst, b=src: h.dma_start(
                                 out=a, in_=b, allow_slow_non_contiguous=True))
        return self._dma('sync', 'dl', deps,
                         lambda h, a=dst, b=src: h.dma_start(out=a, in_=b))

    def store(self, dst, src, deps=()):
        return self._dma('scalar', 'ds', deps,
                         lambda h, a=dst, b=src: h.dma_start(out=a, in_=b))

    def gstore(self, dst, src, deps=(), accum=False):
        if accum:
            return self._dma('gpsimd', 'dg', deps,
                             lambda h, a=dst, b=src:
                                 h.dma_start(out=a, in_=b, accum_op=OP.add))
        return self._dma('gpsimd', 'dg', deps,
                         lambda h, a=dst, b=src: h.dma_start(out=a, in_=b))

    def gather(self, dst, src_rows, idx, deps=()):
        return self._dma('gpsimd', 'dg', deps,
                         lambda h, a=dst, b=src_rows, i=idx: h.indirect_dma_start(
                             out=a, out_offset=None, in_=b,
                             in_offset=bass.IndirectOffsetOnAxis(ap=i, axis=0)))

    def mm(self, out, stat, mov, start, stop, deps=(), inc=False):
        return self._op('tensor', 'st' if inc else None, False, deps,
                        lambda h, o=out, s=stat, m=mov, a=start, b=stop:
                            h.matmul(o, s, m, start=a, stop=b))

    def transp(self, out, in_, ident, deps=(), inc=True):
        return self._op('tensor', 'st' if inc else None, False, deps,
                        lambda h, o=out, i=in_, d=ident: h.transpose(o, i, d))

    def act(self, out, in_, func, deps=(), bias=None, scale=1.0):
        if bias is None:
            return self._op('scalar', 'ss', False, deps,
                            lambda h, o=out, i=in_, f=func, sc=scale:
                                h.activation(o, i, f, scale=sc))
        return self._op('scalar', 'ss', False, deps,
                        lambda h, o=out, i=in_, f=func, b=bias, sc=scale:
                            h.activation(o, i, f, bias=b, scale=sc))

    def v(self, name, *args, deps=(), **kw):
        return self._op('vector', 'sv', False, deps,
                        lambda h, n=name, a=args, k=kw: getattr(h, n)(*a, **k))

    def g(self, name, *args, deps=(), **kw):
        return self._op('gpsimd', 'sg', False, deps,
                        lambda h, n=name, a=args, k=kw: getattr(h, n)(*a, **k))

    def snap(self):
        return [(s, c) for s, c in self.cnt.items() if c > 0]

    def emit(self, block, sems):
        for eng in ENGINES:
            items = self.streams[eng]
            if not items:
                continue

            def mk(items=items):
                def body(h):
                    hw = {}
                    for fn, dd, sem, amt in items:
                        for s, c in sorted(dd.items()):
                            if c > hw.get(s, 0):
                                h.wait_ge(sems[s], c)
                                hw[s] = c
                        inst = fn(h)
                        if sem is not None:
                            inst.then_inc(sems[sem], amt)
                return body
            getattr(block, eng)(mk())


def build_program(mode='full'):
    nc = bass.Bass('TRN2', target_bir_lowering=False)
    x_in = nc.dram_tensor('x', [T, D], F32, kind='ExternalInput').ap()
    w1a = nc.dram_tensor('w1a', [D, FF], F32, kind='ExternalInput').ap()
    b1a = nc.dram_tensor('b1a', [1, FF], F32, kind='ExternalInput').ap()
    w2a = nc.dram_tensor('w2a', [D, D], F32, kind='ExternalInput').ap()
    b2a = nc.dram_tensor('b2a', [1, D], F32, kind='ExternalInput').ap()
    gam = nc.dram_tensor('gam', [1, D], F32, kind='ExternalInput').ap()
    bet = nc.dram_tensor('bet', [1, D], F32, kind='ExternalInput').ap()
    wr = nc.dram_tensor('wr', [D, E], F32, kind='ExternalInput').ap()
    w1m = nc.dram_tensor('w1m', [E, D, FF], F32, kind='ExternalInput').ap()
    b1m = nc.dram_tensor('b1m', [E, FF], F32, kind='ExternalInput').ap()
    w2m = nc.dram_tensor('w2m', [E, FF, D], F32, kind='ExternalInput').ap()
    b2m = nc.dram_tensor('b2m', [E, D], F32, kind='ExternalInput').ap()
    out = nc.dram_tensor('out', [T, D], F32, kind='ExternalOutput').ap()

    x_fm = nc.dram_tensor('x_fm', [D, T], F32).ap()
    ln_fm = nc.dram_tensor('ln_fm', [D, T], F32).ap()
    y_acc = nc.dram_tensor('y_acc', [D, T], F32).ap()
    qk_fm = nc.dram_tensor('qk_fm', [2 * D, T], F32).ap()
    uv_tm = nc.dram_tensor('uv_tm', [T, 2 * D], F32).ap()
    avu_tm = nc.dram_tensor('avu_tm', [T, D], F32).ap()

    w1rows = w1m.rearrange('e d (q c) -> (e d q) c', c=1024)
    w2rows = w2m.rearrange('e f d -> (e f) d')

    def fmc(dram, t0, w):
        return dram[:, t0:t0 + w].rearrange('(c p) w -> p c w', p=128)

    def sb3(ap, w):
        return ap.rearrange('p (c w) -> p c w', w=w)

    P = Prog()
    uid = [0]

    def sbt(sc_, name, shape, dtype=F32):
        uid[0] += 1
        return sc_.enter_context(
            nc.sbuf_tensor(f'{name}_{uid[0]}', shape, dtype)).ap()

    def pst(sc_, name, shape):
        uid[0] += 1
        return sc_.enter_context(
            nc.psum_tensor(f'{name}_{uid[0]}', shape, F32)).ap()

    es = contextlib.ExitStack()
    sems = {s: es.enter_context(nc.semaphore(s)) for s in P.cnt}

    ident = nc.alloc_sbuf_tensor('ident', [128, 128], F32).ap()
    iota128 = nc.alloc_sbuf_tensor('iota128', [128, 1], F32).ap()
    iota4 = nc.alloc_sbuf_tensor('iota4', [128, 1], F32).ap()
    iota8 = nc.alloc_sbuf_tensor('iota8', [1, 8], F32).ap()
    ones_row = nc.alloc_sbuf_tensor('ones_row', [1, 128], F32).ap()
    ones_col = nc.alloc_sbuf_tensor('ones_col', [128, 1], F32).ap()
    pooled = nc.alloc_sbuf_tensor('pooled', [128, 16], F32).ap()
    gcol = nc.alloc_sbuf_tensor('gcol', [128, 4], F32).ap()
    b1cols = nc.alloc_sbuf_tensor('b1cols', [128, 64], F32).ap()
    b2comb = nc.alloc_sbuf_tensor('b2comb', [128, 8], F32).ap()
    idxw1 = nc.alloc_sbuf_tensor('idxw1', [128, 64], U32).ap()
    idxw2 = nc.alloc_sbuf_tensor('idxw2', [128, 64], U32).ap()
    bnbuf = nc.alloc_sbuf_tensor('bnbuf', [128, NTT * NH * 6], F32).ap()
    bnagg = nc.alloc_sbuf_tensor('bnagg', [128, NTT * 2], F32).ap()
    rstdc = nc.alloc_sbuf_tensor('rstdc', [128, NTT], F32).ap()
    sdc = nc.alloc_sbuf_tensor('sdc', [128, NTT], F32).ap()

    c0 = P.g('memset', ident[:], 0.0)
    c1 = P.g('affine_select', out=ident[:], in_=ident[:],
             compare_op=OP.not_equal, fill=1.0, base=0,
             pattern=[[-1, 128]], channel_multiplier=1, deps=[c0])
    c2 = P.g('iota', iota128[:], [[0, 1]], channel_multiplier=1,
             allow_small_or_imprecise_dtypes=True, deps=[c1])
    c3 = P.g('iota', iota8[:], [[1, 8]], channel_multiplier=0,
             allow_small_or_imprecise_dtypes=True, deps=[c2])
    c4 = P.g('memset', ones_row[:], 1.0, deps=[c3])
    c5 = P.g('memset', ones_col[:], 1.0, deps=[c4])
    c6 = P.v('tensor_scalar', iota4[:], iota128[:], 4.0, None, OP.mult,
             deps=[c5])
    P.set_base([c6])

    def phase_in_transpose():
        sc = contextlib.ExitStack()
        stage = [sbt(sc, f'p0stg{i}', [128, T])
                 for i in range(DC)]
        xin = [sbt(sc, f'p0xin{i}', [128, D])
               for i in range(2)]
        ptr = [pst(sc, f'p0ps{i}', [128, 128])
               for i in range(2)]
        xin_free = [None, None]
        ptr_free = [None, None]
        last_cp = None
        i = 0
        for tt in range(NTT):
            b = tt % 2
            ld = P.load(xin[b][:], x_in[tt * 128:(tt + 1) * 128, :],
                        deps=[xin_free[b]])
            for dc in range(DC):
                pb = i % 2
                tr = P.transp(ptr[pb][:], xin[b][:, dc * 128:(dc + 1) * 128],
                              ident[:], deps=[ld, ptr_free[pb]])
                cp = P.v('tensor_copy', stage[dc][:, tt * 128:(tt + 1) * 128],
                         ptr[pb][:], deps=[tr])
                ptr_free[pb] = cp
                last_cp = cp
                i += 1
            xin_free[b] = last_cp
        for dc in range(DC):
            P.store(x_fm[dc * 128:(dc + 1) * 128, :], stage[dc][:],
                    deps=[last_cp])
        sc.close()

    def phase_ln(src, dst, want_pooled):
        sc = contextlib.ExitStack()
        xs = [sbt(sc, f'lnxs{i}', [128, DC * 512])
              for i in range(2)]
        xq = [sbt(sc, f'lnxq{i}', [128, DC * 512])
              for i in range(2)]
        tp = [sbt(sc, f'lntp{i}', [128, DC * 512])
              for i in range(2)]
        lo = [sbt(sc, f'lnlo{i}', [128, DC * 512])
              for i in range(2)]
        mu_s = sbt(sc, 'ln_mu', [1, 512])
        ms_s = sbt(sc, 'ln_ms', [1, 512])
        m2_s = sbt(sc, 'ln_m2', [1, 512])
        va_s = sbt(sc, 'ln_va', [1, 512])
        sd_s = sbt(sc, 'ln_sd', [1, 512])
        rs_s = sbt(sc, 'ln_rs', [1, 512])
        plr = sbt(sc, 'ln_plr', [128, 1])
        p_sum = pst(sc, 'ln_psum', [1, 512])
        p_sq = pst(sc, 'ln_psq', [1, 512])
        p_mu = pst(sc, 'ln_pmu', [128, 512])
        p_rs = pst(sc, 'ln_prs', [128, 512])

        xs_free = [None, None]
        xq_free = [None, None]
        tp_free = [None, None]
        lo_free = [None, None]
        war_psum = war_psq = war_pmu = war_prs = None
        pool_par = [0] * DC
        pool_tok = [None] * DC
        plr_free = None
        if want_pooled:
            z0 = P.v('memset', pooled[:], 0.0)
            pool_tok = [z0] * DC
        for tb in range(TBN):
            b = tb % 2
            ld = P.load(sb3(xs[b][:], 512), fmc(src, tb * 512, 512),
                        deps=[xs_free[b]])
            sq = None
            for dc in range(DC):
                sq = P.act(xq[b][:, dc * 512:(dc + 1) * 512],
                           xs[b][:, dc * 512:(dc + 1) * 512], AF.Square,
                           deps=[ld, xq_free[b]])
            tsum = tsq = None
            for dc in range(DC):
                tsum = P.mm(p_sum[:], ones_col[:],
                            xs[b][:, dc * 512:(dc + 1) * 512],
                            dc == 0, dc == DC - 1,
                            deps=[ld, war_psum], inc=(dc == DC - 1))
            for dc in range(DC):
                tsq = P.mm(p_sq[:], ones_col[:],
                           xq[b][:, dc * 512:(dc + 1) * 512],
                           dc == 0, dc == DC - 1,
                           deps=[sq, war_psq], inc=(dc == DC - 1))
            u1 = P.v('tensor_scalar', mu_s[:], p_sum[:], 1.0 / D, None, OP.mult,
                     deps=[tsum])
            u2 = P.v('tensor_scalar', ms_s[:], p_sq[:], 1.0 / D, EPS,
                     OP.mult, OP.add, deps=[tsq])
            u3 = P.v('tensor_tensor', m2_s[:], mu_s[:], mu_s[:], OP.mult,
                     deps=[u1])
            u4 = P.v('tensor_tensor', va_s[:], ms_s[:], m2_s[:], OP.subtract,
                     deps=[u2, u3])
            u5 = P.act(sd_s[:], va_s[:], AF.Sqrt, deps=[u4])
            u6 = P.v('reciprocal', rs_s[:], sd_s[:], deps=[u5])
            war_psum = u1
            war_psq = u2
            bmu = P.mm(p_mu[:], ones_row[:], mu_s[:], True, True,
                       deps=[u3, war_pmu], inc=True)
            brs = P.mm(p_rs[:], ones_row[:], rs_s[:], True, True,
                       deps=[u6, war_prs], inc=True)
            last_n = None
            for dc in range(DC):
                sl = slice(dc * 512, (dc + 1) * 512)
                n1 = P.v('tensor_tensor', tp[b][:, sl], xs[b][:, sl], p_mu[:],
                         OP.subtract, deps=[bmu, ld, tp_free[b]])
                n2 = P.v('tensor_tensor', lo[b][:, sl], tp[b][:, sl], p_rs[:],
                         OP.mult, deps=[n1, brs, lo_free[b]])
                last_n = n2
                if want_pooled:
                    r1 = P.v('tensor_reduce', plr[:], lo[b][:, sl], AX.X,
                             OP.add, deps=[n2, plr_free])
                    par = pool_par[dc]
                    r2 = P.v('tensor_tensor',
                             pooled[:, dc * 2 + 1 - par:dc * 2 + 2 - par],
                             pooled[:, dc * 2 + par:dc * 2 + par + 1],
                             plr[:], OP.add, deps=[r1, pool_tok[dc]])
                    pool_par[dc] = 1 - par
                    pool_tok[dc] = r2
                    plr_free = r2
            war_pmu = last_n
            war_prs = last_n
            st = P.store(fmc(dst, tb * 512, 512), sb3(lo[b][:], 512),
                         deps=[last_n])
            xs_free[b] = last_n
            xq_free[b] = tsq
            tp_free[b] = last_n
            lo_free[b] = st
        sc.close()
        pool_cols = [dc * 2 + pool_par[dc] for dc in range(DC)]
        return pool_cols, pool_tok

    def phase_routing(pool_cols, pool_tok):
        sc = contextlib.ExitStack()
        wr_sb = sbt(sc, 'rt_wr', [128, 64])
        lg = sbt(sc, 'rt_lg', [1, 8])
        lg2 = sbt(sc, 'rt_lg2', [1, 8])
        mk1 = sbt(sc, 'rt_mk1', [1, 8])
        mk2 = sbt(sc, 'rt_mk2', [1, 8])
        m1 = sbt(sc, 'rt_m1', [1, 1])
        m2 = sbt(sc, 'rt_m2', [1, 1])
        ddf = sbt(sc, 'rt_dd', [1, 1])
        dsc_ = sbt(sc, 'rt_dsc', [1, 1])
        e2v = sbt(sc, 'rt_e2', [1, 1])
        den = sbt(sc, 'rt_den', [1, 1])
        rec = sbt(sc, 'rt_rec', [1, 1])
        w2v = sbt(sc, 'rt_w2v', [1, 1])
        tm8a = sbt(sc, 'rt_tm8a', [1, 8])
        tm8b = sbt(sc, 'rt_tm8b', [1, 8])
        i1 = sbt(sc, 'rt_i1', [1, 1])
        i2 = sbt(sc, 'rt_i2', [1, 1])
        pack = sbt(sc, 'rt_pack', [1, 4])
        stripe = sbt(sc, 'rt_eb', [1, 2])
        ebf = sbt(sc, 'rt_ebf', [2, 1])
        ebu = sbt(sc, 'rt_ebu', [2, 1], U32)
        b1s = sbt(sc, 'rt_b1s', [2, FF])
        b2s = sbt(sc, 'rt_b2s', [2, D])
        b2c = sbt(sc, 'rt_b2c', [128, 16])
        tcol = sbt(sc, 'rt_tcol', [128, 1])
        fst = [sbt(sc, f'rt_fst{i}', [128, 1])
               for i in range(2)]
        base1 = [sbt(sc, f'rt_ba{k}', [128, 1])
                 for k in range(2)]
        base2 = [sbt(sc, f'rt_bb{k}', [128, 1])
                 for k in range(2)]
        p_lg = pst(sc, 'rt_plg', [1, 8])
        p_g = pst(sc, 'rt_pg', [128, 4])
        p_eb = pst(sc, 'rt_peb', [2, 1])
        p_t = [pst(sc, f'rt_pt{i}', [128, 2])
               for i in range(2)]

        wl = P.load(sb3(wr_sb[:], 8), wr.rearrange('(c p) e -> p c e', p=128))
        tl = None
        for dc in range(DC):
            col = pool_cols[dc]
            tl = P.mm(p_lg[:], pooled[:, col:col + 1],
                      wr_sb[:, dc * 8:(dc + 1) * 8], dc == 0, dc == DC - 1,
                      deps=[wl, pool_tok[dc]], inc=(dc == DC - 1))
        a0 = P.v('tensor_copy', lg[:], p_lg[:], deps=[tl])
        a1 = P.v('reduce_max', m1[:], lg[:], AX.X, deps=[a0])
        a2 = P.v('tensor_scalar', mk1[:], lg[:], m1[:], None, OP.is_ge,
                 deps=[a1])
        a3 = P.v('scalar_tensor_tensor', out=lg2[:], in0=mk1[:], scalar=-1e30,
                 in1=lg[:], op0=OP.mult, op1=OP.add, deps=[a2])
        a4 = P.v('reduce_max', m2[:], lg2[:], AX.X, deps=[a3])
        a5 = P.v('tensor_scalar', mk2[:], lg2[:], m2[:], None, OP.is_ge,
                 deps=[a4])
        a6 = P.v('tensor_tensor', ddf[:], m2[:], m1[:], OP.subtract,
                 deps=[a4, a1])
        a7 = P.v('tensor_scalar', dsc_[:], ddf[:], 1.0 / T, None, OP.mult,
                 deps=[a6])
        a8 = P.act(e2v[:], dsc_[:], AF.Exp, deps=[a7])
        a9 = P.v('tensor_scalar', den[:], e2v[:], 1.0, None, OP.add, deps=[a8])
        a10 = P.v('reciprocal', rec[:], den[:], deps=[a9])
        a11 = P.v('tensor_tensor', w2v[:], e2v[:], rec[:], OP.mult,
                  deps=[a8, a10])
        a12 = P.v('tensor_scalar', pack[:, 0:1], rec[:], 0.5, None, OP.mult,
                  deps=[a10])
        a13 = P.v('tensor_scalar', pack[:, 1:2], w2v[:], 0.5, None, OP.mult,
                  deps=[a11])
        a14 = P.v('tensor_tensor', tm8a[:], mk1[:], iota8[:], OP.mult,
                  deps=[a2])
        a15 = P.v('tensor_reduce', i1[:], tm8a[:], AX.X, OP.add, deps=[a14])
        a16 = P.v('tensor_tensor', tm8b[:], mk2[:], iota8[:], OP.mult,
                  deps=[a5])
        a17 = P.v('tensor_reduce', i2[:], tm8b[:], AX.X, OP.add, deps=[a16])
        a18 = P.v('tensor_copy', pack[:, 2:3], i1[:], deps=[a15])
        a19 = P.v('tensor_copy', pack[:, 3:4], i2[:], deps=[a17])
        bg = P.mm(p_g[:], ones_row[:], pack[:], True, True,
                  deps=[a12, a13, a18, a19], inc=True)
        g0 = P.v('tensor_copy', gcol[:], p_g[:], deps=[bg])
        s0 = P.v('tensor_copy', stripe[:, 0:1], i1[:], deps=[a15])
        s1 = P.v('tensor_copy', stripe[:, 1:2], i2[:], deps=[a17])
        te = P.transp(p_eb[:], stripe[:], ident[0:1, 0:1], deps=[s0, s1])
        s2 = P.v('tensor_copy', ebf[:], p_eb[:], deps=[te])
        s3 = P.v('tensor_copy', ebu[:], ebf[:], deps=[s2])
        gb1 = P.gather(b1s[:], b1m, ebu[:], deps=[s3])
        gb2 = P.gather(b2s[:], b2m, ebu[:], deps=[s3])
        ptf = [te, te]
        b1c_tok = None
        for ft in range(32):
            pb = ft % 2
            tr = P.transp(p_t[pb][:], b1s[0:2, ft * 128:(ft + 1) * 128],
                          ident[0:2, 0:2], deps=[gb1, ptf[pb]])
            cp = P.v('tensor_copy', b1cols[:, ft * 2:ft * 2 + 2], p_t[pb][:],
                     deps=[tr])
            ptf[pb] = cp
            b1c_tok = cp
        b2c_tok = None
        for dc in range(DC):
            pb = dc % 2
            tr = P.transp(p_t[pb][:], b2s[0:2, dc * 128:(dc + 1) * 128],
                          ident[0:2, 0:2], deps=[gb2, ptf[pb]])
            cp = P.v('tensor_copy', b2c[:, dc * 2:dc * 2 + 2], p_t[pb][:],
                     deps=[tr])
            ptf[pb] = cp
            b2c_tok = cp
        b2cb_tok = None
        tcol_tok = None
        for dc in range(DC):
            q1 = P.v('tensor_scalar', tcol[:], b2c[:, dc * 2:dc * 2 + 1],
                     gcol[:, 0:1], None, OP.mult, deps=[b2c_tok, g0, tcol_tok])
            q2 = P.v('scalar_tensor_tensor', out=b2comb[:, dc:dc + 1],
                     in0=b2c[:, dc * 2 + 1:dc * 2 + 2], scalar=gcol[:, 1:2],
                     in1=tcol[:], op0=OP.mult, op1=OP.add, deps=[q1])
            tcol_tok = q2
            b2cb_tok = q2
        idx_tok = None
        fs_tok = [g0, g0]
        for k in range(2):
            bb1 = P.v('scalar_tensor_tensor', out=base1[k],
                      in0=gcol[:, 2 + k:3 + k], scalar=4096.0, in1=iota4[:],
                      op0=OP.mult, op1=OP.add, deps=[g0])
            bb2 = P.v('scalar_tensor_tensor', out=base2[k],
                      in0=gcol[:, 2 + k:3 + k], scalar=4096.0, in1=iota128[:],
                      op0=OP.mult, op1=OP.add, deps=[g0])
            for fq in range(4):
                for dc in range(DC):
                    col = k * 32 + fq * 8 + dc
                    fb_ = col % 2
                    f1 = P.v('tensor_scalar', fst[fb_][:], base1[k],
                             float(512 * dc + fq), None, OP.add,
                             deps=[bb1, fs_tok[fb_]])
                    f2 = P.v('tensor_copy', idxw1[:, col:col + 1], fst[fb_][:],
                             deps=[f1])
                    fs_tok[fb_] = f2
                for fb in range(8):
                    col = k * 32 + fq * 8 + fb
                    fb_ = col % 2
                    f1 = P.v('tensor_scalar', fst[fb_][:], base2[k],
                             float(fq * 1024 + fb * 128), None, OP.add,
                             deps=[bb2, fs_tok[fb_]])
                    f2 = P.v('tensor_copy', idxw2[:, col:col + 1], fst[fb_][:],
                             deps=[f1])
                    fs_tok[fb_] = f2
                    idx_tok = f2
        sc.close()
        return dict(gcol=g0, b1c=b1c_tok, b2cb=b2cb_tok, idx=idx_tok)

    def phase_ffn(rt, first_call):
        sc = contextlib.ExitStack()
        w1q = [sbt(sc, f'ff_w1q{i}', [128, DC * 1024])
               for i in range(2)]
        w2q = sbt(sc, 'ff_w2q', [128, 8 * 1024])
        lns = [sbt(sc, f'ff_lns{i}', [128, DC * 512])
               for i in range(2)]
        hbf = sbt(sc, 'ff_h', [128, 8 * 512])
        ysb = [sbt(sc, f'ff_ysb{i}', [128, 4 * 512])
               for i in range(2)]
        ph = [pst(sc, f'ff_ph{i}', [128, 512])
              for i in range(2)]
        py = [pst(sc, f'ff_py{i}', [128, 512])
              for i in range(4)]

        w1_tok = [None, None]
        w1_free = [None, None]
        w2_tok = None
        w2_free = None
        lns_free = [None, None]
        h_tok = [None] * 8
        h_free = None
        ph_free = [None, None]
        py_free = [None] * 4
        ysb_free = [None, None]
        y_chain = [None] * TBN

        iters = [(k, fq) for k in range(2) for fq in range(4)]

        def issue_w1(it_i):
            k, fq = iters[it_i]
            slot = it_i % 2
            toks = []
            for dc in range(DC):
                col = k * 32 + fq * 8 + dc
                toks.append(P.gather(
                    w1q[slot][:, dc * 1024:(dc + 1) * 1024], w1rows,
                    idxw1[:, col:col + 1],
                    deps=[rt['idx'], w1_free[slot]]))
            w1_tok[slot] = toks[-1]

        issue_w1(0)
        for it_i, (k, fq) in enumerate(iters):
            slot = it_i % 2
            toks = []
            for fb in range(8):
                col = k * 32 + fq * 8 + fb
                toks.append(P.gather(
                    w2q[:, fb * 1024:(fb + 1) * 1024], w2rows,
                    idxw2[:, col:col + 1], deps=[rt['idx'], w2_free]))
            w2_tok = toks[-1]
            if it_i + 1 < len(iters):
                issue_w1(it_i + 1)
            for tb in range(TBN):
                lb = tb % 2
                ld = P.load(sb3(lns[lb][:], 512), fmc(ln_fm, tb * 512, 512),
                            deps=[lns_free[lb]])
                for ft in range(8):
                    pb = ft % 2
                    g = None
                    for dc in range(DC):
                        g = P.mm(ph[pb][:],
                                 w1q[slot][:, dc * 1024 + ft * 128:
                                            dc * 1024 + (ft + 1) * 128],
                                 lns[lb][:, dc * 512:(dc + 1) * 512],
                                 dc == 0, dc == DC - 1,
                                 deps=[w1_tok[slot], ld, ph_free[pb]],
                                 inc=(dc == DC - 1))
                    h_tok[ft] = P.act(hbf[:, ft * 512:(ft + 1) * 512], ph[pb][:],
                                      AF.Gelu,
                                      bias=b1cols[:, (fq * 8 + ft) * 2 + k:
                                                  (fq * 8 + ft) * 2 + k + 1],
                                      deps=[g, rt['b1c'], h_free])
                    ph_free[pb] = h_tok[ft]
                for grp in range(2):
                    gt = None
                    for fb in range(8):
                        for dtl in range(4):
                            dt = grp * 4 + dtl
                            gt = P.mm(py[dtl][:],
                                      w2q[:, fb * 1024 + dt * 128:
                                           fb * 1024 + (dt + 1) * 128],
                                      hbf[:, fb * 512:(fb + 1) * 512],
                                      fb == 0, fb == 7,
                                      deps=[h_tok[fb], w2_tok, py_free[dtl]],
                                      inc=(fb == 7 and dtl == 3))
                    yb = ysb[grp]
                    ev = None
                    for dtl in range(4):
                        ev = P.v('tensor_scalar',
                                 yb[:, dtl * 512:(dtl + 1) * 512], py[dtl][:],
                                 gcol[:, k:k + 1], None, OP.mult,
                                 deps=[gt, rt['gcol'], ysb_free[grp]])
                        py_free[dtl] = ev
                    dst = y_acc[grp * 512:(grp + 1) * 512,
                                tb * 512:(tb + 1) * 512].rearrange(
                                    '(d p) w -> d p w', p=128).transpose([1, 0, 2])
                    srcv = yb[:].rearrange('p (d w) -> p d w', w=512)
                    y_chain[tb] = P.gstore(dst, srcv,
                                           deps=[ev, y_chain[tb]],
                                           accum=not first_call_iter(it_i))
                    ysb_free[grp] = y_chain[tb]
                h_free = gt
                lns_free[lb] = gt
            w1_free[slot] = P.snap_tensor_token()
            w2_free = w1_free[slot]

        sc.close()
        P.set_base(P.snap())
        sc = contextlib.ExitStack()
        ysb2 = [sbt(sc, f'fc_y{i}', [128, DC * 512])
                for i in range(2)]
        xsb2 = [sbt(sc, f'fc_x{i}', [128, DC * 512])
                for i in range(2)]
        tsb2 = [sbt(sc, f'fc_t{i}', [128, DC * 512])
                for i in range(2)]
        y2f = [None, None]
        x2f = [None, None]
        t2f = [None, None]
        for tb in range(TBN):
            b = tb % 2
            ly = P.load(sb3(ysb2[b][:], 512), fmc(y_acc, tb * 512, 512),
                        deps=[y2f[b], y_chain[tb]])
            lx = P.load(sb3(xsb2[b][:], 512), fmc(x_fm, tb * 512, 512),
                        deps=[x2f[b]])
            last = None
            for dc in range(DC):
                sl = slice(dc * 512, (dc + 1) * 512)
                q1 = P.v('tensor_scalar', tsb2[b][:, sl], ysb2[b][:, sl],
                         b2comb[:, dc:dc + 1], None, OP.add,
                         deps=[ly, rt['b2cb'], t2f[b]])
                q2 = P.v('tensor_tensor', ysb2[b][:, sl], tsb2[b][:, sl],
                         xsb2[b][:, sl], OP.add, deps=[q1, lx])
                last = q2
            st = P.store(fmc(x_fm, tb * 512, 512), sb3(ysb2[b][:], 512),
                         deps=[last])
            y2f[b] = st
            x2f[b] = last
            t2f[b] = last
        sc.close()

    def first_call_iter(it_i):
        return it_i == 0

    def snap_tensor_token():
        c = P.cnt['st']
        return ('st', c) if c > 0 else None
    P.snap_tensor_token = snap_tensor_token

    def phase_xproj():
        sc = contextlib.ExitStack()
        lns = [sbt(sc, f'xp_ln{i}', [128, DC * 512])
               for i in range(2)]
        wct = [sbt(sc, f'xp_wct{i}', [128, DC * 128])
               for i in range(2)]
        wuv = [sbt(sc, f'xp_wuv{i}', [128, DC * 512])
               for i in range(2)]
        qst = [sbt(sc, f'xp_qst{i}', [128, 512])
               for i in range(2)]
        ust = [sbt(sc, f'xp_ust{i}', [128, 512])
               for i in range(2)]
        vst = [sbt(sc, f'xp_vst{i}', [128, 512])
               for i in range(2)]
        b1a_sb = sbt(sc, 'xp_b1a', [1, FF])
        b1qk = sbt(sc, 'xp_b1qk', [128, 16])
        pp = [pst(sc, f'xp_pp{i}', [128, 512])
              for i in range(2)]
        p_t = [pst(sc, f'xp_pt{i}', [128, 1])
               for i in range(2)]

        lb1 = P.load(b1a_sb[:], b1a)
        ptf = [None, None]
        bqk_tok = None
        for ct in range(16):
            pb = ct % 2
            tr = P.transp(p_t[pb][:],
                          b1a_sb[0:1, 2048 + ct * 128:2048 + (ct + 1) * 128],
                          ident[0:1, 0:1], deps=[lb1, ptf[pb]])
            cp = P.v('tensor_copy', b1qk[:, ct:ct + 1], p_t[pb][:], deps=[tr])
            ptf[pb] = cp
            bqk_tok = cp

        lns_free = [None, None]
        wct_tok = [None, None]
        wct_free = [None, None]
        wuv_tok = [None, None]
        wuv_free = [None, None]
        pp_free = [None, None]
        qst_free = [None, None]
        ust_free = [None, None]
        vst_free = [None, None]
        for tb in range(TBN):
            lb = tb % 2
            ld = P.load(sb3(lns[lb][:], 512), fmc(ln_fm, tb * 512, 512),
                        deps=[lns_free[lb]])
            for ct in range(16):
                wb = ct % 2
                wct_tok[wb] = P.load(
                    sb3(wct[wb][:], 128),
                    w1a[:, 2048 + ct * 128:2048 + (ct + 1) * 128].rearrange(
                        '(c p) w -> p c w', p=128),
                    deps=[wct_free[wb]])
                g = None
                for dc in range(DC):
                    g = P.mm(pp[wb][:],
                             wct[wb][:, dc * 128:(dc + 1) * 128],
                             lns[lb][:, dc * 512:(dc + 1) * 512],
                             dc == 0, dc == DC - 1,
                             deps=[wct_tok[wb], ld, pp_free[wb]],
                             inc=(dc == DC - 1))
                sl = P.act(qst[wb][:], pp[wb][:], AF.Silu,
                           bias=b1qk[:, ct:ct + 1], deps=[g, bqk_tok,
                                                          qst_free[wb]])
                st = P.store(qk_fm[ct * 128:(ct + 1) * 128,
                                   tb * 512:(tb + 1) * 512], qst[wb][:],
                             deps=[sl])
                pp_free[wb] = sl
                qst_free[wb] = st
                wct_free[wb] = g
            for cc in range(4):
                wb = cc % 2
                wuv_tok[wb] = P.load(
                    sb3(wuv[wb][:], 512),
                    w1a[:, cc * 512:(cc + 1) * 512].rearrange(
                        '(c p) w -> p c w', p=128),
                    deps=[wuv_free[wb]])
                for t4 in range(4):
                    pb = (cc * 4 + t4) % 2
                    g = None
                    for dc in range(DC):
                        g = P.mm(pp[pb][:],
                                 lns[lb][:, dc * 512 + t4 * 128:
                                         dc * 512 + (t4 + 1) * 128],
                                 wuv[wb][:, dc * 512:(dc + 1) * 512],
                                 dc == 0, False,
                                 deps=[wuv_tok[wb], ld, pp_free[pb]])
                    g = P.mm(pp[pb][:], ones_row[:],
                             b1a_sb[0:1, cc * 512:(cc + 1) * 512],
                             False, True, deps=[lb1], inc=True)
                    row0 = tb * 512 + t4 * 128
                    if cc < 2:
                        sl = P.act(ust[pb][:], pp[pb][:], AF.Silu,
                                   deps=[g, ust_free[pb]])
                        st = P.store(uv_tm[row0:row0 + 128,
                                           cc * 512:(cc + 1) * 512],
                                     ust[pb][:], deps=[sl])
                        ust_free[pb] = st
                        pp_free[pb] = sl
                    else:
                        sl = P.act(ust[pb][:], pp[pb][:], AF.Silu,
                                   deps=[g, ust_free[pb]])
                        sc_ = P.v('tensor_scalar', vst[pb][:], ust[pb][:],
                                  1.0 / T, None, OP.mult,
                                  deps=[sl, vst_free[pb]])
                        st = P.store(uv_tm[row0:row0 + 128,
                                           cc * 512:(cc + 1) * 512],
                                     vst[pb][:], deps=[sc_])
                        ust_free[pb] = sc_
                        vst_free[pb] = st
                        pp_free[pb] = sl
                wuv_free[wb] = g
            lns_free[lb] = g
        sc.close()

    def phase_attn_core():
        sc = contextlib.ExitStack()
        qh = [sbt(sc, f'at_q{i}', [64, T])
              for i in range(2)]
        kh = [sbt(sc, f'at_k{i}', [64, T])
              for i in range(2)]
        vh = [sbt(sc, f'at_v{i}', [128, NTT * 64])
              for i in range(2)]
        uh = [sbt(sc, f'at_u{i}', [128, NTT * 64])
              for i in range(2)]
        asb = [sbt(sc, f'at_a{i}', [128, T])
               for i in range(2)]
        avu = [sbt(sc, f'at_avu{i}', [128, NTT * 64])
               for i in range(2)]
        ps = [pst(sc, f'at_ps{i}', [128, 512])
              for i in range(2)]
        pav = [pst(sc, f'at_pav{i}', [128, 512])
               for i in range(4)]

        set_free = [None, None]
        ps_free = [None, None]
        pav_free = [None] * 4
        avu_free = [None, None]
        bn_last = None
        for h in range(NH):
            hb = h % 2
            lq = P.load(qh[hb][:], qk_fm[h * 64:(h + 1) * 64, :],
                        deps=[set_free[hb]])
            lk = P.load(kh[hb][:], qk_fm[D + h * 64:D + (h + 1) * 64, :],
                        deps=[set_free[hb]])
            lv = P.load(vh[hb][:].rearrange('p (t w) -> p t w', w=64),
                        uv_tm[:, D + h * 64:D + (h + 1) * 64].rearrange(
                            '(t p) w -> p t w', p=128),
                        deps=[set_free[hb]])
            lu = P.load(uh[hb][:].rearrange('p (t w) -> p t w', w=64),
                        uv_tm[:, h * 64:(h + 1) * 64].rearrange(
                            '(t p) w -> p t w', p=128),
                        deps=[set_free[hb]])
            for kt in range(NTT):
                ab = kt % 2
                for qc in range(8):
                    sm = P.mm(ps[qc % 2][:],
                              kh[hb][:, kt * 128:(kt + 1) * 128],
                              qh[hb][:, qc * 512:(qc + 1) * 512],
                              True, True,
                              deps=[lq, lk, ps_free[qc % 2]],
                              inc=True)
                    sl = P.act(asb[ab][:, qc * 512:(qc + 1) * 512],
                               ps[qc % 2][:], AF.Silu, deps=[sm])
                    ps_free[qc % 2] = sl
                    for t4 in range(4):
                        qt = qc * 4 + t4
                        bk = qt // 8
                        off = (qt % 8) * 64
                        first = (kt == 0 and qt % 8 == 0)
                        last = (kt == NTT - 1 and qt % 8 == 7)
                        P.mm(pav[bk][:, off:off + 64],
                             asb[ab][:, qt * 128:(qt + 1) * 128],
                             vh[hb][:, kt * 64:(kt + 1) * 64],
                             first, last,
                             deps=[sl, lv] + ([pav_free[bk]] if kt == 0 else []),
                             inc=last)
            av_done = snap_tensor_token()
            mu_t = None
            for qt in range(NTT):
                bk = qt // 8
                off = (qt % 8) * 64
                mu_t = P.v('tensor_tensor', avu[hb][:, qt * 64:(qt + 1) * 64],
                           pav[bk][:, off:off + 64],
                           uh[hb][:, qt * 64:(qt + 1) * 64], OP.mult,
                           deps=[av_done, lu, avu_free[hb]])
                bn_last = P.v('bn_stats',
                              bnbuf[:, (qt * NH + h) * 6:(qt * NH + h) * 6 + 6],
                              avu[hb][:, qt * 64:(qt + 1) * 64], deps=[mu_t])
            for bk in range(4):
                pav_free[bk] = mu_t
            stv = P.store(avu_tm[:, h * 64:(h + 1) * 64].rearrange(
                              '(t p) w -> p t w', p=128),
                          avu[hb][:].rearrange('p (t w) -> p t w', w=64),
                          deps=[mu_t])
            avu_free[hb] = stv
            set_free[hb] = mu_t
        vtmp = sbt(sc, 'at_vtmp', [128, NTT])
        ag_tok = None
        for qt in range(NTT):
            ag = P.v('bn_aggr', bnagg[:, qt * 2:qt * 2 + 2],
                     bnbuf[:, qt * NH * 6:(qt + 1) * NH * 6],
                     deps=[bn_last])
            ve = P.v('tensor_scalar', vtmp[:, qt:qt + 1],
                     bnagg[:, qt * 2 + 1:qt * 2 + 2], EPS, None, OP.add,
                     deps=[ag])
            sq = P.act(sdc[:, qt:qt + 1], vtmp[:, qt:qt + 1], AF.Sqrt,
                       deps=[ve])
            ag_tok = P.v('reciprocal', rstdc[:, qt:qt + 1], sdc[:, qt:qt + 1],
                         deps=[sq])
        sc.close()
        return ag_tok

    def phase_outproj(stats_tok, dump_z=False):
        sc = contextlib.ExitStack()
        w2g = sbt(sc, 'op_w2g', [128, DC * 1024])
        wch = [sbt(sc, f'op_wch{i}', [128, 1024])
               for i in range(2)]
        gcols = sbt(sc, 'op_gc', [128, 8])
        bcols = sbt(sc, 'op_bc', [128, 8])
        b2cols = sbt(sc, 'op_b2c', [128, 8])
        biasc = sbt(sc, 'op_bias', [128, 8])
        avt = [sbt(sc, f'op_avt{i}', [128, D])
               for i in range(2)]
        zt = [sbt(sc, f'op_zt{i}', [128, D])
              for i in range(2)]
        zfm = [sbt(sc, f'op_zfm{i}', [128, DC * 512])
               for i in range(2)]
        xch = [sbt(sc, f'op_xch{i}', [128, DC * 512])
               for i in range(2)]
        xnb = [sbt(sc, f'op_xnb{i}', [128, DC * 512])
               for i in range(2)]
        yst = [sbt(sc, f'op_yst{i}', [128, DC * 512])
               for i in range(2)]
        p_b = pst(sc, 'op_pb', [128, 8])
        p_t = [pst(sc, f'op_pt{i}', [128, 128])
               for i in range(2)]
        p_y = [pst(sc, f'op_py{i}', [128, 512])
               for i in range(2)]

        lg_ = P.load(gcols[:], gam[0].rearrange('(c p) -> p c', p=128), slow=True)
        lb_ = P.load(bcols[:], bet[0].rearrange('(c p) -> p c', p=128), slow=True)
        lb2 = P.load(b2cols[:], b2a[0].rearrange('(c p) -> p c', p=128), slow=True)
        wch_tok = [None, None]
        wch_free = [None, None]
        bias_mm = None
        w2g_tok = None
        for dc in range(DC):
            wb = dc % 2
            wch_tok[wb] = P.load(wch[wb][:], w2a[dc * 128:(dc + 1) * 128, :],
                                 deps=(wch_free[wb] or []))
            for dt in range(DC):
                bias_mm = P.mm(p_b[:, dt:dt + 1],
                               wch[wb][:, dt * 128:(dt + 1) * 128],
                               bcols[:, dc:dc + 1],
                               dc == 0 and dt == 0, dc == DC - 1 and dt == DC - 1,
                               deps=[wch_tok[wb], lb_],
                               inc=(dt == DC - 1))
            w2g_tok = P.v('tensor_scalar', w2g[:, dc * 1024:(dc + 1) * 1024],
                          wch[wb][:], gcols[:, dc:dc + 1], None, OP.mult,
                          deps=[wch_tok[wb], lg_, w2g_tok])
            wch_free[wb] = [w2g_tok, bias_mm]
        bias_tok = P.v('tensor_tensor', biasc[:], p_b[:], b2cols[:], OP.add,
                       deps=[bias_mm, lb2])

        avt_free = [None, None]
        zt_free = [None, None]
        xnb_free = [None, None]
        zfm_free = [None, None]
        ptr_free = [None, None]
        py_free = [None, None]
        xch_free = [None, None]
        yst_free = [None, None]
        i = 0
        for tb in range(TBN):
            zb = tb % 2
            lx = P.load(sb3(xch[zb][:], 512), fmc(x_fm, tb * 512, 512),
                        deps=[xch_free[zb]])
            last_tr = None
            for t4 in range(4):
                tt = tb * 4 + t4
                ab = tt % 2
                la = P.load(avt[ab][:], avu_tm[tt * 128:(tt + 1) * 128, :],
                            deps=[avt_free[ab]])
                zz = P.v('tensor_scalar', zt[ab][:], avt[ab][:],
                         bnagg[:, tt * 2:tt * 2 + 1], rstdc[:, tt:tt + 1],
                         OP.subtract, OP.mult,
                         deps=[la, stats_tok, zt_free[ab]])
                for dc in range(DC):
                    pb = i % 2
                    tr = P.transp(p_t[pb][:], zt[ab][:, dc * 128:(dc + 1) * 128],
                                  ident[:], deps=[zz, ptr_free[pb]])
                    cp = P.v('tensor_copy',
                             zfm[zb][:, dc * 512 + t4 * 128:
                                     dc * 512 + (t4 + 1) * 128],
                             p_t[pb][:], deps=[tr, zfm_free[zb]])
                    ptr_free[pb] = cp
                    last_tr = cp
                    i += 1
                avt_free[ab] = zz
                zt_free[ab] = last_tr
            last_q = None
            for dt in range(DC):
                pb = dt % 2
                g = None
                for dc in range(DC):
                    g = P.mm(p_y[pb][:],
                             w2g[:, dc * 1024 + dt * 128:
                                 dc * 1024 + (dt + 1) * 128],
                             zfm[zb][:, dc * 512:(dc + 1) * 512],
                             dc == 0, dc == DC - 1,
                             deps=[w2g_tok, last_tr, py_free[pb]],
                             inc=(dc == DC - 1))
                sl = slice(dt * 512, (dt + 1) * 512)
                q1 = P.v('tensor_scalar', yst[zb][:, sl], p_y[pb][:],
                         biasc[:, dt:dt + 1], None, OP.add,
                         deps=[g, bias_tok, yst_free[zb]])
                q2 = P.v('tensor_tensor', xnb[zb][:, sl], yst[zb][:, sl],
                         xch[zb][:, sl], OP.add, deps=[q1, lx, xnb_free[zb]])
                py_free[pb] = q1
                last_q = q2
            if dump_z:
                P.store(fmc(y_acc, tb * 512, 512), sb3(zfm[zb][:], 512),
                        deps=[last_tr])
            st = P.store(fmc(x_fm, tb * 512, 512), sb3(xnb[zb][:], 512),
                         deps=[last_q])
            zfm_free[zb] = last_q
            xch_free[zb] = last_q
            xnb_free[zb] = st
            yst_free[zb] = last_q
        sc.close()

    def phase_out_transpose(src):
        sc = contextlib.ExitStack()
        chunk = [sbt(sc, f'pf_ch{i}', [128, T])
                 for i in range(DC)]
        ostg = [sbt(sc, f'pf_os{i}', [128, D])
                for i in range(2)]
        ptr = [pst(sc, f'pf_ps{i}', [128, 128])
               for i in range(2)]
        lds = [P.load(chunk[dc][:], src[dc * 128:(dc + 1) * 128, :])
               for dc in range(DC)]
        ptr_free = [None, None]
        ostg_free = [None, None]
        i = 0
        for tt in range(NTT):
            ob = tt % 2
            last_cp = None
            for dc in range(DC):
                pb = i % 2
                tr = P.transp(ptr[pb][:],
                              chunk[dc][:, tt * 128:(tt + 1) * 128],
                              ident[:], deps=[lds[dc], ptr_free[pb]])
                cp = P.v('tensor_copy', ostg[ob][:, dc * 128:(dc + 1) * 128],
                         ptr[pb][:], deps=[tr, ostg_free[ob]])
                ptr_free[pb] = cp
                last_cp = cp
                i += 1
            st = P.store(out[tt * 128:(tt + 1) * 128, :], ostg[ob][:],
                         deps=[last_cp])
            ostg_free[ob] = st
        sc.close()

    def phase_copy_tm(srcd, col0):
        sc = contextlib.ExitStack()
        stg = [sbt(sc, f'cp_st{i}', [128, D]) for i in range(2)]
        free = [None, None]
        for tt in range(NTT):
            b = tt % 2
            ld = P.load(stg[b][:], srcd[tt * 128:(tt + 1) * 128, col0:col0 + D],
                        deps=[free[b]])
            st = P.store(out[tt * 128:(tt + 1) * 128, :], stg[b][:], deps=[ld])
            free[b] = st
        sc.close()

    phase_in_transpose()
    P.set_base(P.snap())
    if mode == 'transpose':
        phase_out_transpose(x_fm)
    else:
        pc, pt = phase_ln(x_fm, ln_fm, True)
        P.set_base(P.snap())
        if mode == 'ln':
            phase_out_transpose(ln_fm)
        else:
            rt = phase_routing(pc, pt)
            P.set_base(P.snap())
            phase_ffn(rt, True)
            P.set_base(P.snap())
            if mode == 'ffn1':
                phase_out_transpose(x_fm)
            else:
                phase_ln(x_fm, ln_fm, False)
                P.set_base(P.snap())
                phase_xproj()
                P.set_base(P.snap())
                if mode in ('qpart', 'kpart', 'upart', 'vpart'):
                    if mode == 'qpart':
                        phase_out_transpose(qk_fm[0:D, :])
                    elif mode == 'kpart':
                        phase_out_transpose(qk_fm[D:2 * D, :])
                    elif mode == 'upart':
                        phase_copy_tm(uv_tm, 0)
                    else:
                        phase_copy_tm(uv_tm, D)
                    P.set_base([])
                    P._op('sync', None, False, P.snap(), lambda h: h.nop())
                    with nc.Block() as block:
                        P.emit(block, sems)
                    es.close()
                    return nc
                stats_tok = phase_attn_core()
                P.set_base(P.snap())
                if mode == 'avu':
                    phase_copy_tm(avu_tm, 0)
                    P.set_base([])
                    P._op('sync', None, False, P.snap(), lambda h: h.nop())
                    with nc.Block() as block:
                        P.emit(block, sems)
                    es.close()
                    return nc
                phase_outproj(stats_tok, dump_z=(mode == 'zdump'))
                P.set_base(P.snap())
                if mode == 'zdump':
                    phase_out_transpose(y_acc)
                elif mode == 'attn':
                    phase_out_transpose(x_fm)
                else:
                    pc, pt = phase_ln(x_fm, ln_fm, True)
                    P.set_base(P.snap())
                    rt = phase_routing(pc, pt)
                    P.set_base(P.snap())
                    phase_ffn(rt, True)
                    P.set_base(P.snap())
                    phase_out_transpose(x_fm)

    P.set_base([])
    P._op('sync', None, False, P.snap(), lambda h: h.nop())

    with nc.Block() as block:
        P.emit(block, sems)
    es.close()
    return nc


_NC_CACHE = {}


def _get_program(mode):
    if mode not in _NC_CACHE:
        _NC_CACHE[mode] = build_program(mode)
    return _NC_CACHE[mode]


def kernel(L0_attn_x, L0_attn_W1, L0_attn_b1, L0_attn_W2, L0_attn_b2,
           L0_attn_gamma, L0_attn_beta, L0_ffn_Wr, L0_ffn_W1, L0_ffn_b1,
           L0_ffn_W2, L0_ffn_b2, mode='full'):
    f = lambda a: np.ascontiguousarray(np.asarray(a, dtype=np.float32))
    x = f(L0_attn_x)
    shared = dict(
        w1a=f(L0_attn_W1), b1a=f(L0_attn_b1).reshape(1, FF),
        w2a=f(L0_attn_W2), b2a=f(L0_attn_b2).reshape(1, D),
        gam=f(L0_attn_gamma).reshape(1, D), bet=f(L0_attn_beta).reshape(1, D),
        wr=f(L0_ffn_Wr), w1m=f(L0_ffn_W1), b1m=f(L0_ffn_b1),
        w2m=f(L0_ffn_W2), b2m=f(L0_ffn_b2))
    in_maps = [dict(x=np.ascontiguousarray(x[b]), **shared)
               for b in range(N_CORES)]
    nc = _get_program(mode)
    res = run_bass_kernel_spmd(nc, in_maps, core_ids=list(range(N_CORES)))
    return np.stack([res.results[b]['out'] for b in range(N_CORES)], axis=0)


# version 02
# speedup vs baseline: 1.2010x; 1.2010x over previous
import contextlib
import numpy as np

import concourse.bass as bass
import concourse.mybir as mybir
from concourse.bass_utils import run_bass_kernel_spmd

F32 = mybir.dt.float32
U32 = mybir.dt.uint32
AF = mybir.ActivationFunctionType
OP = mybir.AluOpType
AX = mybir.AxisListType

D, T, FF, NH, DH, E = 1024, 4096, 4096, 16, 64, 8
DC = D // 128
TBN = T // 512
NTT = T // 128
EPS = 1e-5
N_CORES = 8

ENGINES = ('sync', 'gpsimd', 'tensor', 'scalar', 'vector')


class SharedTok:

    def __init__(self):
        self.t = None

    def __getitem__(self, i):
        return self.t

    def __setitem__(self, i, v):
        self.t = v


class Prog:

    NPOOL = 6

    def __init__(self):
        self.streams = {e: [] for e in ENGINES}
        self.cnt = dict(st=0, ss=0, sv=0, sg=0)
        for q in ('dl', 'ds', 'dg'):
            for i in range(self.NPOOL):
                self.cnt[f'{q}{i}'] = 0
        self.rr = dict(dl=0, ds=0, dg=0)
        self.base = {}

    def _dma(self, eng, q, deps, fn):
        i = self.rr[q]
        self.rr[q] = (i + 1) % self.NPOOL
        sem = f'{q}{i}'
        prev = self.cnt[sem]
        dd = list(deps)
        if prev:
            dd.append((sem, prev))
        return self._op(eng, sem, True, dd, fn)

    def set_base(self, toks):
        self.base = {}
        for t in toks:
            if t is None:
                continue
            s, c = t
            if c > self.base.get(s, 0):
                self.base[s] = c

    def _op(self, eng, sem, dma, deps, fn):
        tok = None
        if sem is not None:
            self.cnt[sem] += 16 if dma else 1
            tok = (sem, self.cnt[sem])
        dd = dict(self.base)
        for d in deps:
            if d is None:
                continue
            s, c = d
            if c > dd.get(s, 0):
                dd[s] = c
        self.streams[eng].append((fn, dd, sem, 16 if dma else 1))
        return tok

    def load(self, dst, src, deps=(), slow=False):
        if slow:
            return self._dma('sync', 'dl', deps,
                             lambda h, a=dst, b=src: h.dma_start(
                                 out=a, in_=b, allow_slow_non_contiguous=True))
        return self._dma('sync', 'dl', deps,
                         lambda h, a=dst, b=src: h.dma_start(out=a, in_=b))

    def store(self, dst, src, deps=()):
        return self._dma('scalar', 'ds', deps,
                         lambda h, a=dst, b=src: h.dma_start(out=a, in_=b))

    def gstore(self, dst, src, deps=(), accum=False):
        if accum:
            return self._dma('gpsimd', 'dg', deps,
                             lambda h, a=dst, b=src:
                                 h.dma_start(out=a, in_=b, accum_op=OP.add))
        return self._dma('gpsimd', 'dg', deps,
                         lambda h, a=dst, b=src: h.dma_start(out=a, in_=b))

    def gather(self, dst, src_rows, idx, deps=()):
        return self._dma('gpsimd', 'dg', deps,
                         lambda h, a=dst, b=src_rows, i=idx: h.indirect_dma_start(
                             out=a, out_offset=None, in_=b,
                             in_offset=bass.IndirectOffsetOnAxis(ap=i, axis=0)))

    def mm(self, out, stat, mov, start, stop, deps=(), inc=False):
        return self._op('tensor', 'st' if inc else None, False, deps,
                        lambda h, o=out, s=stat, m=mov, a=start, b=stop:
                            h.matmul(o, s, m, start=a, stop=b))

    def transp(self, out, in_, ident, deps=(), inc=True):
        return self._op('tensor', 'st' if inc else None, False, deps,
                        lambda h, o=out, i=in_, d=ident: h.transpose(o, i, d))

    def act(self, out, in_, func, deps=(), bias=None, scale=1.0):
        if bias is None:
            return self._op('scalar', 'ss', False, deps,
                            lambda h, o=out, i=in_, f=func, sc=scale:
                                h.activation(o, i, f, scale=sc))
        return self._op('scalar', 'ss', False, deps,
                        lambda h, o=out, i=in_, f=func, b=bias, sc=scale:
                            h.activation(o, i, f, bias=b, scale=sc))

    def v(self, name, *args, deps=(), **kw):
        return self._op('vector', 'sv', False, deps,
                        lambda h, n=name, a=args, k=kw: getattr(h, n)(*a, **k))

    def g(self, name, *args, deps=(), **kw):
        return self._op('gpsimd', 'sg', False, deps,
                        lambda h, n=name, a=args, k=kw: getattr(h, n)(*a, **k))

    def snap(self):
        return [(s, c) for s, c in self.cnt.items() if c > 0]

    def emit(self, block, sems):
        for eng in ENGINES:
            items = self.streams[eng]
            if not items:
                continue

            def mk(items=items):
                def body(h):
                    hw = {}
                    for fn, dd, sem, amt in items:
                        for s, c in sorted(dd.items()):
                            if c > hw.get(s, 0):
                                h.wait_ge(sems[s], c)
                                hw[s] = c
                        inst = fn(h)
                        if sem is not None:
                            inst.then_inc(sems[sem], amt)
                return body
            getattr(block, eng)(mk())


def build_program(mode='full'):
    nc = bass.Bass('TRN2', target_bir_lowering=False)
    x_in = nc.dram_tensor('x', [T, D], F32, kind='ExternalInput').ap()
    w1a = nc.dram_tensor('w1a', [D, FF], F32, kind='ExternalInput').ap()
    b1a = nc.dram_tensor('b1a', [1, FF], F32, kind='ExternalInput').ap()
    w2a = nc.dram_tensor('w2a', [D, D], F32, kind='ExternalInput').ap()
    b2a = nc.dram_tensor('b2a', [1, D], F32, kind='ExternalInput').ap()
    gam = nc.dram_tensor('gam', [1, D], F32, kind='ExternalInput').ap()
    bet = nc.dram_tensor('bet', [1, D], F32, kind='ExternalInput').ap()
    wr = nc.dram_tensor('wr', [D, E], F32, kind='ExternalInput').ap()
    w1m = nc.dram_tensor('w1m', [E, D, FF], F32, kind='ExternalInput').ap()
    b1m = nc.dram_tensor('b1m', [E, FF], F32, kind='ExternalInput').ap()
    w2m = nc.dram_tensor('w2m', [E, FF, D], F32, kind='ExternalInput').ap()
    b2m = nc.dram_tensor('b2m', [E, D], F32, kind='ExternalInput').ap()
    out = nc.dram_tensor('out', [T, D], F32, kind='ExternalOutput').ap()

    x_fm = nc.dram_tensor('x_fm', [D, T], F32).ap()
    ln_fm = nc.dram_tensor('ln_fm', [D, T], F32).ap()
    y_acc = nc.dram_tensor('y_acc', [D, T], F32).ap()
    proj_fm = nc.dram_tensor('proj_fm', [3 * D, T], F32).ap()
    v_tm = nc.dram_tensor('v_tm', [T, D], F32).ap()
    avu_fm = nc.dram_tensor('avu_fm', [D, T], F32).ap()

    w1rows = w1m.rearrange('e d (q c) -> (e d q) c', c=1024)
    w2rows = w2m.rearrange('e f d -> (e f) d')

    def fmc(dram, t0, w):
        return dram[:, t0:t0 + w].rearrange('(c p) w -> p c w', p=128)

    def sb3(ap, w):
        return ap.rearrange('p (c w) -> p c w', w=w)

    P = Prog()
    uid = [0]

    def sbt(sc_, name, shape, dtype=F32):
        uid[0] += 1
        return sc_.enter_context(
            nc.sbuf_tensor(f'{name}_{uid[0]}', shape, dtype)).ap()

    def pst(sc_, name, shape):
        uid[0] += 1
        return sc_.enter_context(
            nc.psum_tensor(f'{name}_{uid[0]}', shape, F32)).ap()

    es = contextlib.ExitStack()
    sems = {s: es.enter_context(nc.semaphore(s)) for s in P.cnt}

    ident = nc.alloc_sbuf_tensor('ident', [128, 128], F32).ap()
    iota128 = nc.alloc_sbuf_tensor('iota128', [128, 1], F32).ap()
    iota4 = nc.alloc_sbuf_tensor('iota4', [128, 1], F32).ap()
    iota8 = nc.alloc_sbuf_tensor('iota8', [1, 8], F32).ap()
    ones_row = nc.alloc_sbuf_tensor('ones_row', [1, 128], F32).ap()
    ones_col = nc.alloc_sbuf_tensor('ones_col', [128, 1], F32).ap()
    pooled = nc.alloc_sbuf_tensor('pooled', [128, 16], F32).ap()
    gcol = nc.alloc_sbuf_tensor('gcol', [128, 4], F32).ap()
    b1cols = nc.alloc_sbuf_tensor('b1cols', [128, 64], F32).ap()
    b2comb = nc.alloc_sbuf_tensor('b2comb', [128, 8], F32).ap()
    idxw1 = nc.alloc_sbuf_tensor('idxw1', [128, 64], U32).ap()
    idxw2 = nc.alloc_sbuf_tensor('idxw2', [128, 64], U32).ap()

    c0 = P.g('memset', ident[:], 0.0)
    c1 = P.g('affine_select', out=ident[:], in_=ident[:],
             compare_op=OP.not_equal, fill=1.0, base=0,
             pattern=[[-1, 128]], channel_multiplier=1, deps=[c0])
    c2 = P.g('iota', iota128[:], [[0, 1]], channel_multiplier=1,
             allow_small_or_imprecise_dtypes=True, deps=[c1])
    c3 = P.g('iota', iota8[:], [[1, 8]], channel_multiplier=0,
             allow_small_or_imprecise_dtypes=True, deps=[c2])
    c4 = P.g('memset', ones_row[:], 1.0, deps=[c3])
    c5 = P.g('memset', ones_col[:], 1.0, deps=[c4])
    c6 = P.v('tensor_scalar', iota4[:], iota128[:], 4.0, None, OP.mult,
             deps=[c5])
    P.set_base([c6])

    def phase_in_transpose():
        sc = contextlib.ExitStack()
        stage = [sbt(sc, f'p0stg{i}', [128, T])
                 for i in range(DC)]
        xin = [sbt(sc, f'p0xin{i}', [128, D])
               for i in range(2)]
        ptr = [pst(sc, f'p0ps{i}', [128, 128])
               for i in range(2)]
        xin_free = [None, None]
        ptr_free = [None, None]
        last_cp = None
        i = 0
        for tt in range(NTT):
            b = tt % 2
            ld = P.load(xin[b][:], x_in[tt * 128:(tt + 1) * 128, :],
                        deps=[xin_free[b]])
            for dc in range(DC):
                pb = i % 2
                tr = P.transp(ptr[pb][:], xin[b][:, dc * 128:(dc + 1) * 128],
                              ident[:], deps=[ld, ptr_free[pb]])
                cp = P.v('tensor_copy', stage[dc][:, tt * 128:(tt + 1) * 128],
                         ptr[pb][:], deps=[tr])
                ptr_free[pb] = cp
                last_cp = cp
                i += 1
            xin_free[b] = last_cp
        for dc in range(DC):
            P.store(x_fm[dc * 128:(dc + 1) * 128, :], stage[dc][:],
                    deps=[last_cp])
        sc.close()

    def phase_ln(src, dst, want_pooled):
        sc = contextlib.ExitStack()
        xs = [sbt(sc, f'lnxs{i}', [128, DC * 512])
              for i in range(2)]
        xq = [sbt(sc, f'lnxq{i}', [128, DC * 512])
              for i in range(2)]
        tp = [sbt(sc, f'lntp{i}', [128, DC * 512])
              for i in range(2)]
        lo = [sbt(sc, f'lnlo{i}', [128, DC * 512])
              for i in range(2)]
        mu_s = sbt(sc, 'ln_mu', [1, 512])
        ms_s = sbt(sc, 'ln_ms', [1, 512])
        m2_s = sbt(sc, 'ln_m2', [1, 512])
        va_s = sbt(sc, 'ln_va', [1, 512])
        sd_s = sbt(sc, 'ln_sd', [1, 512])
        rs_s = sbt(sc, 'ln_rs', [1, 512])
        plr = sbt(sc, 'ln_plr', [128, 1])
        p_sum = pst(sc, 'ln_psum', [1, 512])
        p_sq = pst(sc, 'ln_psq', [1, 512])
        p_mu = pst(sc, 'ln_pmu', [128, 512])
        p_rs = pst(sc, 'ln_prs', [128, 512])

        xs_free = [None, None]
        xq_free = [None, None]
        tp_free = [None, None]
        lo_free = [None, None]
        war_psum = war_psq = war_pmu = war_prs = None
        pool_par = [0] * DC
        pool_tok = [None] * DC
        plr_free = None
        if want_pooled:
            z0 = P.v('memset', pooled[:], 0.0)
            pool_tok = [z0] * DC
        for tb in range(TBN):
            b = tb % 2
            ld = P.load(sb3(xs[b][:], 512), fmc(src, tb * 512, 512),
                        deps=[xs_free[b]])
            sq = None
            for dc in range(DC):
                sq = P.act(xq[b][:, dc * 512:(dc + 1) * 512],
                           xs[b][:, dc * 512:(dc + 1) * 512], AF.Square,
                           deps=[ld, xq_free[b]])
            tsum = tsq = None
            for dc in range(DC):
                tsum = P.mm(p_sum[:], ones_col[:],
                            xs[b][:, dc * 512:(dc + 1) * 512],
                            dc == 0, dc == DC - 1,
                            deps=[ld, war_psum], inc=(dc == DC - 1))
            for dc in range(DC):
                tsq = P.mm(p_sq[:], ones_col[:],
                           xq[b][:, dc * 512:(dc + 1) * 512],
                           dc == 0, dc == DC - 1,
                           deps=[sq, war_psq], inc=(dc == DC - 1))
            u1 = P.v('tensor_scalar', mu_s[:], p_sum[:], 1.0 / D, None, OP.mult,
                     deps=[tsum])
            u2 = P.v('tensor_scalar', ms_s[:], p_sq[:], 1.0 / D, EPS,
                     OP.mult, OP.add, deps=[tsq])
            u3 = P.v('tensor_tensor', m2_s[:], mu_s[:], mu_s[:], OP.mult,
                     deps=[u1])
            u4 = P.v('tensor_tensor', va_s[:], ms_s[:], m2_s[:], OP.subtract,
                     deps=[u2, u3])
            u5 = P.act(sd_s[:], va_s[:], AF.Sqrt, deps=[u4])
            u6 = P.v('reciprocal', rs_s[:], sd_s[:], deps=[u5])
            war_psum = u1
            war_psq = u2
            bmu = P.mm(p_mu[:], ones_row[:], mu_s[:], True, True,
                       deps=[u3, war_pmu], inc=True)
            brs = P.mm(p_rs[:], ones_row[:], rs_s[:], True, True,
                       deps=[u6, war_prs], inc=True)
            last_n = None
            for dc in range(DC):
                sl = slice(dc * 512, (dc + 1) * 512)
                n1 = P.v('tensor_tensor', tp[b][:, sl], xs[b][:, sl], p_mu[:],
                         OP.subtract, deps=[bmu, ld, tp_free[b]])
                n2 = P.v('tensor_tensor', lo[b][:, sl], tp[b][:, sl], p_rs[:],
                         OP.mult, deps=[n1, brs, lo_free[b]])
                last_n = n2
                if want_pooled:
                    r1 = P.v('tensor_reduce', plr[:], lo[b][:, sl], AX.X,
                             OP.add, deps=[n2, plr_free])
                    par = pool_par[dc]
                    r2 = P.v('tensor_tensor',
                             pooled[:, dc * 2 + 1 - par:dc * 2 + 2 - par],
                             pooled[:, dc * 2 + par:dc * 2 + par + 1],
                             plr[:], OP.add, deps=[r1, pool_tok[dc]])
                    pool_par[dc] = 1 - par
                    pool_tok[dc] = r2
                    plr_free = r2
            war_pmu = last_n
            war_prs = last_n
            st = P.store(fmc(dst, tb * 512, 512), sb3(lo[b][:], 512),
                         deps=[last_n])
            xs_free[b] = last_n
            xq_free[b] = tsq
            tp_free[b] = last_n
            lo_free[b] = st
        sc.close()
        pool_cols = [dc * 2 + pool_par[dc] for dc in range(DC)]
        return pool_cols, pool_tok

    def phase_routing(pool_cols, pool_tok):
        sc = contextlib.ExitStack()
        wr_sb = sbt(sc, 'rt_wr', [128, 64])
        lg = sbt(sc, 'rt_lg', [1, 8])
        lg2 = sbt(sc, 'rt_lg2', [1, 8])
        mk1 = sbt(sc, 'rt_mk1', [1, 8])
        mk2 = sbt(sc, 'rt_mk2', [1, 8])
        m1 = sbt(sc, 'rt_m1', [1, 1])
        m2 = sbt(sc, 'rt_m2', [1, 1])
        ddf = sbt(sc, 'rt_dd', [1, 1])
        dsc_ = sbt(sc, 'rt_dsc', [1, 1])
        e2v = sbt(sc, 'rt_e2', [1, 1])
        den = sbt(sc, 'rt_den', [1, 1])
        rec = sbt(sc, 'rt_rec', [1, 1])
        w2v = sbt(sc, 'rt_w2v', [1, 1])
        tm8a = sbt(sc, 'rt_tm8a', [1, 8])
        tm8b = sbt(sc, 'rt_tm8b', [1, 8])
        i1 = sbt(sc, 'rt_i1', [1, 1])
        i2 = sbt(sc, 'rt_i2', [1, 1])
        pack = sbt(sc, 'rt_pack', [1, 4])
        stripe = sbt(sc, 'rt_eb', [1, 2])
        ebf = sbt(sc, 'rt_ebf', [2, 1])
        ebu = sbt(sc, 'rt_ebu', [2, 1], U32)
        b1s = sbt(sc, 'rt_b1s', [2, FF])
        b2s = sbt(sc, 'rt_b2s', [2, D])
        b2c = sbt(sc, 'rt_b2c', [128, 16])
        tcol = sbt(sc, 'rt_tcol', [128, 1])
        fst = [sbt(sc, f'rt_fst{i}', [128, 1])
               for i in range(2)]
        base1 = [sbt(sc, f'rt_ba{k}', [128, 1])
                 for k in range(2)]
        base2 = [sbt(sc, f'rt_bb{k}', [128, 1])
                 for k in range(2)]
        p_lg = pst(sc, 'rt_plg', [1, 8])
        p_g = pst(sc, 'rt_pg', [128, 4])
        p_eb = pst(sc, 'rt_peb', [2, 1])
        p_t = [pst(sc, f'rt_pt{i}', [128, 2])
               for i in range(2)]

        wl = P.load(sb3(wr_sb[:], 8), wr.rearrange('(c p) e -> p c e', p=128))
        tl = None
        for dc in range(DC):
            col = pool_cols[dc]
            tl = P.mm(p_lg[:], pooled[:, col:col + 1],
                      wr_sb[:, dc * 8:(dc + 1) * 8], dc == 0, dc == DC - 1,
                      deps=[wl, pool_tok[dc]], inc=(dc == DC - 1))
        a0 = P.v('tensor_copy', lg[:], p_lg[:], deps=[tl])
        a1 = P.v('reduce_max', m1[:], lg[:], AX.X, deps=[a0])
        a2 = P.v('tensor_scalar', mk1[:], lg[:], m1[:], None, OP.is_ge,
                 deps=[a1])
        a3 = P.v('scalar_tensor_tensor', out=lg2[:], in0=mk1[:], scalar=-1e30,
                 in1=lg[:], op0=OP.mult, op1=OP.add, deps=[a2])
        a4 = P.v('reduce_max', m2[:], lg2[:], AX.X, deps=[a3])
        a5 = P.v('tensor_scalar', mk2[:], lg2[:], m2[:], None, OP.is_ge,
                 deps=[a4])
        a6 = P.v('tensor_tensor', ddf[:], m2[:], m1[:], OP.subtract,
                 deps=[a4, a1])
        a7 = P.v('tensor_scalar', dsc_[:], ddf[:], 1.0 / T, None, OP.mult,
                 deps=[a6])
        a8 = P.act(e2v[:], dsc_[:], AF.Exp, deps=[a7])
        a9 = P.v('tensor_scalar', den[:], e2v[:], 1.0, None, OP.add, deps=[a8])
        a10 = P.v('reciprocal', rec[:], den[:], deps=[a9])
        a11 = P.v('tensor_tensor', w2v[:], e2v[:], rec[:], OP.mult,
                  deps=[a8, a10])
        a12 = P.v('tensor_scalar', pack[:, 0:1], rec[:], 0.5, None, OP.mult,
                  deps=[a10])
        a13 = P.v('tensor_scalar', pack[:, 1:2], w2v[:], 0.5, None, OP.mult,
                  deps=[a11])
        a14 = P.v('tensor_tensor', tm8a[:], mk1[:], iota8[:], OP.mult,
                  deps=[a2])
        a15 = P.v('tensor_reduce', i1[:], tm8a[:], AX.X, OP.add, deps=[a14])
        a16 = P.v('tensor_tensor', tm8b[:], mk2[:], iota8[:], OP.mult,
                  deps=[a5])
        a17 = P.v('tensor_reduce', i2[:], tm8b[:], AX.X, OP.add, deps=[a16])
        a18 = P.v('tensor_copy', pack[:, 2:3], i1[:], deps=[a15])
        a19 = P.v('tensor_copy', pack[:, 3:4], i2[:], deps=[a17])
        bg = P.mm(p_g[:], ones_row[:], pack[:], True, True,
                  deps=[a12, a13, a18, a19], inc=True)
        g0 = P.v('tensor_copy', gcol[:], p_g[:], deps=[bg])
        s0 = P.v('tensor_copy', stripe[:, 0:1], i1[:], deps=[a15])
        s1 = P.v('tensor_copy', stripe[:, 1:2], i2[:], deps=[a17])
        te = P.transp(p_eb[:], stripe[:], ident[0:1, 0:1], deps=[s0, s1])
        s2 = P.v('tensor_copy', ebf[:], p_eb[:], deps=[te])
        s3 = P.v('tensor_copy', ebu[:], ebf[:], deps=[s2])
        gb1 = P.gather(b1s[:], b1m, ebu[:], deps=[s3])
        gb2 = P.gather(b2s[:], b2m, ebu[:], deps=[s3])
        ptf = [te, te]
        b1c_tok = None
        for ft in range(32):
            pb = ft % 2
            tr = P.transp(p_t[pb][:], b1s[0:2, ft * 128:(ft + 1) * 128],
                          ident[0:2, 0:2], deps=[gb1, ptf[pb]])
            cp = P.v('tensor_copy', b1cols[:, ft * 2:ft * 2 + 2], p_t[pb][:],
                     deps=[tr])
            ptf[pb] = cp
            b1c_tok = cp
        b2c_tok = None
        for dc in range(DC):
            pb = dc % 2
            tr = P.transp(p_t[pb][:], b2s[0:2, dc * 128:(dc + 1) * 128],
                          ident[0:2, 0:2], deps=[gb2, ptf[pb]])
            cp = P.v('tensor_copy', b2c[:, dc * 2:dc * 2 + 2], p_t[pb][:],
                     deps=[tr])
            ptf[pb] = cp
            b2c_tok = cp
        b2cb_tok = None
        tcol_tok = None
        for dc in range(DC):
            q1 = P.v('tensor_scalar', tcol[:], b2c[:, dc * 2:dc * 2 + 1],
                     gcol[:, 0:1], None, OP.mult, deps=[b2c_tok, g0, tcol_tok])
            q2 = P.v('scalar_tensor_tensor', out=b2comb[:, dc:dc + 1],
                     in0=b2c[:, dc * 2 + 1:dc * 2 + 2], scalar=gcol[:, 1:2],
                     in1=tcol[:], op0=OP.mult, op1=OP.add, deps=[q1])
            tcol_tok = q2
            b2cb_tok = q2
        idx_tok = None
        fs_tok = [g0, g0]
        for k in range(2):
            bb1 = P.v('scalar_tensor_tensor', out=base1[k],
                      in0=gcol[:, 2 + k:3 + k], scalar=4096.0, in1=iota4[:],
                      op0=OP.mult, op1=OP.add, deps=[g0])
            bb2 = P.v('scalar_tensor_tensor', out=base2[k],
                      in0=gcol[:, 2 + k:3 + k], scalar=4096.0, in1=iota128[:],
                      op0=OP.mult, op1=OP.add, deps=[g0])
            for fq in range(4):
                for dc in range(DC):
                    col = k * 32 + fq * 8 + dc
                    fb_ = col % 2
                    f1 = P.v('tensor_scalar', fst[fb_][:], base1[k],
                             float(512 * dc + fq), None, OP.add,
                             deps=[bb1, fs_tok[fb_]])
                    f2 = P.v('tensor_copy', idxw1[:, col:col + 1], fst[fb_][:],
                             deps=[f1])
                    fs_tok[fb_] = f2
                for fb in range(8):
                    col = k * 32 + fq * 8 + fb
                    fb_ = col % 2
                    f1 = P.v('tensor_scalar', fst[fb_][:], base2[k],
                             float(fq * 1024 + fb * 128), None, OP.add,
                             deps=[bb2, fs_tok[fb_]])
                    f2 = P.v('tensor_copy', idxw2[:, col:col + 1], fst[fb_][:],
                             deps=[f1])
                    fs_tok[fb_] = f2
                    idx_tok = f2
        sc.close()
        return dict(gcol=g0, b1c=b1c_tok, b2cb=b2cb_tok, idx=idx_tok)

    def phase_ffn(rt, first_call):
        sc = contextlib.ExitStack()
        w1q = [sbt(sc, f'ff_w1q{i}', [128, DC * 1024])
               for i in range(2)]
        w2q = sbt(sc, 'ff_w2q', [128, 8 * 1024])
        lns = [sbt(sc, f'ff_lns{i}', [128, DC * 512])
               for i in range(2)]
        hbf = sbt(sc, 'ff_h', [128, 8 * 512])
        ysb = [sbt(sc, f'ff_ysb{i}', [128, 4 * 512])
               for i in range(2)]
        ph = [pst(sc, f'ff_ph{i}', [128, 512])
              for i in range(2)]
        py = [pst(sc, f'ff_py{i}', [128, 512])
              for i in range(4)]

        w1_tok = [None, None]
        w1_free = [None, None]
        w2_tok = None
        w2_free = None
        lns_free = [None, None]
        h_tok = [None] * 8
        h_free = None
        ph_free = [None, None]
        py_free = [None] * 4
        ysb_free = [None, None]
        y_chain = [None] * TBN

        iters = [(k, fq) for k in range(2) for fq in range(4)]

        def issue_w1(it_i):
            k, fq = iters[it_i]
            slot = it_i % 2
            toks = []
            for dc in range(DC):
                col = k * 32 + fq * 8 + dc
                toks.append(P.gather(
                    w1q[slot][:, dc * 1024:(dc + 1) * 1024], w1rows,
                    idxw1[:, col:col + 1],
                    deps=[rt['idx'], w1_free[slot]]))
            w1_tok[slot] = toks[-1]

        issue_w1(0)
        for it_i, (k, fq) in enumerate(iters):
            slot = it_i % 2
            toks = []
            for fb in range(8):
                col = k * 32 + fq * 8 + fb
                toks.append(P.gather(
                    w2q[:, fb * 1024:(fb + 1) * 1024], w2rows,
                    idxw2[:, col:col + 1], deps=[rt['idx'], w2_free]))
            w2_tok = toks[-1]
            if it_i + 1 < len(iters):
                issue_w1(it_i + 1)
            for tb in range(TBN):
                lb = tb % 2
                ld = P.load(sb3(lns[lb][:], 512), fmc(ln_fm, tb * 512, 512),
                            deps=[lns_free[lb]])
                for ft in range(8):
                    pb = ft % 2
                    g = None
                    for dc in range(DC):
                        g = P.mm(ph[pb][:],
                                 w1q[slot][:, dc * 1024 + ft * 128:
                                            dc * 1024 + (ft + 1) * 128],
                                 lns[lb][:, dc * 512:(dc + 1) * 512],
                                 dc == 0, dc == DC - 1,
                                 deps=[w1_tok[slot], ld, ph_free[pb]],
                                 inc=(dc == DC - 1))
                    h_tok[ft] = P.act(hbf[:, ft * 512:(ft + 1) * 512], ph[pb][:],
                                      AF.Gelu,
                                      bias=b1cols[:, (fq * 8 + ft) * 2 + k:
                                                  (fq * 8 + ft) * 2 + k + 1],
                                      deps=[g, rt['b1c'], h_free])
                    ph_free[pb] = h_tok[ft]
                for grp in range(2):
                    gt = None
                    for fb in range(8):
                        for dtl in range(4):
                            dt = grp * 4 + dtl
                            gt = P.mm(py[dtl][:],
                                      w2q[:, fb * 1024 + dt * 128:
                                           fb * 1024 + (dt + 1) * 128],
                                      hbf[:, fb * 512:(fb + 1) * 512],
                                      fb == 0, fb == 7,
                                      deps=[h_tok[fb], w2_tok, py_free[dtl]],
                                      inc=(fb == 7 and dtl == 3))
                    yb = ysb[grp]
                    ev = None
                    for dtl in range(4):
                        ev = P.v('tensor_scalar',
                                 yb[:, dtl * 512:(dtl + 1) * 512], py[dtl][:],
                                 gcol[:, k:k + 1], None, OP.mult,
                                 deps=[gt, rt['gcol'], ysb_free[grp]])
                        py_free[dtl] = ev
                    dst = y_acc[grp * 512:(grp + 1) * 512,
                                tb * 512:(tb + 1) * 512].rearrange(
                                    '(d p) w -> d p w', p=128).transpose([1, 0, 2])
                    srcv = yb[:].rearrange('p (d w) -> p d w', w=512)
                    y_chain[tb] = P.gstore(dst, srcv,
                                           deps=[ev, y_chain[tb]],
                                           accum=not first_call_iter(it_i))
                    ysb_free[grp] = y_chain[tb]
                h_free = gt
                lns_free[lb] = gt
            w1_free[slot] = P.snap_tensor_token()
            w2_free = w1_free[slot]

        sc.close()
        P.set_base(P.snap())
        sc = contextlib.ExitStack()
        ysb2 = [sbt(sc, f'fc_y{i}', [128, DC * 512])
                for i in range(2)]
        xsb2 = [sbt(sc, f'fc_x{i}', [128, DC * 512])
                for i in range(2)]
        tsb2 = [sbt(sc, f'fc_t{i}', [128, DC * 512])
                for i in range(2)]
        y2f = [None, None]
        x2f = [None, None]
        t2f = [None, None]
        for tb in range(TBN):
            b = tb % 2
            ly = P.load(sb3(ysb2[b][:], 512), fmc(y_acc, tb * 512, 512),
                        deps=[y2f[b], y_chain[tb]])
            lx = P.load(sb3(xsb2[b][:], 512), fmc(x_fm, tb * 512, 512),
                        deps=[x2f[b]])
            last = None
            for dc in range(DC):
                sl = slice(dc * 512, (dc + 1) * 512)
                q1 = P.v('tensor_scalar', tsb2[b][:, sl], ysb2[b][:, sl],
                         b2comb[:, dc:dc + 1], None, OP.add,
                         deps=[ly, rt['b2cb'], t2f[b]])
                q2 = P.v('tensor_tensor', ysb2[b][:, sl], tsb2[b][:, sl],
                         xsb2[b][:, sl], OP.add, deps=[q1, lx])
                last = q2
            st = P.store(fmc(x_fm, tb * 512, 512), sb3(ysb2[b][:], 512),
                         deps=[last])
            y2f[b] = st
            x2f[b] = last
            t2f[b] = last
        sc.close()

    def first_call_iter(it_i):
        return it_i == 0

    def snap_tensor_token():
        c = P.cnt['st']
        return ('st', c) if c > 0 else None
    P.snap_tensor_token = snap_tensor_token

    def phase_xproj():
        sc = contextlib.ExitStack()
        lns = [sbt(sc, f'xp_ln{i}', [128, DC * 512]) for i in range(2)]
        wct = [sbt(sc, f'xp_wct{i}', [128, DC * 128]) for i in range(2)]
        wuv = [sbt(sc, f'xp_wuv{i}', [128, DC * 512]) for i in range(2)]
        qst = [sbt(sc, f'xp_qst{i}', [128, 512]) for i in range(2)]
        ust = [sbt(sc, f'xp_ust{i}', [128, 512]) for i in range(2)]
        vst = [sbt(sc, f'xp_vst{i}', [128, 512]) for i in range(2)]
        b1a_sb = sbt(sc, 'xp_b1a', [1, FF])
        b1f = sbt(sc, 'xp_b1f', [128, 24])
        pp = [pst(sc, f'xp_pp{i}', [128, 512]) for i in range(2)]
        p_t = [pst(sc, f'xp_pt{i}', [128, 1]) for i in range(2)]

        def colof(j):
            return j * 128 if j < 8 else 2048 + (j - 8) * 128

        lb1 = P.load(b1a_sb[:], b1a)
        ptf = [None, None]
        bf_tok = None
        for j in range(24):
            pb = j % 2
            c0 = colof(j)
            tr = P.transp(p_t[pb][:], b1a_sb[0:1, c0:c0 + 128],
                          ident[0:1, 0:1], deps=[lb1, ptf[pb]])
            cp = P.v('tensor_copy', b1f[:, j:j + 1], p_t[pb][:], deps=[tr])
            ptf[pb] = cp
            bf_tok = cp

        lns_free = [None, None]
        wct_tok = [None, None]
        wct_free = [None, None]
        wuv_tok = [None, None]
        wuv_free = [None, None]
        pp_free = [None, None]
        qst_free = [None, None]
        ust_free = [None, None]
        vst_free = [None, None]
        g = None
        for tb in range(TBN):
            lb = tb % 2
            ld = P.load(sb3(lns[lb][:], 512), fmc(ln_fm, tb * 512, 512),
                        deps=[lns_free[lb]])
            for j in range(24):
                wb = j % 2
                c0 = colof(j)
                wct_tok[wb] = P.load(
                    sb3(wct[wb][:], 128),
                    w1a[:, c0:c0 + 128].rearrange('(c p) w -> p c w', p=128),
                    deps=[wct_free[wb]])
                g = None
                for dc in range(DC):
                    g = P.mm(pp[wb][:],
                             wct[wb][:, dc * 128:(dc + 1) * 128],
                             lns[lb][:, dc * 512:(dc + 1) * 512],
                             dc == 0, dc == DC - 1,
                             deps=[wct_tok[wb], ld, pp_free[wb]],
                             inc=(dc == DC - 1))
                sl = P.act(qst[wb][:], pp[wb][:], AF.Silu,
                           bias=b1f[:, j:j + 1], deps=[g, bf_tok, qst_free[wb]])
                st = P.store(proj_fm[j * 128:(j + 1) * 128,
                                     tb * 512:(tb + 1) * 512], qst[wb][:],
                             deps=[sl])
                pp_free[wb] = sl
                qst_free[wb] = st
                wct_free[wb] = g
            for ci, cc in enumerate((2, 3)):
                wb = ci % 2
                wuv_tok[wb] = P.load(
                    sb3(wuv[wb][:], 512),
                    w1a[:, cc * 512:(cc + 1) * 512].rearrange(
                        '(c p) w -> p c w', p=128),
                    deps=[wuv_free[wb]])
                for t4 in range(4):
                    pb = (ci * 4 + t4) % 2
                    g = None
                    for dc in range(DC):
                        g = P.mm(pp[pb][:],
                                 lns[lb][:, dc * 512 + t4 * 128:
                                         dc * 512 + (t4 + 1) * 128],
                                 wuv[wb][:, dc * 512:(dc + 1) * 512],
                                 dc == 0, False,
                                 deps=[wuv_tok[wb], ld, pp_free[pb]])
                    g = P.mm(pp[pb][:], ones_row[:],
                             b1a_sb[0:1, cc * 512:(cc + 1) * 512],
                             False, True, deps=[lb1], inc=True)
                    row0 = tb * 512 + t4 * 128
                    sl = P.act(ust[pb][:], pp[pb][:], AF.Silu,
                               deps=[g, ust_free[pb]])
                    sc_ = P.v('tensor_scalar', vst[pb][:], ust[pb][:],
                              1.0 / T, None, OP.mult,
                              deps=[sl, vst_free[pb]])
                    st = P.store(v_tm[row0:row0 + 128,
                                      (cc - 2) * 512:(cc - 1) * 512],
                                 vst[pb][:], deps=[sc_])
                    ust_free[pb] = sc_
                    vst_free[pb] = st
                    pp_free[pb] = sl
                wuv_free[wb] = g
            lns_free[lb] = g
        sc.close()

    def phase_attn_core():
        sc = contextlib.ExitStack()
        qh = [sbt(sc, f'at_q{i}', [128, T]) for i in range(2)]
        kh = [sbt(sc, f'at_k{i}', [128, T]) for i in range(2)]
        vh = [sbt(sc, f'at_v{i}', [128, NTT * 64]) for i in range(2)]
        uh = [sbt(sc, f'at_u{i}', [64, T]) for i in range(2)]
        att = [sbt(sc, f'at_a{i}', [128, 512]) for i in range(2)]
        avf = [sbt(sc, f'at_avf{i}', [64, T]) for i in range(2)]
        ps = [pst(sc, f'at_ps{i}', [128, 512]) for i in range(2)]
        pav = [pst(sc, f'at_pav{i}', [64, 512]) for i in range(2)]

        zp = []
        for i in range(2):
            zp.append(P.v('memset', qh[i][64:128, :], 0.0))
            zp.append(P.v('memset', kh[i][64:128, :], 0.0))
        zp_tok = zp[-1]

        set_free = [None, None]
        ps_free = [None, None]
        pav_free = [None, None]
        att_free = [None, None]
        avf_free = [None, None]
        for h in range(NH):
            hb = h % 2
            lq = P.load(qh[hb][0:64, :], proj_fm[D + h * 64:D + (h + 1) * 64, :],
                        deps=[set_free[hb], zp_tok])
            lk = P.load(kh[hb][0:64, :],
                        proj_fm[2 * D + h * 64:2 * D + (h + 1) * 64, :],
                        deps=[set_free[hb], zp_tok])
            lv = P.load(vh[hb][:].rearrange('p (t w) -> p t w', w=64),
                        v_tm[:, h * 64:(h + 1) * 64].rearrange(
                            '(t p) w -> p t w', p=128),
                        deps=[set_free[hb]])
            lu = P.load(uh[hb][:], proj_fm[h * 64:(h + 1) * 64, :],
                        deps=[set_free[hb]])
            mu_t = None
            for qc in range(8):
                gav = None
                for kt in range(NTT):
                    ab = kt % 2
                    sm = P.mm(ps[ab][:],
                              kh[hb][:, kt * 128:(kt + 1) * 128],
                              qh[hb][:, qc * 512:(qc + 1) * 512],
                              True, True,
                              deps=[lq, lk, ps_free[ab]], inc=True)
                    sl = P.act(att[ab][:], ps[ab][:], AF.Silu,
                               deps=[sm, att_free[ab]])
                    ps_free[ab] = sl
                    gav = P.mm(pav[qc % 2][:],
                               vh[hb][:, kt * 64:(kt + 1) * 64],
                               att[ab][:],
                               kt == 0, kt == NTT - 1,
                               deps=[sl, lv] +
                                    ([pav_free[qc % 2]] if kt == 0 else []),
                               inc=(kt == NTT - 1))
                    att_free[ab] = gav if kt == NTT - 1 else None
                mu_t = P.v('tensor_tensor',
                           avf[hb][:, qc * 512:(qc + 1) * 512],
                           pav[qc % 2][:],
                           uh[hb][:, qc * 512:(qc + 1) * 512], OP.mult,
                           deps=[gav, lu, avf_free[hb]])
                pav_free[qc % 2] = mu_t
            stv = P.store(avu_fm[h * 64:(h + 1) * 64, :], avf[hb][:],
                          deps=[mu_t])
            avf_free[hb] = stv
            set_free[hb] = mu_t

        sc.close()
        return None

    def phase_outproj(stats_tok, dump_z=False):
        sc = contextlib.ExitStack()
        w2g = sbt(sc, 'op_w2g', [128, DC * 1024])
        wch = [sbt(sc, f'op_wch{i}', [128, 1024]) for i in range(2)]
        gcols = sbt(sc, 'op_gc', [128, 8])
        bcols = sbt(sc, 'op_bc', [128, 8])
        b2cols = sbt(sc, 'op_b2c', [128, 8])
        biasc = sbt(sc, 'op_bias', [128, 8])
        avs = [sbt(sc, f'op_av{i}', [128, DC * 512]) for i in range(2)]
        avq = [sbt(sc, 'op_aq0', [128, DC * 512])] * 2
        tpz = [sbt(sc, 'op_tp0', [128, DC * 512])] * 2
        zfm = [sbt(sc, f'op_zfm{i}', [128, DC * 512]) for i in range(2)]
        xch = [sbt(sc, 'op_xch0', [128, DC * 512])] * 2
        xnb = [sbt(sc, 'op_xnb0', [128, DC * 512])] * 2
        yst = [sbt(sc, 'op_yst0', [128, DC * 512])] * 2
        mu_s = sbt(sc, 'op_mu', [1, 512])
        ms_s = sbt(sc, 'op_ms', [1, 512])
        m2_s = sbt(sc, 'op_m2', [1, 512])
        va_s = sbt(sc, 'op_va', [1, 512])
        sd_s = sbt(sc, 'op_sd', [1, 512])
        rs_s = sbt(sc, 'op_rs', [1, 512])
        p_b = pst(sc, 'op_pb', [128, 8])
        p_y = [pst(sc, f'op_py{i}', [128, 512]) for i in range(2)]
        p_sum = pst(sc, 'op_psum', [1, 512])
        p_sq = pst(sc, 'op_psq', [1, 512])
        p_mu = pst(sc, 'op_pmu', [128, 512])
        p_rs = pst(sc, 'op_prs', [128, 512])

        lg_ = P.load(gcols[:], gam[0].rearrange('(c p) -> p c', p=128),
                     slow=True)
        lb_ = P.load(bcols[:], bet[0].rearrange('(c p) -> p c', p=128),
                     slow=True)
        lb2 = P.load(b2cols[:], b2a[0].rearrange('(c p) -> p c', p=128),
                     slow=True)
        wch_tok = [None, None]
        wch_free = [None, None]
        bias_mm = None
        w2g_tok = None
        for dc in range(DC):
            wb = dc % 2
            wch_tok[wb] = P.load(wch[wb][:], w2a[dc * 128:(dc + 1) * 128, :],
                                 deps=(wch_free[wb] or []))
            for dt in range(DC):
                bias_mm = P.mm(p_b[:, dt:dt + 1],
                               wch[wb][:, dt * 128:(dt + 1) * 128],
                               bcols[:, dc:dc + 1],
                               dc == 0 and dt == 0,
                               dc == DC - 1 and dt == DC - 1,
                               deps=[wch_tok[wb], lb_],
                               inc=(dt == DC - 1))
            w2g_tok = P.v('tensor_scalar', w2g[:, dc * 1024:(dc + 1) * 1024],
                          wch[wb][:], gcols[:, dc:dc + 1], None, OP.mult,
                          deps=[wch_tok[wb], lg_, w2g_tok])
            wch_free[wb] = [w2g_tok, bias_mm]
        bias_tok = P.v('tensor_tensor', biasc[:], p_b[:], b2cols[:], OP.add,
                       deps=[bias_mm, lb2])

        av_free = [None, None]
        aq_free = SharedTok()
        tp_free = SharedTok()
        zfm_free = [None, None]
        py_free = [None, None]
        xch_free = SharedTok()
        xnb_free = SharedTok()
        yst_free = SharedTok()
        war_psum = war_psq = war_pmu = war_prs = None
        for tb in range(TBN):
            zb = tb % 2
            la = P.load(sb3(avs[zb][:], 512), fmc(avu_fm, tb * 512, 512),
                        deps=[av_free[zb]])
            lx = P.load(sb3(xch[zb][:], 512), fmc(x_fm, tb * 512, 512),
                        deps=[xch_free[zb]])
            sq = None
            for dc in range(DC):
                sq = P.act(avq[zb][:, dc * 512:(dc + 1) * 512],
                           avs[zb][:, dc * 512:(dc + 1) * 512], AF.Square,
                           deps=[la, aq_free[zb]])
            tsum = tsq = None
            for dc in range(DC):
                tsum = P.mm(p_sum[:], ones_col[:],
                            avs[zb][:, dc * 512:(dc + 1) * 512],
                            dc == 0, dc == DC - 1,
                            deps=[la, war_psum], inc=(dc == DC - 1))
            for dc in range(DC):
                tsq = P.mm(p_sq[:], ones_col[:],
                           avq[zb][:, dc * 512:(dc + 1) * 512],
                           dc == 0, dc == DC - 1,
                           deps=[sq, war_psq], inc=(dc == DC - 1))
            u1 = P.v('tensor_scalar', mu_s[:], p_sum[:], 1.0 / D, None,
                     OP.mult, deps=[tsum])
            u2 = P.v('tensor_scalar', ms_s[:], p_sq[:], 1.0 / D, EPS,
                     OP.mult, OP.add, deps=[tsq])
            u3 = P.v('tensor_tensor', m2_s[:], mu_s[:], mu_s[:], OP.mult,
                     deps=[u1])
            u4 = P.v('tensor_tensor', va_s[:], ms_s[:], m2_s[:], OP.subtract,
                     deps=[u2, u3])
            u5 = P.act(sd_s[:], va_s[:], AF.Sqrt, deps=[u4])
            u6 = P.v('reciprocal', rs_s[:], sd_s[:], deps=[u5])
            war_psum = u1
            war_psq = u2
            bmu = P.mm(p_mu[:], ones_row[:], mu_s[:], True, True,
                       deps=[u3, war_pmu], inc=True)
            brs = P.mm(p_rs[:], ones_row[:], rs_s[:], True, True,
                       deps=[u6, war_prs], inc=True)
            last_n = None
            for dc in range(DC):
                sl = slice(dc * 512, (dc + 1) * 512)
                n1 = P.v('tensor_tensor', tpz[zb][:, sl], avs[zb][:, sl],
                         p_mu[:], OP.subtract, deps=[bmu, la, tp_free[zb]])
                n2 = P.v('tensor_tensor', zfm[zb][:, sl], tpz[zb][:, sl],
                         p_rs[:], OP.mult, deps=[n1, brs, zfm_free[zb]])
                last_n = n2
            war_pmu = last_n
            war_prs = last_n
            if dump_z:
                P.store(fmc(y_acc, tb * 512, 512), sb3(zfm[zb][:], 512),
                        deps=[last_n])
            last_q = None
            for dt in range(DC):
                pb = dt % 2
                g = None
                for dc in range(DC):
                    g = P.mm(p_y[pb][:],
                             w2g[:, dc * 1024 + dt * 128:
                                 dc * 1024 + (dt + 1) * 128],
                             zfm[zb][:, dc * 512:(dc + 1) * 512],
                             dc == 0, dc == DC - 1,
                             deps=[w2g_tok, last_n, py_free[pb]],
                             inc=(dc == DC - 1))
                sl = slice(dt * 512, (dt + 1) * 512)
                q1 = P.v('tensor_scalar', yst[zb][:, sl], p_y[pb][:],
                         biasc[:, dt:dt + 1], None, OP.add,
                         deps=[g, bias_tok, yst_free[zb]])
                q2 = P.v('tensor_tensor', xnb[zb][:, sl], yst[zb][:, sl],
                         xch[zb][:, sl], OP.add, deps=[q1, lx, xnb_free[zb]])
                py_free[pb] = q1
                last_q = q2
            st = P.store(fmc(x_fm, tb * 512, 512), sb3(xnb[zb][:], 512),
                         deps=[last_q])
            av_free[zb] = last_n
            aq_free[zb] = tsq
            tp_free[zb] = last_n
            zfm_free[zb] = last_q
            xch_free[zb] = last_q
            xnb_free[zb] = st
            yst_free[zb] = last_q
        sc.close()

    def phase_out_transpose(src):
        sc = contextlib.ExitStack()
        chunk = [sbt(sc, f'pf_ch{i}', [128, T])
                 for i in range(DC)]
        ostg = [sbt(sc, f'pf_os{i}', [128, D])
                for i in range(2)]
        ptr = [pst(sc, f'pf_ps{i}', [128, 128])
               for i in range(2)]
        lds = [P.load(chunk[dc][:], src[dc * 128:(dc + 1) * 128, :])
               for dc in range(DC)]
        ptr_free = [None, None]
        ostg_free = [None, None]
        i = 0
        for tt in range(NTT):
            ob = tt % 2
            last_cp = None
            for dc in range(DC):
                pb = i % 2
                tr = P.transp(ptr[pb][:],
                              chunk[dc][:, tt * 128:(tt + 1) * 128],
                              ident[:], deps=[lds[dc], ptr_free[pb]])
                cp = P.v('tensor_copy', ostg[ob][:, dc * 128:(dc + 1) * 128],
                         ptr[pb][:], deps=[tr, ostg_free[ob]])
                ptr_free[pb] = cp
                last_cp = cp
                i += 1
            st = P.store(out[tt * 128:(tt + 1) * 128, :], ostg[ob][:],
                         deps=[last_cp])
            ostg_free[ob] = st
        sc.close()

    def phase_copy_tm(srcd, col0):
        sc = contextlib.ExitStack()
        stg = [sbt(sc, f'cp_st{i}', [128, D]) for i in range(2)]
        free = [None, None]
        for tt in range(NTT):
            b = tt % 2
            ld = P.load(stg[b][:], srcd[tt * 128:(tt + 1) * 128, col0:col0 + D],
                        deps=[free[b]])
            st = P.store(out[tt * 128:(tt + 1) * 128, :], stg[b][:], deps=[ld])
            free[b] = st
        sc.close()

    phase_in_transpose()
    P.set_base(P.snap())
    if mode == 'transpose':
        phase_out_transpose(x_fm)
    else:
        pc, pt = phase_ln(x_fm, ln_fm, True)
        P.set_base(P.snap())
        if mode == 'ln':
            phase_out_transpose(ln_fm)
        else:
            rt = phase_routing(pc, pt)
            P.set_base(P.snap())
            phase_ffn(rt, True)
            P.set_base(P.snap())
            if mode == 'ffn1':
                phase_out_transpose(x_fm)
            else:
                phase_ln(x_fm, ln_fm, False)
                P.set_base(P.snap())
                phase_xproj()
                P.set_base(P.snap())
                if mode in ('qpart', 'kpart', 'upart', 'vpart'):
                    if mode == 'qpart':
                        phase_out_transpose(proj_fm[D:2 * D, :])
                    elif mode == 'kpart':
                        phase_out_transpose(proj_fm[2 * D:3 * D, :])
                    elif mode == 'upart':
                        phase_out_transpose(proj_fm[0:D, :])
                    else:
                        phase_copy_tm(v_tm, 0)
                    P.set_base([])
                    P._op('sync', None, False, P.snap(), lambda h: h.nop())
                    with nc.Block() as block:
                        P.emit(block, sems)
                    es.close()
                    return nc
                stats_tok = phase_attn_core()
                P.set_base(P.snap())
                if mode == 'avu':
                    phase_out_transpose(avu_fm)
                    P.set_base([])
                    P._op('sync', None, False, P.snap(), lambda h: h.nop())
                    with nc.Block() as block:
                        P.emit(block, sems)
                    es.close()
                    return nc
                phase_outproj(stats_tok, dump_z=(mode == 'zdump'))
                P.set_base(P.snap())
                if mode == 'zdump':
                    phase_out_transpose(y_acc)
                elif mode == 'attn':
                    phase_out_transpose(x_fm)
                else:
                    pc, pt = phase_ln(x_fm, ln_fm, True)
                    P.set_base(P.snap())
                    rt = phase_routing(pc, pt)
                    P.set_base(P.snap())
                    phase_ffn(rt, True)
                    P.set_base(P.snap())
                    phase_out_transpose(x_fm)

    P.set_base([])
    P._op('sync', None, False, P.snap(), lambda h: h.nop())

    with nc.Block() as block:
        P.emit(block, sems)
    es.close()
    return nc


_NC_CACHE = {}


def _get_program(mode):
    if mode not in _NC_CACHE:
        _NC_CACHE[mode] = build_program(mode)
    return _NC_CACHE[mode]


def kernel(L0_attn_x, L0_attn_W1, L0_attn_b1, L0_attn_W2, L0_attn_b2,
           L0_attn_gamma, L0_attn_beta, L0_ffn_Wr, L0_ffn_W1, L0_ffn_b1,
           L0_ffn_W2, L0_ffn_b2, mode='full'):
    f = lambda a: np.ascontiguousarray(np.asarray(a, dtype=np.float32))
    x = f(L0_attn_x)
    shared = dict(
        w1a=f(L0_attn_W1), b1a=f(L0_attn_b1).reshape(1, FF),
        w2a=f(L0_attn_W2), b2a=f(L0_attn_b2).reshape(1, D),
        gam=f(L0_attn_gamma).reshape(1, D), bet=f(L0_attn_beta).reshape(1, D),
        wr=f(L0_ffn_Wr), w1m=f(L0_ffn_W1), b1m=f(L0_ffn_b1),
        w2m=f(L0_ffn_W2), b2m=f(L0_ffn_b2))
    in_maps = [dict(x=np.ascontiguousarray(x[b]), **shared)
               for b in range(N_CORES)]
    nc = _get_program(mode)
    res = run_bass_kernel_spmd(nc, in_maps, core_ids=list(range(N_CORES)))
    return np.stack([res.results[b]['out'] for b in range(N_CORES)], axis=0)


# version 03
# speedup vs baseline: 1.3615x; 1.1336x over previous
import contextlib
import numpy as np

import concourse.bass as bass
import concourse.mybir as mybir
from concourse.bass_utils import run_bass_kernel_spmd

F32 = mybir.dt.float32
BF16 = mybir.dt.bfloat16
U32 = mybir.dt.uint32
AF = mybir.ActivationFunctionType
OP = mybir.AluOpType
AX = mybir.AxisListType

D, T, FF, NH, DH, E = 1024, 4096, 4096, 16, 64, 8
DC = D // 128
TBN = T // 512
NTT = T // 128
EPS = 1e-5
N_CORES = 8

ENGINES = ('sync', 'gpsimd', 'tensor', 'scalar', 'vector')


class SharedTok:

    def __init__(self):
        self.t = None

    def __getitem__(self, i):
        return self.t

    def __setitem__(self, i, v):
        self.t = v


class Prog:

    NPOOL = 6

    def __init__(self):
        self.streams = {e: [] for e in ENGINES}
        self.cnt = dict(st=0, ss=0, sv=0, sg=0)
        for q in ('dl', 'ds', 'dg'):
            for i in range(self.NPOOL):
                self.cnt[f'{q}{i}'] = 0
        self.rr = dict(dl=0, ds=0, dg=0)
        self.base = {}

    def _dma(self, eng, q, deps, fn):
        i = self.rr[q]
        self.rr[q] = (i + 1) % self.NPOOL
        sem = f'{q}{i}'
        prev = self.cnt[sem]
        dd = list(deps)
        if prev:
            dd.append((sem, prev))
        return self._op(eng, sem, True, dd, fn)

    def set_base(self, toks):
        self.base = {}
        for t in toks:
            if t is None:
                continue
            s, c = t
            if c > self.base.get(s, 0):
                self.base[s] = c

    def _op(self, eng, sem, dma, deps, fn):
        tok = None
        if sem is not None:
            self.cnt[sem] += 16 if dma else 1
            tok = (sem, self.cnt[sem])
        dd = dict(self.base)
        for d in deps:
            if d is None:
                continue
            s, c = d
            if c > dd.get(s, 0):
                dd[s] = c
        self.streams[eng].append((fn, dd, sem, 16 if dma else 1))
        return tok

    def load(self, dst, src, deps=(), slow=False):
        if slow:
            return self._dma('sync', 'dl', deps,
                             lambda h, a=dst, b=src: h.dma_start(
                                 out=a, in_=b, allow_slow_non_contiguous=True))
        return self._dma('sync', 'dl', deps,
                         lambda h, a=dst, b=src: h.dma_start(out=a, in_=b))

    def store(self, dst, src, deps=()):
        return self._dma('scalar', 'ds', deps,
                         lambda h, a=dst, b=src: h.dma_start(out=a, in_=b))

    def gstore(self, dst, src, deps=(), accum=False):
        if accum:
            return self._dma('gpsimd', 'dg', deps,
                             lambda h, a=dst, b=src:
                                 h.dma_start(out=a, in_=b, accum_op=OP.add))
        return self._dma('gpsimd', 'dg', deps,
                         lambda h, a=dst, b=src: h.dma_start(out=a, in_=b))

    def gather(self, dst, src_rows, idx, deps=()):
        return self._dma('gpsimd', 'dg', deps,
                         lambda h, a=dst, b=src_rows, i=idx: h.indirect_dma_start(
                             out=a, out_offset=None, in_=b,
                             in_offset=bass.IndirectOffsetOnAxis(ap=i, axis=0)))

    def mm(self, out, stat, mov, start, stop, deps=(), inc=False):
        return self._op('tensor', 'st' if inc else None, False, deps,
                        lambda h, o=out, s=stat, m=mov, a=start, b=stop:
                            h.matmul(o, s, m, start=a, stop=b))

    def transp(self, out, in_, ident, deps=(), inc=True):
        return self._op('tensor', 'st' if inc else None, False, deps,
                        lambda h, o=out, i=in_, d=ident: h.transpose(o, i, d))

    def act(self, out, in_, func, deps=(), bias=None, scale=1.0):
        if bias is None:
            return self._op('scalar', 'ss', False, deps,
                            lambda h, o=out, i=in_, f=func, sc=scale:
                                h.activation(o, i, f, scale=sc))
        return self._op('scalar', 'ss', False, deps,
                        lambda h, o=out, i=in_, f=func, b=bias, sc=scale:
                            h.activation(o, i, f, bias=b, scale=sc))

    def v(self, name, *args, deps=(), **kw):
        return self._op('vector', 'sv', False, deps,
                        lambda h, n=name, a=args, k=kw: getattr(h, n)(*a, **k))

    def g(self, name, *args, deps=(), **kw):
        return self._op('gpsimd', 'sg', False, deps,
                        lambda h, n=name, a=args, k=kw: getattr(h, n)(*a, **k))

    def snap(self):
        return [(s, c) for s, c in self.cnt.items() if c > 0]

    def emit(self, block, sems):
        for eng in ENGINES:
            items = self.streams[eng]
            if not items:
                continue

            def mk(items=items):
                def body(h):
                    hw = {}
                    for fn, dd, sem, amt in items:
                        for s, c in sorted(dd.items()):
                            if c > hw.get(s, 0):
                                h.wait_ge(sems[s], c)
                                hw[s] = c
                        inst = fn(h)
                        if sem is not None:
                            inst.then_inc(sems[sem], amt)
                return body
            getattr(block, eng)(mk())


def build_program(mode='full'):
    nc = bass.Bass('TRN2', target_bir_lowering=False)
    x_in = nc.dram_tensor('x', [T, D], F32, kind='ExternalInput').ap()
    w1a = nc.dram_tensor('w1a', [D, FF], F32, kind='ExternalInput').ap()
    b1a = nc.dram_tensor('b1a', [1, FF], F32, kind='ExternalInput').ap()
    w2a = nc.dram_tensor('w2a', [D, D], F32, kind='ExternalInput').ap()
    b2a = nc.dram_tensor('b2a', [1, D], F32, kind='ExternalInput').ap()
    gam = nc.dram_tensor('gam', [1, D], F32, kind='ExternalInput').ap()
    bet = nc.dram_tensor('bet', [1, D], F32, kind='ExternalInput').ap()
    wr = nc.dram_tensor('wr', [D, E], F32, kind='ExternalInput').ap()
    w1m = nc.dram_tensor('w1m', [E, D, FF], F32, kind='ExternalInput').ap()
    b1m = nc.dram_tensor('b1m', [E, FF], F32, kind='ExternalInput').ap()
    w2m = nc.dram_tensor('w2m', [E, FF, D], F32, kind='ExternalInput').ap()
    b2m = nc.dram_tensor('b2m', [E, D], F32, kind='ExternalInput').ap()
    out = nc.dram_tensor('out', [T, D], F32, kind='ExternalOutput').ap()

    x_fm = nc.dram_tensor('x_fm', [D, T], F32).ap()
    ln_fm = nc.dram_tensor('ln_fm', [D, T], F32).ap()
    lnh_fm = nc.dram_tensor('lnh_fm', [D, T], BF16).ap()
    lnl_fm = nc.dram_tensor('lnl_fm', [D, T], BF16).ap()
    y_acc = nc.dram_tensor('y_acc', [D, T], F32).ap()
    proj_fm = nc.dram_tensor('proj_fm', [3 * D, T], F32).ap()
    v_tm = nc.dram_tensor('v_tm', [T, D], F32).ap()
    avu_fm = nc.dram_tensor('avu_fm', [D, T], F32).ap()

    w1rows = w1m.rearrange('e d (q c) -> (e d q) c', c=1024)
    w2rows = w2m.rearrange('e f d -> (e f) d')

    def fmc(dram, t0, w):
        return dram[:, t0:t0 + w].rearrange('(c p) w -> p c w', p=128)

    def sb3(ap, w):
        return ap.rearrange('p (c w) -> p c w', w=w)

    P = Prog()
    uid = [0]

    def sbt(sc_, name, shape, dtype=F32):
        uid[0] += 1
        return sc_.enter_context(
            nc.sbuf_tensor(f'{name}_{uid[0]}', shape, dtype)).ap()

    def pst(sc_, name, shape):
        uid[0] += 1
        return sc_.enter_context(
            nc.psum_tensor(f'{name}_{uid[0]}', shape, F32)).ap()

    def mm3(out, sh, sl_, mh, ml, first, last, deps=(), inc=False):
        P.mm(out, sh, mh, first, False, deps=deps)
        P.mm(out, sh, ml, False, False, deps=deps)
        return P.mm(out, sl_, mh, False, last, deps=deps, inc=inc)

    es = contextlib.ExitStack()
    sems = {s: es.enter_context(nc.semaphore(s)) for s in P.cnt}

    ident = nc.alloc_sbuf_tensor('ident', [128, 128], F32).ap()
    iota128 = nc.alloc_sbuf_tensor('iota128', [128, 1], F32).ap()
    iota4 = nc.alloc_sbuf_tensor('iota4', [128, 1], F32).ap()
    iota8 = nc.alloc_sbuf_tensor('iota8', [1, 8], F32).ap()
    ones_row = nc.alloc_sbuf_tensor('ones_row', [1, 128], F32).ap()
    ones_col = nc.alloc_sbuf_tensor('ones_col', [128, 1], F32).ap()
    pooled = nc.alloc_sbuf_tensor('pooled', [128, 16], F32).ap()
    gcol = nc.alloc_sbuf_tensor('gcol', [128, 4], F32).ap()
    b1cols = nc.alloc_sbuf_tensor('b1cols', [128, 64], F32).ap()
    b2comb = nc.alloc_sbuf_tensor('b2comb', [128, 8], F32).ap()
    idxw1 = nc.alloc_sbuf_tensor('idxw1', [128, 64], U32).ap()
    idxw2 = nc.alloc_sbuf_tensor('idxw2', [128, 64], U32).ap()

    c0 = P.g('memset', ident[:], 0.0)
    c1 = P.g('affine_select', out=ident[:], in_=ident[:],
             compare_op=OP.not_equal, fill=1.0, base=0,
             pattern=[[-1, 128]], channel_multiplier=1, deps=[c0])
    c2 = P.g('iota', iota128[:], [[0, 1]], channel_multiplier=1,
             allow_small_or_imprecise_dtypes=True, deps=[c1])
    c3 = P.g('iota', iota8[:], [[1, 8]], channel_multiplier=0,
             allow_small_or_imprecise_dtypes=True, deps=[c2])
    c4 = P.g('memset', ones_row[:], 1.0, deps=[c3])
    c5 = P.g('memset', ones_col[:], 1.0, deps=[c4])
    c6 = P.v('tensor_scalar', iota4[:], iota128[:], 4.0, None, OP.mult,
             deps=[c5])
    P.set_base([c6])

    def phase_in_transpose():
        sc = contextlib.ExitStack()
        stage = [sbt(sc, f'p0stg{i}', [128, T])
                 for i in range(DC)]
        xin = [sbt(sc, f'p0xin{i}', [128, D])
               for i in range(2)]
        ptr = [pst(sc, f'p0ps{i}', [128, 128])
               for i in range(2)]
        xin_free = [None, None]
        ptr_free = [None, None]
        last_cp = None
        i = 0
        for tt in range(NTT):
            b = tt % 2
            ld = P.load(xin[b][:], x_in[tt * 128:(tt + 1) * 128, :],
                        deps=[xin_free[b]])
            for dc in range(DC):
                pb = i % 2
                tr = P.transp(ptr[pb][:], xin[b][:, dc * 128:(dc + 1) * 128],
                              ident[:], deps=[ld, ptr_free[pb]])
                cp = P.v('tensor_copy', stage[dc][:, tt * 128:(tt + 1) * 128],
                         ptr[pb][:], deps=[tr])
                ptr_free[pb] = cp
                last_cp = cp
                i += 1
            xin_free[b] = last_cp
        for dc in range(DC):
            P.store(x_fm[dc * 128:(dc + 1) * 128, :], stage[dc][:],
                    deps=[last_cp])
        sc.close()

    def phase_ln(src, dst, want_pooled):
        sc = contextlib.ExitStack()
        xs = [sbt(sc, f'lnxs{i}', [128, DC * 512])
              for i in range(2)]
        xq = [sbt(sc, f'lnxq{i}', [128, DC * 512])
              for i in range(2)]
        tp = [sbt(sc, f'lntp{i}', [128, DC * 512])
              for i in range(2)]
        lo = [sbt(sc, f'lnlo{i}', [128, DC * 512])
              for i in range(2)]
        lhh = [sbt(sc, f'lnhh{i}', [128, DC * 512], BF16) for i in range(2)]
        lll = [sbt(sc, f'lnll{i}', [128, DC * 512], BF16) for i in range(2)]
        mu_s = sbt(sc, 'ln_mu', [1, 512])
        ms_s = sbt(sc, 'ln_ms', [1, 512])
        m2_s = sbt(sc, 'ln_m2', [1, 512])
        va_s = sbt(sc, 'ln_va', [1, 512])
        sd_s = sbt(sc, 'ln_sd', [1, 512])
        rs_s = sbt(sc, 'ln_rs', [1, 512])
        plr = sbt(sc, 'ln_plr', [128, 1])
        p_sum = pst(sc, 'ln_psum', [1, 512])
        p_sq = pst(sc, 'ln_psq', [1, 512])
        p_mu = pst(sc, 'ln_pmu', [128, 512])
        p_rs = pst(sc, 'ln_prs', [128, 512])

        xs_free = [None, None]
        xq_free = [None, None]
        tp_free = [None, None]
        lo_free = [None, None]
        lh_free = [None, None]
        ll_free = [None, None]
        war_psum = war_psq = war_pmu = war_prs = None
        pool_par = [0] * DC
        pool_tok = [None] * DC
        plr_free = None
        if want_pooled:
            z0 = P.v('memset', pooled[:], 0.0)
            pool_tok = [z0] * DC
        for tb in range(TBN):
            b = tb % 2
            ld = P.load(sb3(xs[b][:], 512), fmc(src, tb * 512, 512),
                        deps=[xs_free[b]])
            sq = None
            for dc in range(DC):
                sq = P.act(xq[b][:, dc * 512:(dc + 1) * 512],
                           xs[b][:, dc * 512:(dc + 1) * 512], AF.Square,
                           deps=[ld, xq_free[b]])
            tsum = tsq = None
            for dc in range(DC):
                tsum = P.mm(p_sum[:], ones_col[:],
                            xs[b][:, dc * 512:(dc + 1) * 512],
                            dc == 0, dc == DC - 1,
                            deps=[ld, war_psum], inc=(dc == DC - 1))
            for dc in range(DC):
                tsq = P.mm(p_sq[:], ones_col[:],
                           xq[b][:, dc * 512:(dc + 1) * 512],
                           dc == 0, dc == DC - 1,
                           deps=[sq, war_psq], inc=(dc == DC - 1))
            u1 = P.v('tensor_scalar', mu_s[:], p_sum[:], 1.0 / D, None, OP.mult,
                     deps=[tsum])
            u2 = P.v('tensor_scalar', ms_s[:], p_sq[:], 1.0 / D, EPS,
                     OP.mult, OP.add, deps=[tsq])
            u3 = P.v('tensor_tensor', m2_s[:], mu_s[:], mu_s[:], OP.mult,
                     deps=[u1])
            u4 = P.v('tensor_tensor', va_s[:], ms_s[:], m2_s[:], OP.subtract,
                     deps=[u2, u3])
            u5 = P.act(sd_s[:], va_s[:], AF.Sqrt, deps=[u4])
            u6 = P.v('reciprocal', rs_s[:], sd_s[:], deps=[u5])
            war_psum = u1
            war_psq = u2
            bmu = P.mm(p_mu[:], ones_row[:], mu_s[:], True, True,
                       deps=[u3, war_pmu], inc=True)
            brs = P.mm(p_rs[:], ones_row[:], rs_s[:], True, True,
                       deps=[u6, war_prs], inc=True)
            last_n = None
            for dc in range(DC):
                sl = slice(dc * 512, (dc + 1) * 512)
                n1 = P.v('tensor_tensor', tp[b][:, sl], xs[b][:, sl], p_mu[:],
                         OP.subtract, deps=[bmu, ld, tp_free[b]])
                n2 = P.v('tensor_tensor', lo[b][:, sl], tp[b][:, sl], p_rs[:],
                         OP.mult, deps=[n1, brs, lo_free[b]])
                last_n = n2
                if want_pooled:
                    r1 = P.v('tensor_reduce', plr[:], lo[b][:, sl], AX.X,
                             OP.add, deps=[n2, plr_free])
                    par = pool_par[dc]
                    r2 = P.v('tensor_tensor',
                             pooled[:, dc * 2 + 1 - par:dc * 2 + 2 - par],
                             pooled[:, dc * 2 + par:dc * 2 + par + 1],
                             plr[:], OP.add, deps=[r1, pool_tok[dc]])
                    pool_par[dc] = 1 - par
                    pool_tok[dc] = r2
                    plr_free = r2
            war_pmu = last_n
            war_prs = last_n
            st = P.store(fmc(dst, tb * 512, 512), sb3(lo[b][:], 512),
                         deps=[last_n])
            hsplit = None
            for dc in range(DC):
                sl = slice(dc * 512, (dc + 1) * 512)
                s1 = P.v('tensor_copy', lhh[b][:, sl], lo[b][:, sl],
                         deps=[last_n, lh_free[b]])
                hsplit = P.v('tensor_tensor', lll[b][:, sl], lo[b][:, sl],
                             lhh[b][:, sl], OP.subtract, deps=[s1, ll_free[b]])
            sth = P.store(fmc(lnh_fm, tb * 512, 512), sb3(lhh[b][:], 512),
                          deps=[hsplit])
            stl = P.store(fmc(lnl_fm, tb * 512, 512), sb3(lll[b][:], 512),
                          deps=[hsplit])
            lh_free[b] = sth
            ll_free[b] = stl
            xs_free[b] = last_n
            xq_free[b] = tsq
            tp_free[b] = last_n
            lo_free[b] = hsplit
        sc.close()
        pool_cols = [dc * 2 + pool_par[dc] for dc in range(DC)]
        return pool_cols, pool_tok

    def phase_routing(pool_cols, pool_tok):
        sc = contextlib.ExitStack()
        wr_sb = sbt(sc, 'rt_wr', [128, 64])
        lg = sbt(sc, 'rt_lg', [1, 8])
        lg2 = sbt(sc, 'rt_lg2', [1, 8])
        mk1 = sbt(sc, 'rt_mk1', [1, 8])
        mk2 = sbt(sc, 'rt_mk2', [1, 8])
        m1 = sbt(sc, 'rt_m1', [1, 1])
        m2 = sbt(sc, 'rt_m2', [1, 1])
        ddf = sbt(sc, 'rt_dd', [1, 1])
        dsc_ = sbt(sc, 'rt_dsc', [1, 1])
        e2v = sbt(sc, 'rt_e2', [1, 1])
        den = sbt(sc, 'rt_den', [1, 1])
        rec = sbt(sc, 'rt_rec', [1, 1])
        w2v = sbt(sc, 'rt_w2v', [1, 1])
        tm8a = sbt(sc, 'rt_tm8a', [1, 8])
        tm8b = sbt(sc, 'rt_tm8b', [1, 8])
        i1 = sbt(sc, 'rt_i1', [1, 1])
        i2 = sbt(sc, 'rt_i2', [1, 1])
        pack = sbt(sc, 'rt_pack', [1, 4])
        stripe = sbt(sc, 'rt_eb', [1, 2])
        ebf = sbt(sc, 'rt_ebf', [2, 1])
        ebu = sbt(sc, 'rt_ebu', [2, 1], U32)
        b1s = sbt(sc, 'rt_b1s', [2, FF])
        b2s = sbt(sc, 'rt_b2s', [2, D])
        b2c = sbt(sc, 'rt_b2c', [128, 16])
        tcol = sbt(sc, 'rt_tcol', [128, 1])
        fst = [sbt(sc, f'rt_fst{i}', [128, 1])
               for i in range(2)]
        base1 = [sbt(sc, f'rt_ba{k}', [128, 1])
                 for k in range(2)]
        base2 = [sbt(sc, f'rt_bb{k}', [128, 1])
                 for k in range(2)]
        p_lg = pst(sc, 'rt_plg', [1, 8])
        p_g = pst(sc, 'rt_pg', [128, 4])
        p_eb = pst(sc, 'rt_peb', [2, 1])
        p_t = [pst(sc, f'rt_pt{i}', [128, 2])
               for i in range(2)]

        wl = P.load(sb3(wr_sb[:], 8), wr.rearrange('(c p) e -> p c e', p=128))
        tl = None
        for dc in range(DC):
            col = pool_cols[dc]
            tl = P.mm(p_lg[:], pooled[:, col:col + 1],
                      wr_sb[:, dc * 8:(dc + 1) * 8], dc == 0, dc == DC - 1,
                      deps=[wl, pool_tok[dc]], inc=(dc == DC - 1))
        a0 = P.v('tensor_copy', lg[:], p_lg[:], deps=[tl])
        a1 = P.v('reduce_max', m1[:], lg[:], AX.X, deps=[a0])
        a2 = P.v('tensor_scalar', mk1[:], lg[:], m1[:], None, OP.is_ge,
                 deps=[a1])
        a3 = P.v('scalar_tensor_tensor', out=lg2[:], in0=mk1[:], scalar=-1e30,
                 in1=lg[:], op0=OP.mult, op1=OP.add, deps=[a2])
        a4 = P.v('reduce_max', m2[:], lg2[:], AX.X, deps=[a3])
        a5 = P.v('tensor_scalar', mk2[:], lg2[:], m2[:], None, OP.is_ge,
                 deps=[a4])
        a6 = P.v('tensor_tensor', ddf[:], m2[:], m1[:], OP.subtract,
                 deps=[a4, a1])
        a7 = P.v('tensor_scalar', dsc_[:], ddf[:], 1.0 / T, None, OP.mult,
                 deps=[a6])
        a8 = P.act(e2v[:], dsc_[:], AF.Exp, deps=[a7])
        a9 = P.v('tensor_scalar', den[:], e2v[:], 1.0, None, OP.add, deps=[a8])
        a10 = P.v('reciprocal', rec[:], den[:], deps=[a9])
        a11 = P.v('tensor_tensor', w2v[:], e2v[:], rec[:], OP.mult,
                  deps=[a8, a10])
        a12 = P.v('tensor_scalar', pack[:, 0:1], rec[:], 0.5, None, OP.mult,
                  deps=[a10])
        a13 = P.v('tensor_scalar', pack[:, 1:2], w2v[:], 0.5, None, OP.mult,
                  deps=[a11])
        a14 = P.v('tensor_tensor', tm8a[:], mk1[:], iota8[:], OP.mult,
                  deps=[a2])
        a15 = P.v('tensor_reduce', i1[:], tm8a[:], AX.X, OP.add, deps=[a14])
        a16 = P.v('tensor_tensor', tm8b[:], mk2[:], iota8[:], OP.mult,
                  deps=[a5])
        a17 = P.v('tensor_reduce', i2[:], tm8b[:], AX.X, OP.add, deps=[a16])
        a18 = P.v('tensor_copy', pack[:, 2:3], i1[:], deps=[a15])
        a19 = P.v('tensor_copy', pack[:, 3:4], i2[:], deps=[a17])
        bg = P.mm(p_g[:], ones_row[:], pack[:], True, True,
                  deps=[a12, a13, a18, a19], inc=True)
        g0 = P.v('tensor_copy', gcol[:], p_g[:], deps=[bg])
        s0 = P.v('tensor_copy', stripe[:, 0:1], i1[:], deps=[a15])
        s1 = P.v('tensor_copy', stripe[:, 1:2], i2[:], deps=[a17])
        te = P.transp(p_eb[:], stripe[:], ident[0:1, 0:1], deps=[s0, s1])
        s2 = P.v('tensor_copy', ebf[:], p_eb[:], deps=[te])
        s3 = P.v('tensor_copy', ebu[:], ebf[:], deps=[s2])
        gb1 = P.gather(b1s[:], b1m, ebu[:], deps=[s3])
        gb2 = P.gather(b2s[:], b2m, ebu[:], deps=[s3])
        ptf = [te, te]
        b1c_tok = None
        for ft in range(32):
            pb = ft % 2
            tr = P.transp(p_t[pb][:], b1s[0:2, ft * 128:(ft + 1) * 128],
                          ident[0:2, 0:2], deps=[gb1, ptf[pb]])
            cp = P.v('tensor_copy', b1cols[:, ft * 2:ft * 2 + 2], p_t[pb][:],
                     deps=[tr])
            ptf[pb] = cp
            b1c_tok = cp
        b2c_tok = None
        for dc in range(DC):
            pb = dc % 2
            tr = P.transp(p_t[pb][:], b2s[0:2, dc * 128:(dc + 1) * 128],
                          ident[0:2, 0:2], deps=[gb2, ptf[pb]])
            cp = P.v('tensor_copy', b2c[:, dc * 2:dc * 2 + 2], p_t[pb][:],
                     deps=[tr])
            ptf[pb] = cp
            b2c_tok = cp
        b2cb_tok = None
        tcol_tok = None
        for dc in range(DC):
            q1 = P.v('tensor_scalar', tcol[:], b2c[:, dc * 2:dc * 2 + 1],
                     gcol[:, 0:1], None, OP.mult, deps=[b2c_tok, g0, tcol_tok])
            q2 = P.v('scalar_tensor_tensor', out=b2comb[:, dc:dc + 1],
                     in0=b2c[:, dc * 2 + 1:dc * 2 + 2], scalar=gcol[:, 1:2],
                     in1=tcol[:], op0=OP.mult, op1=OP.add, deps=[q1])
            tcol_tok = q2
            b2cb_tok = q2
        idx_tok = None
        fs_tok = [g0, g0]
        for k in range(2):
            bb1 = P.v('scalar_tensor_tensor', out=base1[k],
                      in0=gcol[:, 2 + k:3 + k], scalar=4096.0, in1=iota4[:],
                      op0=OP.mult, op1=OP.add, deps=[g0])
            bb2 = P.v('scalar_tensor_tensor', out=base2[k],
                      in0=gcol[:, 2 + k:3 + k], scalar=4096.0, in1=iota128[:],
                      op0=OP.mult, op1=OP.add, deps=[g0])
            for fq in range(4):
                for dc in range(DC):
                    col = k * 32 + fq * 8 + dc
                    fb_ = col % 2
                    f1 = P.v('tensor_scalar', fst[fb_][:], base1[k],
                             float(512 * dc + fq), None, OP.add,
                             deps=[bb1, fs_tok[fb_]])
                    f2 = P.v('tensor_copy', idxw1[:, col:col + 1], fst[fb_][:],
                             deps=[f1])
                    fs_tok[fb_] = f2
                for fb in range(8):
                    col = k * 32 + fq * 8 + fb
                    fb_ = col % 2
                    f1 = P.v('tensor_scalar', fst[fb_][:], base2[k],
                             float(fq * 1024 + fb * 128), None, OP.add,
                             deps=[bb2, fs_tok[fb_]])
                    f2 = P.v('tensor_copy', idxw2[:, col:col + 1], fst[fb_][:],
                             deps=[f1])
                    fs_tok[fb_] = f2
                    idx_tok = f2
        sc.close()
        return dict(gcol=g0, b1c=b1c_tok, b2cb=b2cb_tok, idx=idx_tok)

    def phase_ffn(rt, first_call):
        sc = contextlib.ExitStack()
        w1h = [sbt(sc, f'ff_w1h{i}', [128, DC * 1024], BF16) for i in range(2)]
        w1l = [sbt(sc, f'ff_w1l{i}', [128, DC * 1024], BF16) for i in range(2)]
        w2h = sbt(sc, 'ff_w2h', [128, 8 * 1024], BF16)
        w2l = sbt(sc, 'ff_w2l', [128, 8 * 1024], BF16)
        wtr = sbt(sc, 'ff_wtr', [128, 4 * 1024])
        lnsh = [sbt(sc, f'ff_lnh{i}', [128, DC * 512], BF16) for i in range(2)]
        lnsl = [sbt(sc, f'ff_lnl{i}', [128, DC * 512], BF16) for i in range(2)]
        hbf = sbt(sc, 'ff_h', [128, 8 * 512])
        hh = sbt(sc, 'ff_hh', [128, 8 * 512], BF16)
        hl = sbt(sc, 'ff_hl', [128, 8 * 512], BF16)
        ysb = [sbt(sc, f'ff_ysb{i}', [128, 4 * 512]) for i in range(2)]
        ph = [pst(sc, f'ff_ph{i}', [128, 512]) for i in range(2)]
        py = [pst(sc, f'ff_py{i}', [128, 512]) for i in range(4)]

        w1_tok = [None, None]
        w1_free = [None, None]
        w2_tok = None
        w2_free = None
        wtr_free = None
        lns_free = [None, None, None, None]
        h_tok = [None] * 8
        h_free = None
        hs_tok = [None] * 8
        hs_free = None
        ph_free = [None, None]
        py_free = [None] * 4
        ysb_free = [None, None]
        y_chain = [None] * TBN

        iters = [(k, fq) for k in range(2) for fq in range(4)]

        def gather_split(rows_ap, idx_cols, dsth, dstl, nblk, blkw, dep0):
            nonlocal wtr_free
            sp = None
            half = 4096 // blkw
            for r0 in range(0, nblk, half):
                gts = []
                for j in range(half):
                    blk = r0 + j
                    gts.append(P.gather(
                        wtr[:, j * blkw:(j + 1) * blkw], rows_ap,
                        idx_cols[blk], deps=[rt['idx'], wtr_free, dep0]))
                gt = gts[-1]
                for j in range(half):
                    blk = r0 + j
                    s1 = P.v('tensor_copy',
                             dsth[:, blk * blkw:(blk + 1) * blkw],
                             wtr[:, j * blkw:(j + 1) * blkw], deps=[gt])
                    sp = P.v('tensor_tensor',
                             dstl[:, blk * blkw:(blk + 1) * blkw],
                             wtr[:, j * blkw:(j + 1) * blkw],
                             dsth[:, blk * blkw:(blk + 1) * blkw],
                             OP.subtract, deps=[s1])
                wtr_free = sp
            return sp

        def issue_w1(it_i):
            k, fq = iters[it_i]
            slot = it_i % 2
            idxs = [idxw1[:, k * 32 + fq * 8 + dc:k * 32 + fq * 8 + dc + 1]
                    for dc in range(DC)]
            w1_tok[slot] = gather_split(w1rows, idxs, w1h[slot], w1l[slot],
                                        DC, 1024, w1_free[slot])

        issue_w1(0)
        for it_i, (k, fq) in enumerate(iters):
            slot = it_i % 2
            idxs = [idxw2[:, k * 32 + fq * 8 + fb:k * 32 + fq * 8 + fb + 1]
                    for fb in range(8)]
            w2_tok = gather_split(w2rows, idxs, w2h, w2l, 8, 1024, w2_free)
            if it_i + 1 < len(iters):
                issue_w1(it_i + 1)
            for tb in range(TBN):
                lb = tb % 2
                ldh = P.load(sb3(lnsh[lb][:], 512), fmc(lnh_fm, tb * 512, 512),
                             deps=[lns_free[lb * 2]])
                ldl = P.load(sb3(lnsl[lb][:], 512), fmc(lnl_fm, tb * 512, 512),
                             deps=[lns_free[lb * 2 + 1]])
                for ft in range(8):
                    pb = ft % 2
                    g = None
                    for dc in range(DC):
                        g = mm3(ph[pb][:],
                                w1h[slot][:, dc * 1024 + ft * 128:
                                           dc * 1024 + (ft + 1) * 128],
                                w1l[slot][:, dc * 1024 + ft * 128:
                                           dc * 1024 + (ft + 1) * 128],
                                lnsh[lb][:, dc * 512:(dc + 1) * 512],
                                lnsl[lb][:, dc * 512:(dc + 1) * 512],
                                dc == 0, dc == DC - 1,
                                deps=[w1_tok[slot], ldh, ldl, ph_free[pb]],
                                inc=(dc == DC - 1))
                    sl = slice(ft * 512, (ft + 1) * 512)
                    h_tok[ft] = P.act(hbf[:, sl], ph[pb][:], AF.Gelu,
                                      bias=b1cols[:, (fq * 8 + ft) * 2 + k:
                                                  (fq * 8 + ft) * 2 + k + 1],
                                      deps=[g, rt['b1c'], h_free])
                    s1 = P.v('tensor_copy', hh[:, sl], hbf[:, sl],
                             deps=[h_tok[ft], hs_free])
                    hs_tok[ft] = P.v('tensor_tensor', hl[:, sl], hbf[:, sl],
                                     hh[:, sl], OP.subtract, deps=[s1])
                    ph_free[pb] = h_tok[ft]
                for grp in range(2):
                    gt = None
                    for fb in range(8):
                        for dtl in range(4):
                            dt = grp * 4 + dtl
                            gt = mm3(py[dtl][:],
                                     w2h[:, fb * 1024 + dt * 128:
                                          fb * 1024 + (dt + 1) * 128],
                                     w2l[:, fb * 1024 + dt * 128:
                                          fb * 1024 + (dt + 1) * 128],
                                     hh[:, fb * 512:(fb + 1) * 512],
                                     hl[:, fb * 512:(fb + 1) * 512],
                                     fb == 0, fb == 7,
                                     deps=[hs_tok[fb], w2_tok, py_free[dtl]],
                                     inc=(fb == 7 and dtl == 3))
                    yb = ysb[grp]
                    ev = None
                    for dtl in range(4):
                        ev = P.v('tensor_scalar',
                                 yb[:, dtl * 512:(dtl + 1) * 512], py[dtl][:],
                                 gcol[:, k:k + 1], None, OP.mult,
                                 deps=[gt, rt['gcol'], ysb_free[grp]])
                        py_free[dtl] = ev
                    dst = y_acc[grp * 512:(grp + 1) * 512,
                                tb * 512:(tb + 1) * 512].rearrange(
                                    '(d p) w -> d p w', p=128).transpose([1, 0, 2])
                    srcv = yb[:].rearrange('p (d w) -> p d w', w=512)
                    y_chain[tb] = P.gstore(dst, srcv,
                                           deps=[ev, y_chain[tb]],
                                           accum=(it_i != 0))
                    ysb_free[grp] = y_chain[tb]
                h_free = gt
                hs_free = gt
                lns_free[lb * 2] = gt
                lns_free[lb * 2 + 1] = gt
            w1_free[slot] = P.snap_tensor_token()
            w2_free = w1_free[slot]

        sc.close()
        P.set_base(P.snap())
        sc = contextlib.ExitStack()
        ysb2 = [sbt(sc, f'fc_y{i}', [128, DC * 512])
                for i in range(2)]
        xsb2 = [sbt(sc, f'fc_x{i}', [128, DC * 512])
                for i in range(2)]
        tsb2 = [sbt(sc, f'fc_t{i}', [128, DC * 512])
                for i in range(2)]
        y2f = [None, None]
        x2f = [None, None]
        t2f = [None, None]
        for tb in range(TBN):
            b = tb % 2
            ly = P.load(sb3(ysb2[b][:], 512), fmc(y_acc, tb * 512, 512),
                        deps=[y2f[b], y_chain[tb]])
            lx = P.load(sb3(xsb2[b][:], 512), fmc(x_fm, tb * 512, 512),
                        deps=[x2f[b]])
            last = None
            for dc in range(DC):
                sl = slice(dc * 512, (dc + 1) * 512)
                q1 = P.v('tensor_scalar', tsb2[b][:, sl], ysb2[b][:, sl],
                         b2comb[:, dc:dc + 1], None, OP.add,
                         deps=[ly, rt['b2cb'], t2f[b]])
                q2 = P.v('tensor_tensor', ysb2[b][:, sl], tsb2[b][:, sl],
                         xsb2[b][:, sl], OP.add, deps=[q1, lx])
                last = q2
            st = P.store(fmc(x_fm, tb * 512, 512), sb3(ysb2[b][:], 512),
                         deps=[last])
            y2f[b] = st
            x2f[b] = last
            t2f[b] = last
        sc.close()

    def first_call_iter(it_i):
        return it_i == 0

    def snap_tensor_token():
        c = P.cnt['st']
        return ('st', c) if c > 0 else None
    P.snap_tensor_token = snap_tensor_token

    def phase_xproj():
        sc = contextlib.ExitStack()
        lns = [sbt(sc, f'xp_ln{i}', [128, DC * 512]) for i in range(2)]
        wct = [sbt(sc, f'xp_wct{i}', [128, DC * 128]) for i in range(2)]
        wuv = [sbt(sc, f'xp_wuv{i}', [128, DC * 512]) for i in range(2)]
        qst = [sbt(sc, f'xp_qst{i}', [128, 512]) for i in range(2)]
        ust = [sbt(sc, f'xp_ust{i}', [128, 512]) for i in range(2)]
        vst = [sbt(sc, f'xp_vst{i}', [128, 512]) for i in range(2)]
        b1a_sb = sbt(sc, 'xp_b1a', [1, FF])
        b1f = sbt(sc, 'xp_b1f', [128, 24])
        pp = [pst(sc, f'xp_pp{i}', [128, 512]) for i in range(2)]
        p_t = [pst(sc, f'xp_pt{i}', [128, 1]) for i in range(2)]

        def colof(j):
            return j * 128 if j < 8 else 2048 + (j - 8) * 128

        lb1 = P.load(b1a_sb[:], b1a)
        ptf = [None, None]
        bf_tok = None
        for j in range(24):
            pb = j % 2
            c0 = colof(j)
            tr = P.transp(p_t[pb][:], b1a_sb[0:1, c0:c0 + 128],
                          ident[0:1, 0:1], deps=[lb1, ptf[pb]])
            cp = P.v('tensor_copy', b1f[:, j:j + 1], p_t[pb][:], deps=[tr])
            ptf[pb] = cp
            bf_tok = cp

        lns_free = [None, None]
        wct_tok = [None, None]
        wct_free = [None, None]
        wuv_tok = [None, None]
        wuv_free = [None, None]
        pp_free = [None, None]
        qst_free = [None, None]
        ust_free = [None, None]
        vst_free = [None, None]
        g = None
        for tb in range(TBN):
            lb = tb % 2
            ld = P.load(sb3(lns[lb][:], 512), fmc(ln_fm, tb * 512, 512),
                        deps=[lns_free[lb]])
            for j in range(24):
                wb = j % 2
                c0 = colof(j)
                wct_tok[wb] = P.load(
                    sb3(wct[wb][:], 128),
                    w1a[:, c0:c0 + 128].rearrange('(c p) w -> p c w', p=128),
                    deps=[wct_free[wb]])
                g = None
                for dc in range(DC):
                    g = P.mm(pp[wb][:],
                             wct[wb][:, dc * 128:(dc + 1) * 128],
                             lns[lb][:, dc * 512:(dc + 1) * 512],
                             dc == 0, dc == DC - 1,
                             deps=[wct_tok[wb], ld, pp_free[wb]],
                             inc=(dc == DC - 1))
                sl = P.act(qst[wb][:], pp[wb][:], AF.Silu,
                           bias=b1f[:, j:j + 1], deps=[g, bf_tok, qst_free[wb]])
                st = P.store(proj_fm[j * 128:(j + 1) * 128,
                                     tb * 512:(tb + 1) * 512], qst[wb][:],
                             deps=[sl])
                pp_free[wb] = sl
                qst_free[wb] = st
                wct_free[wb] = g
            for ci, cc in enumerate((2, 3)):
                wb = ci % 2
                wuv_tok[wb] = P.load(
                    sb3(wuv[wb][:], 512),
                    w1a[:, cc * 512:(cc + 1) * 512].rearrange(
                        '(c p) w -> p c w', p=128),
                    deps=[wuv_free[wb]])
                for t4 in range(4):
                    pb = (ci * 4 + t4) % 2
                    g = None
                    for dc in range(DC):
                        g = P.mm(pp[pb][:],
                                 lns[lb][:, dc * 512 + t4 * 128:
                                         dc * 512 + (t4 + 1) * 128],
                                 wuv[wb][:, dc * 512:(dc + 1) * 512],
                                 dc == 0, False,
                                 deps=[wuv_tok[wb], ld, pp_free[pb]])
                    g = P.mm(pp[pb][:], ones_row[:],
                             b1a_sb[0:1, cc * 512:(cc + 1) * 512],
                             False, True, deps=[lb1], inc=True)
                    row0 = tb * 512 + t4 * 128
                    sl = P.act(ust[pb][:], pp[pb][:], AF.Silu,
                               deps=[g, ust_free[pb]])
                    sc_ = P.v('tensor_scalar', vst[pb][:], ust[pb][:],
                              1.0 / T, None, OP.mult,
                              deps=[sl, vst_free[pb]])
                    st = P.store(v_tm[row0:row0 + 128,
                                      (cc - 2) * 512:(cc - 1) * 512],
                                 vst[pb][:], deps=[sc_])
                    ust_free[pb] = sc_
                    vst_free[pb] = st
                    pp_free[pb] = sl
                wuv_free[wb] = g
            lns_free[lb] = g
        sc.close()

    def phase_attn_core():
        sc = contextlib.ExitStack()
        qh = [sbt(sc, f'at_q{i}', [128, T]) for i in range(2)]
        kh = [sbt(sc, f'at_k{i}', [128, T]) for i in range(2)]
        vh = [sbt(sc, f'at_v{i}', [128, NTT * 64]) for i in range(2)]
        uh = [sbt(sc, f'at_u{i}', [64, T]) for i in range(2)]
        att = [sbt(sc, f'at_a{i}', [128, 512]) for i in range(2)]
        avf = [sbt(sc, f'at_avf{i}', [64, T]) for i in range(2)]
        ps = [pst(sc, f'at_ps{i}', [128, 512]) for i in range(2)]
        pav = [pst(sc, f'at_pav{i}', [64, 512]) for i in range(2)]

        zp = []
        for i in range(2):
            zp.append(P.v('memset', qh[i][64:128, :], 0.0))
            zp.append(P.v('memset', kh[i][64:128, :], 0.0))
        zp_tok = zp[-1]

        set_free = [None, None]
        ps_free = [None, None]
        pav_free = [None, None]
        att_free = [None, None]
        avf_free = [None, None]
        for h in range(NH):
            hb = h % 2
            lq = P.load(qh[hb][0:64, :], proj_fm[D + h * 64:D + (h + 1) * 64, :],
                        deps=[set_free[hb], zp_tok])
            lk = P.load(kh[hb][0:64, :],
                        proj_fm[2 * D + h * 64:2 * D + (h + 1) * 64, :],
                        deps=[set_free[hb], zp_tok])
            lv = P.load(vh[hb][:].rearrange('p (t w) -> p t w', w=64),
                        v_tm[:, h * 64:(h + 1) * 64].rearrange(
                            '(t p) w -> p t w', p=128),
                        deps=[set_free[hb]])
            lu = P.load(uh[hb][:], proj_fm[h * 64:(h + 1) * 64, :],
                        deps=[set_free[hb]])
            mu_t = None
            for qc in range(8):
                gav = None
                for kt in range(NTT):
                    ab = kt % 2
                    sm = P.mm(ps[ab][:],
                              kh[hb][:, kt * 128:(kt + 1) * 128],
                              qh[hb][:, qc * 512:(qc + 1) * 512],
                              True, True,
                              deps=[lq, lk, ps_free[ab]], inc=True)
                    sl = P.act(att[ab][:], ps[ab][:], AF.Silu,
                               deps=[sm, att_free[ab]])
                    ps_free[ab] = sl
                    gav = P.mm(pav[qc % 2][:],
                               vh[hb][:, kt * 64:(kt + 1) * 64],
                               att[ab][:],
                               kt == 0, kt == NTT - 1,
                               deps=[sl, lv] +
                                    ([pav_free[qc % 2]] if kt == 0 else []),
                               inc=(kt == NTT - 1))
                    att_free[ab] = gav if kt == NTT - 1 else None
                mu_t = P.v('tensor_tensor',
                           avf[hb][:, qc * 512:(qc + 1) * 512],
                           pav[qc % 2][:],
                           uh[hb][:, qc * 512:(qc + 1) * 512], OP.mult,
                           deps=[gav, lu, avf_free[hb]])
                pav_free[qc % 2] = mu_t
            stv = P.store(avu_fm[h * 64:(h + 1) * 64, :], avf[hb][:],
                          deps=[mu_t])
            avf_free[hb] = stv
            set_free[hb] = mu_t

        sc.close()
        return None

    def phase_outproj(stats_tok, dump_z=False):
        sc = contextlib.ExitStack()
        w2g = sbt(sc, 'op_w2g', [128, DC * 1024])
        wch = [sbt(sc, f'op_wch{i}', [128, 1024]) for i in range(2)]
        gcols = sbt(sc, 'op_gc', [128, 8])
        bcols = sbt(sc, 'op_bc', [128, 8])
        b2cols = sbt(sc, 'op_b2c', [128, 8])
        biasc = sbt(sc, 'op_bias', [128, 8])
        avs = [sbt(sc, f'op_av{i}', [128, DC * 512]) for i in range(2)]
        avq = [sbt(sc, 'op_aq0', [128, DC * 512])] * 2
        tpz = [sbt(sc, 'op_tp0', [128, DC * 512])] * 2
        zfm = [sbt(sc, f'op_zfm{i}', [128, DC * 512]) for i in range(2)]
        xch = [sbt(sc, 'op_xch0', [128, DC * 512])] * 2
        xnb = [sbt(sc, 'op_xnb0', [128, DC * 512])] * 2
        yst = [sbt(sc, 'op_yst0', [128, DC * 512])] * 2
        mu_s = sbt(sc, 'op_mu', [1, 512])
        ms_s = sbt(sc, 'op_ms', [1, 512])
        m2_s = sbt(sc, 'op_m2', [1, 512])
        va_s = sbt(sc, 'op_va', [1, 512])
        sd_s = sbt(sc, 'op_sd', [1, 512])
        rs_s = sbt(sc, 'op_rs', [1, 512])
        p_b = pst(sc, 'op_pb', [128, 8])
        p_y = [pst(sc, f'op_py{i}', [128, 512]) for i in range(2)]
        p_sum = pst(sc, 'op_psum', [1, 512])
        p_sq = pst(sc, 'op_psq', [1, 512])
        p_mu = pst(sc, 'op_pmu', [128, 512])
        p_rs = pst(sc, 'op_prs', [128, 512])

        lg_ = P.load(gcols[:], gam[0].rearrange('(c p) -> p c', p=128),
                     slow=True)
        lb_ = P.load(bcols[:], bet[0].rearrange('(c p) -> p c', p=128),
                     slow=True)
        lb2 = P.load(b2cols[:], b2a[0].rearrange('(c p) -> p c', p=128),
                     slow=True)
        wch_tok = [None, None]
        wch_free = [None, None]
        bias_mm = None
        w2g_tok = None
        for dc in range(DC):
            wb = dc % 2
            wch_tok[wb] = P.load(wch[wb][:], w2a[dc * 128:(dc + 1) * 128, :],
                                 deps=(wch_free[wb] or []))
            for dt in range(DC):
                bias_mm = P.mm(p_b[:, dt:dt + 1],
                               wch[wb][:, dt * 128:(dt + 1) * 128],
                               bcols[:, dc:dc + 1],
                               dc == 0 and dt == 0,
                               dc == DC - 1 and dt == DC - 1,
                               deps=[wch_tok[wb], lb_],
                               inc=(dt == DC - 1))
            w2g_tok = P.v('tensor_scalar', w2g[:, dc * 1024:(dc + 1) * 1024],
                          wch[wb][:], gcols[:, dc:dc + 1], None, OP.mult,
                          deps=[wch_tok[wb], lg_, w2g_tok])
            wch_free[wb] = [w2g_tok, bias_mm]
        bias_tok = P.v('tensor_tensor', biasc[:], p_b[:], b2cols[:], OP.add,
                       deps=[bias_mm, lb2])

        av_free = [None, None]
        aq_free = SharedTok()
        tp_free = SharedTok()
        zfm_free = [None, None]
        py_free = [None, None]
        xch_free = SharedTok()
        xnb_free = SharedTok()
        yst_free = SharedTok()
        war_psum = war_psq = war_pmu = war_prs = None
        for tb in range(TBN):
            zb = tb % 2
            la = P.load(sb3(avs[zb][:], 512), fmc(avu_fm, tb * 512, 512),
                        deps=[av_free[zb]])
            lx = P.load(sb3(xch[zb][:], 512), fmc(x_fm, tb * 512, 512),
                        deps=[xch_free[zb]])
            sq = None
            for dc in range(DC):
                sq = P.act(avq[zb][:, dc * 512:(dc + 1) * 512],
                           avs[zb][:, dc * 512:(dc + 1) * 512], AF.Square,
                           deps=[la, aq_free[zb]])
            tsum = tsq = None
            for dc in range(DC):
                tsum = P.mm(p_sum[:], ones_col[:],
                            avs[zb][:, dc * 512:(dc + 1) * 512],
                            dc == 0, dc == DC - 1,
                            deps=[la, war_psum], inc=(dc == DC - 1))
            for dc in range(DC):
                tsq = P.mm(p_sq[:], ones_col[:],
                           avq[zb][:, dc * 512:(dc + 1) * 512],
                           dc == 0, dc == DC - 1,
                           deps=[sq, war_psq], inc=(dc == DC - 1))
            u1 = P.v('tensor_scalar', mu_s[:], p_sum[:], 1.0 / D, None,
                     OP.mult, deps=[tsum])
            u2 = P.v('tensor_scalar', ms_s[:], p_sq[:], 1.0 / D, EPS,
                     OP.mult, OP.add, deps=[tsq])
            u3 = P.v('tensor_tensor', m2_s[:], mu_s[:], mu_s[:], OP.mult,
                     deps=[u1])
            u4 = P.v('tensor_tensor', va_s[:], ms_s[:], m2_s[:], OP.subtract,
                     deps=[u2, u3])
            u5 = P.act(sd_s[:], va_s[:], AF.Sqrt, deps=[u4])
            u6 = P.v('reciprocal', rs_s[:], sd_s[:], deps=[u5])
            war_psum = u1
            war_psq = u2
            bmu = P.mm(p_mu[:], ones_row[:], mu_s[:], True, True,
                       deps=[u3, war_pmu], inc=True)
            brs = P.mm(p_rs[:], ones_row[:], rs_s[:], True, True,
                       deps=[u6, war_prs], inc=True)
            last_n = None
            for dc in range(DC):
                sl = slice(dc * 512, (dc + 1) * 512)
                n1 = P.v('tensor_tensor', tpz[zb][:, sl], avs[zb][:, sl],
                         p_mu[:], OP.subtract, deps=[bmu, la, tp_free[zb]])
                n2 = P.v('tensor_tensor', zfm[zb][:, sl], tpz[zb][:, sl],
                         p_rs[:], OP.mult, deps=[n1, brs, zfm_free[zb]])
                last_n = n2
            war_pmu = last_n
            war_prs = last_n
            if dump_z:
                P.store(fmc(y_acc, tb * 512, 512), sb3(zfm[zb][:], 512),
                        deps=[last_n])
            last_q = None
            for dt in range(DC):
                pb = dt % 2
                g = None
                for dc in range(DC):
                    g = P.mm(p_y[pb][:],
                             w2g[:, dc * 1024 + dt * 128:
                                 dc * 1024 + (dt + 1) * 128],
                             zfm[zb][:, dc * 512:(dc + 1) * 512],
                             dc == 0, dc == DC - 1,
                             deps=[w2g_tok, last_n, py_free[pb]],
                             inc=(dc == DC - 1))
                sl = slice(dt * 512, (dt + 1) * 512)
                q1 = P.v('tensor_scalar', yst[zb][:, sl], p_y[pb][:],
                         biasc[:, dt:dt + 1], None, OP.add,
                         deps=[g, bias_tok, yst_free[zb]])
                q2 = P.v('tensor_tensor', xnb[zb][:, sl], yst[zb][:, sl],
                         xch[zb][:, sl], OP.add, deps=[q1, lx, xnb_free[zb]])
                py_free[pb] = q1
                last_q = q2
            st = P.store(fmc(x_fm, tb * 512, 512), sb3(xnb[zb][:], 512),
                         deps=[last_q])
            av_free[zb] = last_n
            aq_free[zb] = tsq
            tp_free[zb] = last_n
            zfm_free[zb] = last_q
            xch_free[zb] = last_q
            xnb_free[zb] = st
            yst_free[zb] = last_q
        sc.close()

    def phase_out_transpose(src):
        sc = contextlib.ExitStack()
        chunk = [sbt(sc, f'pf_ch{i}', [128, T])
                 for i in range(DC)]
        ostg = [sbt(sc, f'pf_os{i}', [128, D])
                for i in range(2)]
        ptr = [pst(sc, f'pf_ps{i}', [128, 128])
               for i in range(2)]
        lds = [P.load(chunk[dc][:], src[dc * 128:(dc + 1) * 128, :])
               for dc in range(DC)]
        ptr_free = [None, None]
        ostg_free = [None, None]
        i = 0
        for tt in range(NTT):
            ob = tt % 2
            last_cp = None
            for dc in range(DC):
                pb = i % 2
                tr = P.transp(ptr[pb][:],
                              chunk[dc][:, tt * 128:(tt + 1) * 128],
                              ident[:], deps=[lds[dc], ptr_free[pb]])
                cp = P.v('tensor_copy', ostg[ob][:, dc * 128:(dc + 1) * 128],
                         ptr[pb][:], deps=[tr, ostg_free[ob]])
                ptr_free[pb] = cp
                last_cp = cp
                i += 1
            st = P.store(out[tt * 128:(tt + 1) * 128, :], ostg[ob][:],
                         deps=[last_cp])
            ostg_free[ob] = st
        sc.close()

    def phase_copy_tm(srcd, col0):
        sc = contextlib.ExitStack()
        stg = [sbt(sc, f'cp_st{i}', [128, D]) for i in range(2)]
        free = [None, None]
        for tt in range(NTT):
            b = tt % 2
            ld = P.load(stg[b][:], srcd[tt * 128:(tt + 1) * 128, col0:col0 + D],
                        deps=[free[b]])
            st = P.store(out[tt * 128:(tt + 1) * 128, :], stg[b][:], deps=[ld])
            free[b] = st
        sc.close()

    phase_in_transpose()
    P.set_base(P.snap())
    if mode == 'transpose':
        phase_out_transpose(x_fm)
    else:
        pc, pt = phase_ln(x_fm, ln_fm, True)
        P.set_base(P.snap())
        if mode == 'ln':
            phase_out_transpose(ln_fm)
        else:
            rt = phase_routing(pc, pt)
            P.set_base(P.snap())
            phase_ffn(rt, True)
            P.set_base(P.snap())
            if mode == 'ffn1':
                phase_out_transpose(x_fm)
            else:
                phase_ln(x_fm, ln_fm, False)
                P.set_base(P.snap())
                phase_xproj()
                P.set_base(P.snap())
                if mode in ('qpart', 'kpart', 'upart', 'vpart'):
                    if mode == 'qpart':
                        phase_out_transpose(proj_fm[D:2 * D, :])
                    elif mode == 'kpart':
                        phase_out_transpose(proj_fm[2 * D:3 * D, :])
                    elif mode == 'upart':
                        phase_out_transpose(proj_fm[0:D, :])
                    else:
                        phase_copy_tm(v_tm, 0)
                    P.set_base([])
                    P._op('sync', None, False, P.snap(), lambda h: h.nop())
                    with nc.Block() as block:
                        P.emit(block, sems)
                    es.close()
                    return nc
                stats_tok = phase_attn_core()
                P.set_base(P.snap())
                if mode == 'avu':
                    phase_out_transpose(avu_fm)
                    P.set_base([])
                    P._op('sync', None, False, P.snap(), lambda h: h.nop())
                    with nc.Block() as block:
                        P.emit(block, sems)
                    es.close()
                    return nc
                phase_outproj(stats_tok, dump_z=(mode == 'zdump'))
                P.set_base(P.snap())
                if mode == 'zdump':
                    phase_out_transpose(y_acc)
                elif mode == 'attn':
                    phase_out_transpose(x_fm)
                else:
                    pc, pt = phase_ln(x_fm, ln_fm, True)
                    P.set_base(P.snap())
                    rt = phase_routing(pc, pt)
                    P.set_base(P.snap())
                    phase_ffn(rt, True)
                    P.set_base(P.snap())
                    phase_out_transpose(x_fm)

    P.set_base([])
    P._op('sync', None, False, P.snap(), lambda h: h.nop())

    with nc.Block() as block:
        P.emit(block, sems)
    es.close()
    return nc


_NC_CACHE = {}


def _get_program(mode):
    if mode not in _NC_CACHE:
        _NC_CACHE[mode] = build_program(mode)
    return _NC_CACHE[mode]


def kernel(L0_attn_x, L0_attn_W1, L0_attn_b1, L0_attn_W2, L0_attn_b2,
           L0_attn_gamma, L0_attn_beta, L0_ffn_Wr, L0_ffn_W1, L0_ffn_b1,
           L0_ffn_W2, L0_ffn_b2, mode='full'):
    f = lambda a: np.ascontiguousarray(np.asarray(a, dtype=np.float32))
    x = f(L0_attn_x)
    shared = dict(
        w1a=f(L0_attn_W1), b1a=f(L0_attn_b1).reshape(1, FF),
        w2a=f(L0_attn_W2), b2a=f(L0_attn_b2).reshape(1, D),
        gam=f(L0_attn_gamma).reshape(1, D), bet=f(L0_attn_beta).reshape(1, D),
        wr=f(L0_ffn_Wr), w1m=f(L0_ffn_W1), b1m=f(L0_ffn_b1),
        w2m=f(L0_ffn_W2), b2m=f(L0_ffn_b2))
    in_maps = [dict(x=np.ascontiguousarray(x[b]), **shared)
               for b in range(N_CORES)]
    nc = _get_program(mode)
    res = run_bass_kernel_spmd(nc, in_maps, core_ids=list(range(N_CORES)))
    return np.stack([res.results[b]['out'] for b in range(N_CORES)], axis=0)


# version 04
# speedup vs baseline: 1.5146x; 1.1125x over previous
import contextlib
import numpy as np

import concourse.bass as bass
import concourse.mybir as mybir
from concourse.bass_utils import run_bass_kernel_spmd

F32 = mybir.dt.float32
BF16 = mybir.dt.bfloat16
U32 = mybir.dt.uint32
AF = mybir.ActivationFunctionType
OP = mybir.AluOpType
AX = mybir.AxisListType

D, T, FF, NH, DH, E = 1024, 4096, 4096, 16, 64, 8
DC = D // 128
TBN = T // 512
NTT = T // 128
EPS = 1e-5
N_CORES = 8

ENGINES = ('sync', 'gpsimd', 'tensor', 'scalar', 'vector')


class SharedTok:

    def __init__(self):
        self.t = None

    def __getitem__(self, i):
        return self.t

    def __setitem__(self, i, v):
        self.t = v


class Prog:

    NPOOL = 6

    def __init__(self):
        self.streams = {e: [] for e in ENGINES}
        self.cnt = dict(st=0, ss=0, sv=0, sg=0)
        for q in ('dl', 'ds', 'dg'):
            for i in range(self.NPOOL):
                self.cnt[f'{q}{i}'] = 0
        self.rr = dict(dl=0, ds=0, dg=0)
        self.base = {}

    def _dma(self, eng, q, deps, fn):
        i = self.rr[q]
        self.rr[q] = (i + 1) % self.NPOOL
        sem = f'{q}{i}'
        prev = self.cnt[sem]
        dd = list(deps)
        if prev:
            dd.append((sem, prev))
        return self._op(eng, sem, True, dd, fn)

    def set_base(self, toks):
        self.base = {}
        for t in toks:
            if t is None:
                continue
            s, c = t
            if c > self.base.get(s, 0):
                self.base[s] = c

    def _op(self, eng, sem, dma, deps, fn):
        tok = None
        if sem is not None:
            self.cnt[sem] += 16 if dma else 1
            tok = (sem, self.cnt[sem])
        dd = dict(self.base)
        for d in deps:
            if d is None:
                continue
            s, c = d
            if c > dd.get(s, 0):
                dd[s] = c
        self.streams[eng].append((fn, dd, sem, 16 if dma else 1))
        return tok

    def load(self, dst, src, deps=(), slow=False):
        if slow:
            return self._dma('sync', 'dl', deps,
                             lambda h, a=dst, b=src: h.dma_start(
                                 out=a, in_=b, allow_slow_non_contiguous=True))
        return self._dma('sync', 'dl', deps,
                         lambda h, a=dst, b=src: h.dma_start(out=a, in_=b))

    def store(self, dst, src, deps=()):
        return self._dma('scalar', 'ds', deps,
                         lambda h, a=dst, b=src: h.dma_start(out=a, in_=b))

    def gstore(self, dst, src, deps=(), accum=False):
        if accum:
            return self._dma('gpsimd', 'dg', deps,
                             lambda h, a=dst, b=src:
                                 h.dma_start(out=a, in_=b, accum_op=OP.add))
        return self._dma('gpsimd', 'dg', deps,
                         lambda h, a=dst, b=src: h.dma_start(out=a, in_=b))

    def gather(self, dst, src_rows, idx, deps=()):
        return self._dma('gpsimd', 'dg', deps,
                         lambda h, a=dst, b=src_rows, i=idx: h.indirect_dma_start(
                             out=a, out_offset=None, in_=b,
                             in_offset=bass.IndirectOffsetOnAxis(ap=i, axis=0)))

    def mm(self, out, stat, mov, start, stop, deps=(), inc=False):
        return self._op('tensor', 'st' if inc else None, False, deps,
                        lambda h, o=out, s=stat, m=mov, a=start, b=stop:
                            h.matmul(o, s, m, start=a, stop=b))

    def transp(self, out, in_, ident, deps=(), inc=True):
        return self._op('tensor', 'st' if inc else None, False, deps,
                        lambda h, o=out, i=in_, d=ident: h.transpose(o, i, d))

    def act(self, out, in_, func, deps=(), bias=None, scale=1.0):
        if bias is None:
            return self._op('scalar', 'ss', False, deps,
                            lambda h, o=out, i=in_, f=func, sc=scale:
                                h.activation(o, i, f, scale=sc))
        return self._op('scalar', 'ss', False, deps,
                        lambda h, o=out, i=in_, f=func, b=bias, sc=scale:
                            h.activation(o, i, f, bias=b, scale=sc))

    def v(self, name, *args, deps=(), **kw):
        return self._op('vector', 'sv', False, deps,
                        lambda h, n=name, a=args, k=kw: getattr(h, n)(*a, **k))

    def g(self, name, *args, deps=(), **kw):
        return self._op('gpsimd', 'sg', False, deps,
                        lambda h, n=name, a=args, k=kw: getattr(h, n)(*a, **k))

    def snap(self):
        return [(s, c) for s, c in self.cnt.items() if c > 0]

    def emit(self, block, sems):
        for eng in ENGINES:
            items = self.streams[eng]
            if not items:
                continue

            def mk(items=items):
                def body(h):
                    hw = {}
                    for fn, dd, sem, amt in items:
                        for s, c in sorted(dd.items()):
                            if c > hw.get(s, 0):
                                h.wait_ge(sems[s], c)
                                hw[s] = c
                        inst = fn(h)
                        if sem is not None:
                            inst.then_inc(sems[sem], amt)
                return body
            getattr(block, eng)(mk())


def build_program(mode='full'):
    nc = bass.Bass('TRN2', target_bir_lowering=False)
    x_in = nc.dram_tensor('x', [T, D], F32, kind='ExternalInput').ap()
    w1a = nc.dram_tensor('w1a', [D, FF], F32, kind='ExternalInput').ap()
    b1a = nc.dram_tensor('b1a', [1, FF], F32, kind='ExternalInput').ap()
    w2a = nc.dram_tensor('w2a', [D, D], F32, kind='ExternalInput').ap()
    b2a = nc.dram_tensor('b2a', [1, D], F32, kind='ExternalInput').ap()
    gam = nc.dram_tensor('gam', [1, D], F32, kind='ExternalInput').ap()
    bet = nc.dram_tensor('bet', [1, D], F32, kind='ExternalInput').ap()
    wr = nc.dram_tensor('wr', [D, E], F32, kind='ExternalInput').ap()
    w1m = nc.dram_tensor('w1m', [E, D, FF], F32, kind='ExternalInput').ap()
    b1m = nc.dram_tensor('b1m', [E, FF], F32, kind='ExternalInput').ap()
    w2m = nc.dram_tensor('w2m', [E, FF, D], F32, kind='ExternalInput').ap()
    b2m = nc.dram_tensor('b2m', [E, D], F32, kind='ExternalInput').ap()
    out = nc.dram_tensor('out', [T, D], F32, kind='ExternalOutput').ap()

    x_fm = nc.dram_tensor('x_fm', [D, T], F32).ap()
    ln_fm = nc.dram_tensor('ln_fm', [D, T], F32).ap()
    lnh_fm = nc.dram_tensor('lnh_fm', [D, T], BF16).ap()
    lnl_fm = nc.dram_tensor('lnl_fm', [D, T], BF16).ap()
    y_acc = nc.dram_tensor('y_acc', [D, T], F32).ap()
    proj_fm = nc.dram_tensor('proj_fm', [3 * D, T], F32).ap()
    projh = nc.dram_tensor('projh', [2 * D, T], BF16).ap()
    projl = nc.dram_tensor('projl', [2 * D, T], BF16).ap()
    w1ah = nc.dram_tensor('w1ah', [D, FF], BF16).ap()
    w1al = nc.dram_tensor('w1al', [D, FF], BF16).ap()
    v_tm = nc.dram_tensor('v_tm', [T, D], F32).ap()
    avu_fm = nc.dram_tensor('avu_fm', [D, T], F32).ap()

    w1rows = w1m.rearrange('e d (q c) -> (e d q) c', c=1024)
    w2rows = w2m.rearrange('e f d -> (e f) d')

    def fmc(dram, t0, w):
        return dram[:, t0:t0 + w].rearrange('(c p) w -> p c w', p=128)

    def sb3(ap, w):
        return ap.rearrange('p (c w) -> p c w', w=w)

    P = Prog()
    uid = [0]

    def sbt(sc_, name, shape, dtype=F32):
        uid[0] += 1
        return sc_.enter_context(
            nc.sbuf_tensor(f'{name}_{uid[0]}', shape, dtype)).ap()

    def pst(sc_, name, shape):
        uid[0] += 1
        return sc_.enter_context(
            nc.psum_tensor(f'{name}_{uid[0]}', shape, F32)).ap()

    def mm3(out, sh, sl_, mh, ml, first, last, deps=(), inc=False):
        P.mm(out, sh, mh, first, False, deps=deps)
        P.mm(out, sh, ml, False, False, deps=deps)
        return P.mm(out, sl_, mh, False, last, deps=deps, inc=inc)

    es = contextlib.ExitStack()
    sems = {s: es.enter_context(nc.semaphore(s)) for s in P.cnt}

    ident = nc.alloc_sbuf_tensor('ident', [128, 128], F32).ap()
    iota128 = nc.alloc_sbuf_tensor('iota128', [128, 1], F32).ap()
    iota4 = nc.alloc_sbuf_tensor('iota4', [128, 1], F32).ap()
    iota8 = nc.alloc_sbuf_tensor('iota8', [1, 8], F32).ap()
    ones_row = nc.alloc_sbuf_tensor('ones_row', [1, 128], F32).ap()
    ones_col = nc.alloc_sbuf_tensor('ones_col', [128, 1], F32).ap()
    pooled = nc.alloc_sbuf_tensor('pooled', [128, 16], F32).ap()
    gcol = nc.alloc_sbuf_tensor('gcol', [128, 4], F32).ap()
    b1cols = nc.alloc_sbuf_tensor('b1cols', [128, 64], F32).ap()
    b2comb = nc.alloc_sbuf_tensor('b2comb', [128, 8], F32).ap()
    idxw1 = nc.alloc_sbuf_tensor('idxw1', [128, 64], U32).ap()
    idxw2 = nc.alloc_sbuf_tensor('idxw2', [128, 64], U32).ap()

    c0 = P.g('memset', ident[:], 0.0)
    c1 = P.g('affine_select', out=ident[:], in_=ident[:],
             compare_op=OP.not_equal, fill=1.0, base=0,
             pattern=[[-1, 128]], channel_multiplier=1, deps=[c0])
    c2 = P.g('iota', iota128[:], [[0, 1]], channel_multiplier=1,
             allow_small_or_imprecise_dtypes=True, deps=[c1])
    c3 = P.g('iota', iota8[:], [[1, 8]], channel_multiplier=0,
             allow_small_or_imprecise_dtypes=True, deps=[c2])
    c4 = P.g('memset', ones_row[:], 1.0, deps=[c3])
    c5 = P.g('memset', ones_col[:], 1.0, deps=[c4])
    c6 = P.v('tensor_scalar', iota4[:], iota128[:], 4.0, None, OP.mult,
             deps=[c5])
    P.set_base([c6])

    def phase_in_transpose():
        sc = contextlib.ExitStack()
        stage = [sbt(sc, f'p0stg{i}', [128, T])
                 for i in range(DC)]
        xin = [sbt(sc, f'p0xin{i}', [128, D])
               for i in range(2)]
        ptr = [pst(sc, f'p0ps{i}', [128, 128])
               for i in range(2)]
        xin_free = [None, None]
        ptr_free = [None, None]
        last_cp = None
        i = 0
        for tt in range(NTT):
            b = tt % 2
            ld = P.load(xin[b][:], x_in[tt * 128:(tt + 1) * 128, :],
                        deps=[xin_free[b]])
            for dc in range(DC):
                pb = i % 2
                tr = P.transp(ptr[pb][:], xin[b][:, dc * 128:(dc + 1) * 128],
                              ident[:], deps=[ld, ptr_free[pb]])
                cp = P.v('tensor_copy', stage[dc][:, tt * 128:(tt + 1) * 128],
                         ptr[pb][:], deps=[tr])
                ptr_free[pb] = cp
                last_cp = cp
                i += 1
            xin_free[b] = last_cp
        for dc in range(DC):
            P.store(x_fm[dc * 128:(dc + 1) * 128, :], stage[dc][:],
                    deps=[last_cp])
        sc.close()

    def phase_ln(src, dst, want_pooled):
        sc = contextlib.ExitStack()
        xs = [sbt(sc, f'lnxs{i}', [128, DC * 512])
              for i in range(2)]
        xq = [sbt(sc, f'lnxq{i}', [128, DC * 512])
              for i in range(2)]
        tp = [sbt(sc, f'lntp{i}', [128, DC * 512])
              for i in range(2)]
        lo = [sbt(sc, f'lnlo{i}', [128, DC * 512])
              for i in range(2)]
        lhh = [sbt(sc, f'lnhh{i}', [128, DC * 512], BF16) for i in range(2)]
        lll = [sbt(sc, f'lnll{i}', [128, DC * 512], BF16) for i in range(2)]
        mu_s = sbt(sc, 'ln_mu', [1, 512])
        ms_s = sbt(sc, 'ln_ms', [1, 512])
        m2_s = sbt(sc, 'ln_m2', [1, 512])
        va_s = sbt(sc, 'ln_va', [1, 512])
        sd_s = sbt(sc, 'ln_sd', [1, 512])
        rs_s = sbt(sc, 'ln_rs', [1, 512])
        plr = sbt(sc, 'ln_plr', [128, 1])
        p_sum = pst(sc, 'ln_psum', [1, 512])
        p_sq = pst(sc, 'ln_psq', [1, 512])
        p_mu = pst(sc, 'ln_pmu', [128, 512])
        p_rs = pst(sc, 'ln_prs', [128, 512])

        xs_free = [None, None]
        xq_free = [None, None]
        tp_free = [None, None]
        lo_free = [None, None]
        lh_free = [None, None]
        ll_free = [None, None]
        war_psum = war_psq = war_pmu = war_prs = None
        pool_par = [0] * DC
        pool_tok = [None] * DC
        plr_free = None
        if want_pooled:
            z0 = P.v('memset', pooled[:], 0.0)
            pool_tok = [z0] * DC
        for tb in range(TBN):
            b = tb % 2
            ld = P.load(sb3(xs[b][:], 512), fmc(src, tb * 512, 512),
                        deps=[xs_free[b]])
            sq = None
            for dc in range(DC):
                sq = P.act(xq[b][:, dc * 512:(dc + 1) * 512],
                           xs[b][:, dc * 512:(dc + 1) * 512], AF.Square,
                           deps=[ld, xq_free[b]])
            tsum = tsq = None
            for dc in range(DC):
                tsum = P.mm(p_sum[:], ones_col[:],
                            xs[b][:, dc * 512:(dc + 1) * 512],
                            dc == 0, dc == DC - 1,
                            deps=[ld, war_psum], inc=(dc == DC - 1))
            for dc in range(DC):
                tsq = P.mm(p_sq[:], ones_col[:],
                           xq[b][:, dc * 512:(dc + 1) * 512],
                           dc == 0, dc == DC - 1,
                           deps=[sq, war_psq], inc=(dc == DC - 1))
            u1 = P.v('tensor_scalar', mu_s[:], p_sum[:], 1.0 / D, None, OP.mult,
                     deps=[tsum])
            u2 = P.v('tensor_scalar', ms_s[:], p_sq[:], 1.0 / D, EPS,
                     OP.mult, OP.add, deps=[tsq])
            u3 = P.v('tensor_tensor', m2_s[:], mu_s[:], mu_s[:], OP.mult,
                     deps=[u1])
            u4 = P.v('tensor_tensor', va_s[:], ms_s[:], m2_s[:], OP.subtract,
                     deps=[u2, u3])
            u5 = P.act(sd_s[:], va_s[:], AF.Sqrt, deps=[u4])
            u6 = P.v('reciprocal', rs_s[:], sd_s[:], deps=[u5])
            war_psum = u1
            war_psq = u2
            bmu = P.mm(p_mu[:], ones_row[:], mu_s[:], True, True,
                       deps=[u3, war_pmu], inc=True)
            brs = P.mm(p_rs[:], ones_row[:], rs_s[:], True, True,
                       deps=[u6, war_prs], inc=True)
            last_n = None
            for dc in range(DC):
                sl = slice(dc * 512, (dc + 1) * 512)
                n1 = P.v('tensor_tensor', tp[b][:, sl], xs[b][:, sl], p_mu[:],
                         OP.subtract, deps=[bmu, ld, tp_free[b]])
                n2 = P.v('tensor_tensor', lo[b][:, sl], tp[b][:, sl], p_rs[:],
                         OP.mult, deps=[n1, brs, lo_free[b]])
                last_n = n2
                if want_pooled:
                    r1 = P.v('tensor_reduce', plr[:], lo[b][:, sl], AX.X,
                             OP.add, deps=[n2, plr_free])
                    par = pool_par[dc]
                    r2 = P.v('tensor_tensor',
                             pooled[:, dc * 2 + 1 - par:dc * 2 + 2 - par],
                             pooled[:, dc * 2 + par:dc * 2 + par + 1],
                             plr[:], OP.add, deps=[r1, pool_tok[dc]])
                    pool_par[dc] = 1 - par
                    pool_tok[dc] = r2
                    plr_free = r2
            war_pmu = last_n
            war_prs = last_n
            st = P.store(fmc(dst, tb * 512, 512), sb3(lo[b][:], 512),
                         deps=[last_n])
            hsplit = None
            for dc in range(DC):
                sl = slice(dc * 512, (dc + 1) * 512)
                s1 = P.v('tensor_copy', lhh[b][:, sl], lo[b][:, sl],
                         deps=[last_n, lh_free[b]])
                hsplit = P.v('tensor_tensor', lll[b][:, sl], lo[b][:, sl],
                             lhh[b][:, sl], OP.subtract, deps=[s1, ll_free[b]])
            sth = P.store(fmc(lnh_fm, tb * 512, 512), sb3(lhh[b][:], 512),
                          deps=[hsplit])
            stl = P.store(fmc(lnl_fm, tb * 512, 512), sb3(lll[b][:], 512),
                          deps=[hsplit])
            lh_free[b] = sth
            ll_free[b] = stl
            xs_free[b] = last_n
            xq_free[b] = tsq
            tp_free[b] = last_n
            lo_free[b] = hsplit
        sc.close()
        pool_cols = [dc * 2 + pool_par[dc] for dc in range(DC)]
        return pool_cols, pool_tok

    def phase_routing(pool_cols, pool_tok, split_w1a=False):
        sc = contextlib.ExitStack()
        if split_w1a:
            phase_w1a_split(sc)
        wr_sb = sbt(sc, 'rt_wr', [128, 64])
        lg = sbt(sc, 'rt_lg', [1, 8])
        lg2 = sbt(sc, 'rt_lg2', [1, 8])
        mk1 = sbt(sc, 'rt_mk1', [1, 8])
        mk2 = sbt(sc, 'rt_mk2', [1, 8])
        m1 = sbt(sc, 'rt_m1', [1, 1])
        m2 = sbt(sc, 'rt_m2', [1, 1])
        ddf = sbt(sc, 'rt_dd', [1, 1])
        dsc_ = sbt(sc, 'rt_dsc', [1, 1])
        e2v = sbt(sc, 'rt_e2', [1, 1])
        den = sbt(sc, 'rt_den', [1, 1])
        rec = sbt(sc, 'rt_rec', [1, 1])
        w2v = sbt(sc, 'rt_w2v', [1, 1])
        tm8a = sbt(sc, 'rt_tm8a', [1, 8])
        tm8b = sbt(sc, 'rt_tm8b', [1, 8])
        i1 = sbt(sc, 'rt_i1', [1, 1])
        i2 = sbt(sc, 'rt_i2', [1, 1])
        pack = sbt(sc, 'rt_pack', [1, 4])
        stripe = sbt(sc, 'rt_eb', [1, 2])
        ebf = sbt(sc, 'rt_ebf', [2, 1])
        ebu = sbt(sc, 'rt_ebu', [2, 1], U32)
        b1s = sbt(sc, 'rt_b1s', [2, FF])
        b2s = sbt(sc, 'rt_b2s', [2, D])
        b2c = sbt(sc, 'rt_b2c', [128, 16])
        tcol = sbt(sc, 'rt_tcol', [128, 1])
        fst = [sbt(sc, f'rt_fst{i}', [128, 1])
               for i in range(2)]
        base1 = [sbt(sc, f'rt_ba{k}', [128, 1])
                 for k in range(2)]
        base2 = [sbt(sc, f'rt_bb{k}', [128, 1])
                 for k in range(2)]
        p_lg = pst(sc, 'rt_plg', [1, 8])
        p_g = pst(sc, 'rt_pg', [128, 4])
        p_eb = pst(sc, 'rt_peb', [2, 1])
        p_t = [pst(sc, f'rt_pt{i}', [128, 2])
               for i in range(2)]

        wl = P.load(sb3(wr_sb[:], 8), wr.rearrange('(c p) e -> p c e', p=128))
        tl = None
        for dc in range(DC):
            col = pool_cols[dc]
            tl = P.mm(p_lg[:], pooled[:, col:col + 1],
                      wr_sb[:, dc * 8:(dc + 1) * 8], dc == 0, dc == DC - 1,
                      deps=[wl, pool_tok[dc]], inc=(dc == DC - 1))
        a0 = P.v('tensor_copy', lg[:], p_lg[:], deps=[tl])
        a1 = P.v('reduce_max', m1[:], lg[:], AX.X, deps=[a0])
        a2 = P.v('tensor_scalar', mk1[:], lg[:], m1[:], None, OP.is_ge,
                 deps=[a1])
        a3 = P.v('scalar_tensor_tensor', out=lg2[:], in0=mk1[:], scalar=-1e30,
                 in1=lg[:], op0=OP.mult, op1=OP.add, deps=[a2])
        a4 = P.v('reduce_max', m2[:], lg2[:], AX.X, deps=[a3])
        a5 = P.v('tensor_scalar', mk2[:], lg2[:], m2[:], None, OP.is_ge,
                 deps=[a4])
        a6 = P.v('tensor_tensor', ddf[:], m2[:], m1[:], OP.subtract,
                 deps=[a4, a1])
        a7 = P.v('tensor_scalar', dsc_[:], ddf[:], 1.0 / T, None, OP.mult,
                 deps=[a6])
        a8 = P.act(e2v[:], dsc_[:], AF.Exp, deps=[a7])
        a9 = P.v('tensor_scalar', den[:], e2v[:], 1.0, None, OP.add, deps=[a8])
        a10 = P.v('reciprocal', rec[:], den[:], deps=[a9])
        a11 = P.v('tensor_tensor', w2v[:], e2v[:], rec[:], OP.mult,
                  deps=[a8, a10])
        a12 = P.v('tensor_scalar', pack[:, 0:1], rec[:], 0.5, None, OP.mult,
                  deps=[a10])
        a13 = P.v('tensor_scalar', pack[:, 1:2], w2v[:], 0.5, None, OP.mult,
                  deps=[a11])
        a14 = P.v('tensor_tensor', tm8a[:], mk1[:], iota8[:], OP.mult,
                  deps=[a2])
        a15 = P.v('tensor_reduce', i1[:], tm8a[:], AX.X, OP.add, deps=[a14])
        a16 = P.v('tensor_tensor', tm8b[:], mk2[:], iota8[:], OP.mult,
                  deps=[a5])
        a17 = P.v('tensor_reduce', i2[:], tm8b[:], AX.X, OP.add, deps=[a16])
        a18 = P.v('tensor_copy', pack[:, 2:3], i1[:], deps=[a15])
        a19 = P.v('tensor_copy', pack[:, 3:4], i2[:], deps=[a17])
        bg = P.mm(p_g[:], ones_row[:], pack[:], True, True,
                  deps=[a12, a13, a18, a19], inc=True)
        g0 = P.v('tensor_copy', gcol[:], p_g[:], deps=[bg])
        s0 = P.v('tensor_copy', stripe[:, 0:1], i1[:], deps=[a15])
        s1 = P.v('tensor_copy', stripe[:, 1:2], i2[:], deps=[a17])
        te = P.transp(p_eb[:], stripe[:], ident[0:1, 0:1], deps=[s0, s1])
        s2 = P.v('tensor_copy', ebf[:], p_eb[:], deps=[te])
        s3 = P.v('tensor_copy', ebu[:], ebf[:], deps=[s2])
        gb1 = P.gather(b1s[:], b1m, ebu[:], deps=[s3])
        gb2 = P.gather(b2s[:], b2m, ebu[:], deps=[s3])
        ptf = [te, te]
        b1c_tok = None
        for ft in range(32):
            pb = ft % 2
            tr = P.transp(p_t[pb][:], b1s[0:2, ft * 128:(ft + 1) * 128],
                          ident[0:2, 0:2], deps=[gb1, ptf[pb]])
            cp = P.v('tensor_copy', b1cols[:, ft * 2:ft * 2 + 2], p_t[pb][:],
                     deps=[tr])
            ptf[pb] = cp
            b1c_tok = cp
        b2c_tok = None
        for dc in range(DC):
            pb = dc % 2
            tr = P.transp(p_t[pb][:], b2s[0:2, dc * 128:(dc + 1) * 128],
                          ident[0:2, 0:2], deps=[gb2, ptf[pb]])
            cp = P.v('tensor_copy', b2c[:, dc * 2:dc * 2 + 2], p_t[pb][:],
                     deps=[tr])
            ptf[pb] = cp
            b2c_tok = cp
        b2cb_tok = None
        tcol_tok = None
        for dc in range(DC):
            q1 = P.v('tensor_scalar', tcol[:], b2c[:, dc * 2:dc * 2 + 1],
                     gcol[:, 0:1], None, OP.mult, deps=[b2c_tok, g0, tcol_tok])
            q2 = P.v('scalar_tensor_tensor', out=b2comb[:, dc:dc + 1],
                     in0=b2c[:, dc * 2 + 1:dc * 2 + 2], scalar=gcol[:, 1:2],
                     in1=tcol[:], op0=OP.mult, op1=OP.add, deps=[q1])
            tcol_tok = q2
            b2cb_tok = q2
        idx_tok = None
        fs_tok = [g0, g0]
        for k in range(2):
            bb1 = P.v('scalar_tensor_tensor', out=base1[k],
                      in0=gcol[:, 2 + k:3 + k], scalar=4096.0, in1=iota4[:],
                      op0=OP.mult, op1=OP.add, deps=[g0])
            bb2 = P.v('scalar_tensor_tensor', out=base2[k],
                      in0=gcol[:, 2 + k:3 + k], scalar=4096.0, in1=iota128[:],
                      op0=OP.mult, op1=OP.add, deps=[g0])
            for fq in range(4):
                for dc in range(DC):
                    col = k * 32 + fq * 8 + dc
                    fb_ = col % 2
                    f1 = P.v('tensor_scalar', fst[fb_][:], base1[k],
                             float(512 * dc + fq), None, OP.add,
                             deps=[bb1, fs_tok[fb_]])
                    f2 = P.v('tensor_copy', idxw1[:, col:col + 1], fst[fb_][:],
                             deps=[f1])
                    fs_tok[fb_] = f2
                for fb in range(8):
                    col = k * 32 + fq * 8 + fb
                    fb_ = col % 2
                    f1 = P.v('tensor_scalar', fst[fb_][:], base2[k],
                             float(fq * 1024 + fb * 128), None, OP.add,
                             deps=[bb2, fs_tok[fb_]])
                    f2 = P.v('tensor_copy', idxw2[:, col:col + 1], fst[fb_][:],
                             deps=[f1])
                    fs_tok[fb_] = f2
                    idx_tok = f2
        sc.close()
        return dict(gcol=g0, b1c=b1c_tok, b2cb=b2cb_tok, idx=idx_tok)

    def phase_ffn(rt, first_call):
        sc = contextlib.ExitStack()
        w1h = [sbt(sc, f'ff_w1h{i}', [128, DC * 1024], BF16) for i in range(2)]
        w1l = [sbt(sc, f'ff_w1l{i}', [128, DC * 1024], BF16) for i in range(2)]
        w2h = sbt(sc, 'ff_w2h', [128, 8 * 1024], BF16)
        w2l = sbt(sc, 'ff_w2l', [128, 8 * 1024], BF16)
        wtr = sbt(sc, 'ff_wtr', [128, 4 * 1024])
        lnsh = [sbt(sc, f'ff_lnh{i}', [128, DC * 512], BF16) for i in range(2)]
        lnsl = [sbt(sc, f'ff_lnl{i}', [128, DC * 512], BF16) for i in range(2)]
        hbf = sbt(sc, 'ff_h', [128, 8 * 512])
        hh = sbt(sc, 'ff_hh', [128, 8 * 512], BF16)
        hl = sbt(sc, 'ff_hl', [128, 8 * 512], BF16)
        ysb = [sbt(sc, f'ff_ysb{i}', [128, 4 * 512]) for i in range(2)]
        ph = [pst(sc, f'ff_ph{i}', [128, 512]) for i in range(2)]
        py = [pst(sc, f'ff_py{i}', [128, 512]) for i in range(4)]

        w1_tok = [None, None]
        w1_free = [None, None]
        w2_tok = None
        w2_free = None
        wtr_free = None
        lns_free = [None, None, None, None]
        h_tok = [None] * 8
        h_free = None
        hs_tok = [None] * 8
        hs_free = None
        ph_free = [None, None]
        py_free = [None] * 4
        ysb_free = [None, None]
        y_chain = [None] * TBN

        iters = [(k, fq) for k in range(2) for fq in range(4)]

        def gather_split(rows_ap, idx_cols, dsth, dstl, nblk, blkw, dep0):
            nonlocal wtr_free
            sp = None
            half = 4096 // blkw
            for r0 in range(0, nblk, half):
                gts = []
                for j in range(half):
                    blk = r0 + j
                    gts.append(P.gather(
                        wtr[:, j * blkw:(j + 1) * blkw], rows_ap,
                        idx_cols[blk], deps=[rt['idx'], wtr_free, dep0]))
                gt = gts[-1]
                for j in range(half):
                    blk = r0 + j
                    s1 = P.v('tensor_copy',
                             dsth[:, blk * blkw:(blk + 1) * blkw],
                             wtr[:, j * blkw:(j + 1) * blkw], deps=[gt])
                    sp = P.v('tensor_tensor',
                             dstl[:, blk * blkw:(blk + 1) * blkw],
                             wtr[:, j * blkw:(j + 1) * blkw],
                             dsth[:, blk * blkw:(blk + 1) * blkw],
                             OP.subtract, deps=[s1])
                wtr_free = sp
            return sp

        def issue_w1(it_i):
            k, fq = iters[it_i]
            slot = it_i % 2
            idxs = [idxw1[:, k * 32 + fq * 8 + dc:k * 32 + fq * 8 + dc + 1]
                    for dc in range(DC)]
            w1_tok[slot] = gather_split(w1rows, idxs, w1h[slot], w1l[slot],
                                        DC, 1024, w1_free[slot])

        issue_w1(0)
        for it_i, (k, fq) in enumerate(iters):
            slot = it_i % 2
            idxs = [idxw2[:, k * 32 + fq * 8 + fb:k * 32 + fq * 8 + fb + 1]
                    for fb in range(8)]
            w2_tok = gather_split(w2rows, idxs, w2h, w2l, 8, 1024, w2_free)
            if it_i + 1 < len(iters):
                issue_w1(it_i + 1)
            for tb in range(TBN):
                lb = tb % 2
                ldh = P.load(sb3(lnsh[lb][:], 512), fmc(lnh_fm, tb * 512, 512),
                             deps=[lns_free[lb * 2]])
                ldl = P.load(sb3(lnsl[lb][:], 512), fmc(lnl_fm, tb * 512, 512),
                             deps=[lns_free[lb * 2 + 1]])
                for ft in range(8):
                    pb = ft % 2
                    g = None
                    for dc in range(DC):
                        g = mm3(ph[pb][:],
                                w1h[slot][:, dc * 1024 + ft * 128:
                                           dc * 1024 + (ft + 1) * 128],
                                w1l[slot][:, dc * 1024 + ft * 128:
                                           dc * 1024 + (ft + 1) * 128],
                                lnsh[lb][:, dc * 512:(dc + 1) * 512],
                                lnsl[lb][:, dc * 512:(dc + 1) * 512],
                                dc == 0, dc == DC - 1,
                                deps=[w1_tok[slot], ldh, ldl, ph_free[pb]],
                                inc=(dc == DC - 1))
                    sl = slice(ft * 512, (ft + 1) * 512)
                    h_tok[ft] = P.act(hbf[:, sl], ph[pb][:], AF.Gelu,
                                      bias=b1cols[:, (fq * 8 + ft) * 2 + k:
                                                  (fq * 8 + ft) * 2 + k + 1],
                                      deps=[g, rt['b1c'], h_free])
                    s1 = P.v('tensor_copy', hh[:, sl], hbf[:, sl],
                             deps=[h_tok[ft], hs_free])
                    hs_tok[ft] = P.v('tensor_tensor', hl[:, sl], hbf[:, sl],
                                     hh[:, sl], OP.subtract, deps=[s1])
                    ph_free[pb] = h_tok[ft]
                for grp in range(2):
                    gt = None
                    for fb in range(8):
                        for dtl in range(4):
                            dt = grp * 4 + dtl
                            gt = mm3(py[dtl][:],
                                     w2h[:, fb * 1024 + dt * 128:
                                          fb * 1024 + (dt + 1) * 128],
                                     w2l[:, fb * 1024 + dt * 128:
                                          fb * 1024 + (dt + 1) * 128],
                                     hh[:, fb * 512:(fb + 1) * 512],
                                     hl[:, fb * 512:(fb + 1) * 512],
                                     fb == 0, fb == 7,
                                     deps=[hs_tok[fb], w2_tok, py_free[dtl]],
                                     inc=(fb == 7 and dtl == 3))
                    yb = ysb[grp]
                    ev = None
                    for dtl in range(4):
                        ev = P.v('tensor_scalar',
                                 yb[:, dtl * 512:(dtl + 1) * 512], py[dtl][:],
                                 gcol[:, k:k + 1], None, OP.mult,
                                 deps=[gt, rt['gcol'], ysb_free[grp]])
                        py_free[dtl] = ev
                    dst = y_acc[grp * 512:(grp + 1) * 512,
                                tb * 512:(tb + 1) * 512].rearrange(
                                    '(d p) w -> d p w', p=128).transpose([1, 0, 2])
                    srcv = yb[:].rearrange('p (d w) -> p d w', w=512)
                    y_chain[tb] = P.gstore(dst, srcv,
                                           deps=[ev, y_chain[tb]],
                                           accum=(it_i != 0))
                    ysb_free[grp] = y_chain[tb]
                h_free = gt
                hs_free = gt
                lns_free[lb * 2] = gt
                lns_free[lb * 2 + 1] = gt
            w1_free[slot] = P.snap_tensor_token()
            w2_free = w1_free[slot]

        sc.close()
        P.set_base(P.snap())
        sc = contextlib.ExitStack()
        ysb2 = [sbt(sc, f'fc_y{i}', [128, DC * 512])
                for i in range(2)]
        xsb2 = [sbt(sc, f'fc_x{i}', [128, DC * 512])
                for i in range(2)]
        tsb2 = [sbt(sc, f'fc_t{i}', [128, DC * 512])
                for i in range(2)]
        y2f = [None, None]
        x2f = [None, None]
        t2f = [None, None]
        for tb in range(TBN):
            b = tb % 2
            ly = P.load(sb3(ysb2[b][:], 512), fmc(y_acc, tb * 512, 512),
                        deps=[y2f[b], y_chain[tb]])
            lx = P.load(sb3(xsb2[b][:], 512), fmc(x_fm, tb * 512, 512),
                        deps=[x2f[b]])
            last = None
            for dc in range(DC):
                sl = slice(dc * 512, (dc + 1) * 512)
                q1 = P.v('tensor_scalar', tsb2[b][:, sl], ysb2[b][:, sl],
                         b2comb[:, dc:dc + 1], None, OP.add,
                         deps=[ly, rt['b2cb'], t2f[b]])
                q2 = P.v('tensor_tensor', ysb2[b][:, sl], tsb2[b][:, sl],
                         xsb2[b][:, sl], OP.add, deps=[q1, lx])
                last = q2
            st = P.store(fmc(x_fm, tb * 512, 512), sb3(ysb2[b][:], 512),
                         deps=[last])
            y2f[b] = st
            x2f[b] = last
            t2f[b] = last
        sc.close()

    def first_call_iter(it_i):
        return it_i == 0

    def snap_tensor_token():
        c = P.cnt['st']
        return ('st', c) if c > 0 else None
    P.snap_tensor_token = snap_tensor_token

    def phase_w1a_split(sc):
        wf = [sbt(sc, f'ws_f{i}', [128, DC * 512]) for i in range(2)]
        wh = [sbt(sc, f'ws_h{i}', [128, DC * 512], BF16) for i in range(2)]
        wl = [sbt(sc, f'ws_l{i}', [128, DC * 512], BF16) for i in range(2)]
        f_free = [None, None]
        h_free = [None, None]
        for cc in range(8):
            b = cc % 2
            ld = P.load(sb3(wf[b][:], 512),
                        w1a[:, cc * 512:(cc + 1) * 512].rearrange(
                            '(c p) w -> p c w', p=128), deps=[f_free[b]])
            sp = None
            for dc in range(DC):
                sl = slice(dc * 512, (dc + 1) * 512)
                s1 = P.v('tensor_copy', wh[b][:, sl], wf[b][:, sl],
                         deps=[ld, h_free[b]])
                sp = P.v('tensor_tensor', wl[b][:, sl], wf[b][:, sl],
                         wh[b][:, sl], OP.subtract, deps=[s1])
            sth = P.store(w1ah[:, cc * 512:(cc + 1) * 512].rearrange(
                              '(c p) w -> p c w', p=128), sb3(wh[b][:], 512),
                          deps=[sp])
            stl = P.store(w1al[:, cc * 512:(cc + 1) * 512].rearrange(
                              '(c p) w -> p c w', p=128), sb3(wl[b][:], 512),
                          deps=[sp])
            f_free[b] = sp
            h_free[b] = stl

    def phase_xproj():
        sc = contextlib.ExitStack()
        lnsh = [sbt(sc, f'xp_lnh{i}', [128, DC * 512], BF16) for i in range(2)]
        lnsl = [sbt(sc, f'xp_lnl{i}', [128, DC * 512], BF16) for i in range(2)]
        wcth = [sbt(sc, f'xp_wcth{i}', [128, DC * 128], BF16) for i in range(2)]
        wctl = [sbt(sc, f'xp_wctl{i}', [128, DC * 128], BF16) for i in range(2)]
        wuvh = [sbt(sc, f'xp_wuvh{i}', [128, DC * 512], BF16) for i in range(2)]
        wuvl = [sbt(sc, f'xp_wuvl{i}', [128, DC * 512], BF16) for i in range(2)]
        qst = [sbt(sc, f'xp_qst{i}', [128, 512]) for i in range(2)]
        qsh = [sbt(sc, f'xp_qsh{i}', [128, 512], BF16) for i in range(2)]
        qsl = [sbt(sc, f'xp_qsl{i}', [128, 512], BF16) for i in range(2)]
        ust = [sbt(sc, f'xp_ust{i}', [128, 512]) for i in range(2)]
        vst = [sbt(sc, f'xp_vst{i}', [128, 512]) for i in range(2)]
        b1a_sb = sbt(sc, 'xp_b1a', [1, FF])
        b1f = sbt(sc, 'xp_b1f', [128, 24])
        pp = [pst(sc, f'xp_pp{i}', [128, 512]) for i in range(2)]
        p_t = [pst(sc, f'xp_pt{i}', [128, 1]) for i in range(2)]

        def colof(j):
            return j * 128 if j < 8 else 2048 + (j - 8) * 128

        lb1 = P.load(b1a_sb[:], b1a)
        ptf = [None, None]
        bf_tok = None
        for j in range(24):
            pb = j % 2
            c0 = colof(j)
            tr = P.transp(p_t[pb][:], b1a_sb[0:1, c0:c0 + 128],
                          ident[0:1, 0:1], deps=[lb1, ptf[pb]])
            cp = P.v('tensor_copy', b1f[:, j:j + 1], p_t[pb][:], deps=[tr])
            ptf[pb] = cp
            bf_tok = cp

        lns_free = [None, None]
        wct_tok = [None, None]
        wct_free = [None, None]
        wuv_tok = [None, None]
        wuv_free = [None, None]
        pp_free = [None, None]
        qst_free = [None, None]
        qs_free = [None, None]
        ust_free = [None, None]
        vst_free = [None, None]
        g = None
        for tb in range(TBN):
            lb = tb % 2
            ldh = P.load(sb3(lnsh[lb][:], 512), fmc(lnh_fm, tb * 512, 512),
                         deps=[lns_free[lb]])
            ldl = P.load(sb3(lnsl[lb][:], 512), fmc(lnl_fm, tb * 512, 512),
                         deps=[lns_free[lb]])
            for j in range(24):
                wb = j % 2
                c0 = colof(j)
                wct_tok[wb] = P.load(
                    sb3(wcth[wb][:], 128),
                    w1ah[:, c0:c0 + 128].rearrange('(c p) w -> p c w', p=128),
                    deps=[wct_free[wb]])
                wct_tok2 = P.load(
                    sb3(wctl[wb][:], 128),
                    w1al[:, c0:c0 + 128].rearrange('(c p) w -> p c w', p=128),
                    deps=[wct_free[wb]])
                g = None
                for dc in range(DC):
                    g = mm3(pp[wb][:],
                            wcth[wb][:, dc * 128:(dc + 1) * 128],
                            wctl[wb][:, dc * 128:(dc + 1) * 128],
                            lnsh[lb][:, dc * 512:(dc + 1) * 512],
                            lnsl[lb][:, dc * 512:(dc + 1) * 512],
                            dc == 0, dc == DC - 1,
                            deps=[wct_tok[wb], wct_tok2, ldh, ldl,
                                  pp_free[wb]],
                            inc=(dc == DC - 1))
                sl = P.act(qst[wb][:], pp[wb][:], AF.Silu,
                           bias=b1f[:, j:j + 1], deps=[g, bf_tok, qst_free[wb]])
                if j < 8:
                    st = P.store(proj_fm[j * 128:(j + 1) * 128,
                                         tb * 512:(tb + 1) * 512], qst[wb][:],
                                 deps=[sl])
                    qst_free[wb] = st
                else:
                    r0 = (j - 8) * 128
                    s1 = P.v('tensor_copy', qsh[wb][:], qst[wb][:],
                             deps=[sl, qs_free[wb]])
                    s2 = P.v('tensor_tensor', qsl[wb][:], qst[wb][:],
                             qsh[wb][:], OP.subtract, deps=[s1])
                    sth = P.store(projh[r0:r0 + 128,
                                        tb * 512:(tb + 1) * 512], qsh[wb][:],
                                  deps=[s2])
                    stl = P.store(projl[r0:r0 + 128,
                                        tb * 512:(tb + 1) * 512], qsl[wb][:],
                                  deps=[s2])
                    qst_free[wb] = s2
                    qs_free[wb] = stl
                pp_free[wb] = sl
                wct_free[wb] = g
            for ci, cc in enumerate((2, 3)):
                wb = ci % 2
                wuv_tok[wb] = P.load(
                    sb3(wuvh[wb][:], 512),
                    w1ah[:, cc * 512:(cc + 1) * 512].rearrange(
                        '(c p) w -> p c w', p=128),
                    deps=[wuv_free[wb]])
                wuv_tok2 = P.load(
                    sb3(wuvl[wb][:], 512),
                    w1al[:, cc * 512:(cc + 1) * 512].rearrange(
                        '(c p) w -> p c w', p=128),
                    deps=[wuv_free[wb]])
                for t4 in range(4):
                    pb = (ci * 4 + t4) % 2
                    g = None
                    for dc in range(DC):
                        g = mm3(pp[pb][:],
                                lnsh[lb][:, dc * 512 + t4 * 128:
                                        dc * 512 + (t4 + 1) * 128],
                                lnsl[lb][:, dc * 512 + t4 * 128:
                                        dc * 512 + (t4 + 1) * 128],
                                wuvh[wb][:, dc * 512:(dc + 1) * 512],
                                wuvl[wb][:, dc * 512:(dc + 1) * 512],
                                dc == 0, False,
                                deps=[wuv_tok[wb], wuv_tok2, ldh, ldl,
                                      pp_free[pb]])
                    g = P.mm(pp[pb][:], ones_row[:],
                             b1a_sb[0:1, cc * 512:(cc + 1) * 512],
                             False, True, deps=[lb1], inc=True)
                    row0 = tb * 512 + t4 * 128
                    sl = P.act(ust[pb][:], pp[pb][:], AF.Silu,
                               deps=[g, ust_free[pb]])
                    sc_ = P.v('tensor_scalar', vst[pb][:], ust[pb][:],
                              1.0 / T, None, OP.mult,
                              deps=[sl, vst_free[pb]])
                    st = P.store(v_tm[row0:row0 + 128,
                                      (cc - 2) * 512:(cc - 1) * 512],
                                 vst[pb][:], deps=[sc_])
                    ust_free[pb] = sc_
                    vst_free[pb] = st
                    pp_free[pb] = sl
                wuv_free[wb] = g
            lns_free[lb] = g
        sc.close()

    def phase_attn_core():
        sc = contextlib.ExitStack()
        qhh = [sbt(sc, f'at_qh{i}', [128, T], BF16) for i in range(2)]
        qhl = [sbt(sc, f'at_ql{i}', [128, T], BF16) for i in range(2)]
        khh = [sbt(sc, f'at_kh{i}', [128, T], BF16) for i in range(2)]
        khl = [sbt(sc, f'at_kl{i}', [128, T], BF16) for i in range(2)]
        vh = [sbt(sc, f'at_v{i}', [128, NTT * 64]) for i in range(2)]
        uh = [sbt(sc, f'at_u{i}', [64, T]) for i in range(2)]
        att = [sbt(sc, f'at_a{i}', [128, 512]) for i in range(2)]
        avf = [sbt(sc, f'at_avf{i}', [64, T]) for i in range(2)]
        ps = [pst(sc, f'at_ps{i}', [128, 512]) for i in range(2)]
        pav = [pst(sc, f'at_pav{i}', [64, 512]) for i in range(2)]

        zp = None
        for i in range(2):
            for tns in (qhh, qhl, khh, khl):
                zp = P.v('memset', tns[i][64:128, :], 0.0, deps=[zp])
        zp_tok = zp

        set_free = [None, None]
        ps_free = [None, None]
        pav_free = [None, None]
        att_free = [None, None]
        avf_free = [None, None]
        for h in range(NH):
            hb = h % 2
            lq1 = P.load(qhh[hb][0:64, :], projh[h * 64:(h + 1) * 64, :],
                         deps=[set_free[hb], zp_tok])
            lq2 = P.load(qhl[hb][0:64, :], projl[h * 64:(h + 1) * 64, :],
                         deps=[set_free[hb], zp_tok])
            lk1 = P.load(khh[hb][0:64, :], projh[D + h * 64:D + (h + 1) * 64, :],
                         deps=[set_free[hb], zp_tok])
            lk2 = P.load(khl[hb][0:64, :], projl[D + h * 64:D + (h + 1) * 64, :],
                         deps=[set_free[hb], zp_tok])
            lv = P.load(vh[hb][:].rearrange('p (t w) -> p t w', w=64),
                        v_tm[:, h * 64:(h + 1) * 64].rearrange(
                            '(t p) w -> p t w', p=128),
                        deps=[set_free[hb]])
            lu = P.load(uh[hb][:], proj_fm[h * 64:(h + 1) * 64, :],
                        deps=[set_free[hb]])
            mu_t = None
            for qc in range(8):
                gav = None
                for kt in range(NTT):
                    ab = kt % 2
                    sm = mm3(ps[ab][:],
                             khh[hb][:, kt * 128:(kt + 1) * 128],
                             khl[hb][:, kt * 128:(kt + 1) * 128],
                             qhh[hb][:, qc * 512:(qc + 1) * 512],
                             qhl[hb][:, qc * 512:(qc + 1) * 512],
                             True, True,
                             deps=[lq1, lq2, lk1, lk2, ps_free[ab]], inc=True)
                    sl = P.act(att[ab][:], ps[ab][:], AF.Silu,
                               deps=[sm, att_free[ab]])
                    ps_free[ab] = sl
                    gav = P.mm(pav[qc % 2][:],
                               vh[hb][:, kt * 64:(kt + 1) * 64],
                               att[ab][:],
                               kt == 0, kt == NTT - 1,
                               deps=[sl, lv] +
                                    ([pav_free[qc % 2]] if kt == 0 else []),
                               inc=(kt == NTT - 1))
                    att_free[ab] = gav if kt == NTT - 1 else None
                mu_t = P.v('tensor_tensor',
                           avf[hb][:, qc * 512:(qc + 1) * 512],
                           pav[qc % 2][:],
                           uh[hb][:, qc * 512:(qc + 1) * 512], OP.mult,
                           deps=[gav, lu, avf_free[hb]])
                pav_free[qc % 2] = mu_t
            stv = P.store(avu_fm[h * 64:(h + 1) * 64, :], avf[hb][:],
                          deps=[mu_t])
            avf_free[hb] = stv
            set_free[hb] = mu_t

        sc.close()
        return None

    def phase_outproj(stats_tok, dump_z=False):
        sc = contextlib.ExitStack()
        w2gh = sbt(sc, 'op_w2gh', [128, DC * 1024], BF16)
        w2gl = sbt(sc, 'op_w2gl', [128, DC * 1024], BF16)
        wgs = sbt(sc, 'op_wgs', [128, 1024])
        wch = [sbt(sc, f'op_wch{i}', [128, 1024]) for i in range(2)]
        gcols = sbt(sc, 'op_gc', [128, 8])
        bcols = sbt(sc, 'op_bc', [128, 8])
        b2cols = sbt(sc, 'op_b2c', [128, 8])
        biasc = sbt(sc, 'op_bias', [128, 8])
        avs = [sbt(sc, f'op_av{i}', [128, DC * 512]) for i in range(2)]
        avq = [sbt(sc, 'op_aq0', [128, DC * 512])] * 2
        tpz = avq
        zfm = [sbt(sc, f'op_zfm{i}', [128, DC * 512]) for i in range(2)]
        zfh = [sbt(sc, 'op_zfh0', [128, DC * 512], BF16)] * 2
        zfl = [sbt(sc, 'op_zfl0', [128, DC * 512], BF16)] * 2
        xch = [sbt(sc, 'op_xch0', [128, DC * 512])] * 2
        xnb = [sbt(sc, 'op_xnb0', [128, DC * 512])] * 2
        yst = [sbt(sc, 'op_yst0', [128, DC * 512])] * 2
        mu_s = sbt(sc, 'op_mu', [1, 512])
        ms_s = sbt(sc, 'op_ms', [1, 512])
        m2_s = sbt(sc, 'op_m2', [1, 512])
        va_s = sbt(sc, 'op_va', [1, 512])
        sd_s = sbt(sc, 'op_sd', [1, 512])
        rs_s = sbt(sc, 'op_rs', [1, 512])
        p_b = pst(sc, 'op_pb', [128, 8])
        p_y = [pst(sc, f'op_py{i}', [128, 512]) for i in range(2)]
        p_sum = pst(sc, 'op_psum', [1, 512])
        p_sq = pst(sc, 'op_psq', [1, 512])
        p_mu = pst(sc, 'op_pmu', [128, 512])
        p_rs = pst(sc, 'op_prs', [128, 512])

        lg_ = P.load(gcols[:], gam[0].rearrange('(c p) -> p c', p=128),
                     slow=True)
        lb_ = P.load(bcols[:], bet[0].rearrange('(c p) -> p c', p=128),
                     slow=True)
        lb2 = P.load(b2cols[:], b2a[0].rearrange('(c p) -> p c', p=128),
                     slow=True)
        wch_tok = [None, None]
        wch_free = [None, None]
        bias_mm = None
        w2g_tok = None
        for dc in range(DC):
            wb = dc % 2
            wch_tok[wb] = P.load(wch[wb][:], w2a[dc * 128:(dc + 1) * 128, :],
                                 deps=(wch_free[wb] or []))
            for dt in range(DC):
                bias_mm = P.mm(p_b[:, dt:dt + 1],
                               wch[wb][:, dt * 128:(dt + 1) * 128],
                               bcols[:, dc:dc + 1],
                               dc == 0 and dt == 0,
                               dc == DC - 1 and dt == DC - 1,
                               deps=[wch_tok[wb], lb_],
                               inc=(dt == DC - 1))
            wg1 = P.v('tensor_scalar', wgs[:], wch[wb][:],
                      gcols[:, dc:dc + 1], None, OP.mult,
                      deps=[wch_tok[wb], lg_, w2g_tok])
            wg2 = P.v('tensor_copy', w2gh[:, dc * 1024:(dc + 1) * 1024],
                      wgs[:], deps=[wg1])
            w2g_tok = P.v('tensor_tensor', w2gl[:, dc * 1024:(dc + 1) * 1024],
                          wgs[:], w2gh[:, dc * 1024:(dc + 1) * 1024],
                          OP.subtract, deps=[wg2])
            wch_free[wb] = [w2g_tok, bias_mm]
        bias_tok = P.v('tensor_tensor', biasc[:], p_b[:], b2cols[:], OP.add,
                       deps=[bias_mm, lb2])

        av_free = [None, None]
        zs_free = SharedTok()
        aq_free = SharedTok()
        tp_free = SharedTok()
        zfm_free = [None, None]
        py_free = [None, None]
        xch_free = SharedTok()
        xnb_free = SharedTok()
        yst_free = SharedTok()
        war_psum = war_psq = war_pmu = war_prs = None
        for tb in range(TBN):
            zb = tb % 2
            la = P.load(sb3(avs[zb][:], 512), fmc(avu_fm, tb * 512, 512),
                        deps=[av_free[zb]])
            lx = P.load(sb3(xch[zb][:], 512), fmc(x_fm, tb * 512, 512),
                        deps=[xch_free[zb]])
            sq = None
            for dc in range(DC):
                sq = P.act(avq[zb][:, dc * 512:(dc + 1) * 512],
                           avs[zb][:, dc * 512:(dc + 1) * 512], AF.Square,
                           deps=[la, aq_free[zb]])
            tsum = tsq = None
            for dc in range(DC):
                tsum = P.mm(p_sum[:], ones_col[:],
                            avs[zb][:, dc * 512:(dc + 1) * 512],
                            dc == 0, dc == DC - 1,
                            deps=[la, war_psum], inc=(dc == DC - 1))
            for dc in range(DC):
                tsq = P.mm(p_sq[:], ones_col[:],
                           avq[zb][:, dc * 512:(dc + 1) * 512],
                           dc == 0, dc == DC - 1,
                           deps=[sq, war_psq], inc=(dc == DC - 1))
            u1 = P.v('tensor_scalar', mu_s[:], p_sum[:], 1.0 / D, None,
                     OP.mult, deps=[tsum])
            u2 = P.v('tensor_scalar', ms_s[:], p_sq[:], 1.0 / D, EPS,
                     OP.mult, OP.add, deps=[tsq])
            u3 = P.v('tensor_tensor', m2_s[:], mu_s[:], mu_s[:], OP.mult,
                     deps=[u1])
            u4 = P.v('tensor_tensor', va_s[:], ms_s[:], m2_s[:], OP.subtract,
                     deps=[u2, u3])
            u5 = P.act(sd_s[:], va_s[:], AF.Sqrt, deps=[u4])
            u6 = P.v('reciprocal', rs_s[:], sd_s[:], deps=[u5])
            war_psum = u1
            war_psq = u2
            bmu = P.mm(p_mu[:], ones_row[:], mu_s[:], True, True,
                       deps=[u3, war_pmu], inc=True)
            brs = P.mm(p_rs[:], ones_row[:], rs_s[:], True, True,
                       deps=[u6, war_prs], inc=True)
            last_n = None
            for dc in range(DC):
                sl = slice(dc * 512, (dc + 1) * 512)
                n1 = P.v('tensor_tensor', tpz[zb][:, sl], avs[zb][:, sl],
                         p_mu[:], OP.subtract,
                         deps=[bmu, la, tsq, tp_free[zb]])
                n2 = P.v('tensor_tensor', zfm[zb][:, sl], tpz[zb][:, sl],
                         p_rs[:], OP.mult, deps=[n1, brs, zfm_free[zb]])
                z1 = P.v('tensor_copy', zfh[zb][:, sl], zfm[zb][:, sl],
                         deps=[n2, zs_free[0]])
                last_n = P.v('tensor_tensor', zfl[zb][:, sl], zfm[zb][:, sl],
                             zfh[zb][:, sl], OP.subtract,
                             deps=[z1, zs_free[0]])
            war_pmu = last_n
            war_prs = last_n
            if dump_z:
                P.store(fmc(y_acc, tb * 512, 512), sb3(zfm[zb][:], 512),
                        deps=[last_n])
            last_q = None
            for dt in range(DC):
                pb = dt % 2
                g = None
                for dc in range(DC):
                    g = mm3(p_y[pb][:],
                            w2gh[:, dc * 1024 + dt * 128:
                                 dc * 1024 + (dt + 1) * 128],
                            w2gl[:, dc * 1024 + dt * 128:
                                 dc * 1024 + (dt + 1) * 128],
                            zfh[zb][:, dc * 512:(dc + 1) * 512],
                            zfl[zb][:, dc * 512:(dc + 1) * 512],
                            dc == 0, dc == DC - 1,
                            deps=[w2g_tok, last_n, py_free[pb]],
                            inc=(dc == DC - 1))
                sl = slice(dt * 512, (dt + 1) * 512)
                q1 = P.v('tensor_scalar', yst[zb][:, sl], p_y[pb][:],
                         biasc[:, dt:dt + 1], None, OP.add,
                         deps=[g, bias_tok, yst_free[zb]])
                q2 = P.v('tensor_tensor', xnb[zb][:, sl], yst[zb][:, sl],
                         xch[zb][:, sl], OP.add, deps=[q1, lx, xnb_free[zb]])
                py_free[pb] = q1
                last_q = q2
            st = P.store(fmc(x_fm, tb * 512, 512), sb3(xnb[zb][:], 512),
                         deps=[last_q])
            av_free[zb] = last_n
            aq_free[zb] = last_n
            tp_free[zb] = last_n
            zfm_free[zb] = last_n
            zs_free[0] = last_q
            xch_free[zb] = last_q
            xnb_free[zb] = st
            yst_free[zb] = last_q
        sc.close()

    def phase_out_transpose(src):
        sc = contextlib.ExitStack()
        chunk = [sbt(sc, f'pf_ch{i}', [128, T])
                 for i in range(DC)]
        ostg = [sbt(sc, f'pf_os{i}', [128, D])
                for i in range(2)]
        ptr = [pst(sc, f'pf_ps{i}', [128, 128])
               for i in range(2)]
        lds = [P.load(chunk[dc][:], src[dc * 128:(dc + 1) * 128, :])
               for dc in range(DC)]
        ptr_free = [None, None]
        ostg_free = [None, None]
        i = 0
        for tt in range(NTT):
            ob = tt % 2
            last_cp = None
            for dc in range(DC):
                pb = i % 2
                tr = P.transp(ptr[pb][:],
                              chunk[dc][:, tt * 128:(tt + 1) * 128],
                              ident[:], deps=[lds[dc], ptr_free[pb]])
                cp = P.v('tensor_copy', ostg[ob][:, dc * 128:(dc + 1) * 128],
                         ptr[pb][:], deps=[tr, ostg_free[ob]])
                ptr_free[pb] = cp
                last_cp = cp
                i += 1
            st = P.store(out[tt * 128:(tt + 1) * 128, :], ostg[ob][:],
                         deps=[last_cp])
            ostg_free[ob] = st
        sc.close()

    def phase_dump_bf16_fm(srcd):
        sc = contextlib.ExitStack()
        bb = [sbt(sc, f'db_b{i}', [128, DC * 512], BF16) for i in range(2)]
        ff = [sbt(sc, f'db_f{i}', [128, DC * 512]) for i in range(2)]
        bfree = [None, None]
        ffree = [None, None]
        for tb in range(TBN):
            b = tb % 2
            ld = P.load(sb3(bb[b][:], 512), fmc(srcd, tb * 512, 512),
                        deps=[bfree[b]])
            cp = P.v('tensor_copy', ff[b][:], bb[b][:], deps=[ld, ffree[b]])
            st = P.store(fmc(y_acc, tb * 512, 512), sb3(ff[b][:], 512),
                         deps=[cp])
            bfree[b] = cp
            ffree[b] = st
        sc.close()

    def phase_copy_tm(srcd, col0):
        sc = contextlib.ExitStack()
        stg = [sbt(sc, f'cp_st{i}', [128, D]) for i in range(2)]
        free = [None, None]
        for tt in range(NTT):
            b = tt % 2
            ld = P.load(stg[b][:], srcd[tt * 128:(tt + 1) * 128, col0:col0 + D],
                        deps=[free[b]])
            st = P.store(out[tt * 128:(tt + 1) * 128, :], stg[b][:], deps=[ld])
            free[b] = st
        sc.close()

    phase_in_transpose()
    P.set_base(P.snap())
    if mode == 'transpose':
        phase_out_transpose(x_fm)
    else:
        pc, pt = phase_ln(x_fm, ln_fm, True)
        P.set_base(P.snap())
        if mode == 'ln':
            phase_out_transpose(ln_fm)
        else:
            rt = phase_routing(pc, pt, split_w1a=True)
            P.set_base(P.snap())
            if mode in ('w1ahdump', 'w1aldump'):
                phase_dump_bf16_fm(w1ah if mode == 'w1ahdump' else w1al)
                P.set_base(P.snap())
                phase_out_transpose(y_acc)
                P.set_base([])
                P._op('sync', None, False, P.snap(), lambda h: h.nop())
                with nc.Block() as block:
                    P.emit(block, sems)
                es.close()
                return nc
            phase_ffn(rt, True)
            P.set_base(P.snap())
            if mode == 'ffn1':
                phase_out_transpose(x_fm)
            else:
                phase_ln(x_fm, ln_fm, False)
                P.set_base(P.snap())
                phase_xproj()
                P.set_base(P.snap())
                if mode in ('qpart', 'kpart', 'upart', 'vpart'):
                    if mode == 'qpart':
                        phase_out_transpose(proj_fm[D:2 * D, :])
                    elif mode == 'kpart':
                        phase_out_transpose(proj_fm[2 * D:3 * D, :])
                    elif mode == 'upart':
                        phase_out_transpose(proj_fm[0:D, :])
                    else:
                        phase_copy_tm(v_tm, 0)
                    P.set_base([])
                    P._op('sync', None, False, P.snap(), lambda h: h.nop())
                    with nc.Block() as block:
                        P.emit(block, sems)
                    es.close()
                    return nc
                stats_tok = phase_attn_core()
                P.set_base(P.snap())
                if mode == 'avu':
                    phase_out_transpose(avu_fm)
                    P.set_base([])
                    P._op('sync', None, False, P.snap(), lambda h: h.nop())
                    with nc.Block() as block:
                        P.emit(block, sems)
                    es.close()
                    return nc
                phase_outproj(stats_tok, dump_z=(mode == 'zdump'))
                P.set_base(P.snap())
                if mode == 'zdump':
                    phase_out_transpose(y_acc)
                elif mode == 'attn':
                    phase_out_transpose(x_fm)
                else:
                    pc, pt = phase_ln(x_fm, ln_fm, True)
                    P.set_base(P.snap())
                    rt = phase_routing(pc, pt)
                    P.set_base(P.snap())
                    phase_ffn(rt, True)
                    P.set_base(P.snap())
                    phase_out_transpose(x_fm)

    P.set_base([])
    P._op('sync', None, False, P.snap(), lambda h: h.nop())

    with nc.Block() as block:
        P.emit(block, sems)
    es.close()
    return nc


_NC_CACHE = {}


def _get_program(mode):
    if mode not in _NC_CACHE:
        _NC_CACHE[mode] = build_program(mode)
    return _NC_CACHE[mode]


def kernel(L0_attn_x, L0_attn_W1, L0_attn_b1, L0_attn_W2, L0_attn_b2,
           L0_attn_gamma, L0_attn_beta, L0_ffn_Wr, L0_ffn_W1, L0_ffn_b1,
           L0_ffn_W2, L0_ffn_b2, mode='full'):
    f = lambda a: np.ascontiguousarray(np.asarray(a, dtype=np.float32))
    x = f(L0_attn_x)
    shared = dict(
        w1a=f(L0_attn_W1), b1a=f(L0_attn_b1).reshape(1, FF),
        w2a=f(L0_attn_W2), b2a=f(L0_attn_b2).reshape(1, D),
        gam=f(L0_attn_gamma).reshape(1, D), bet=f(L0_attn_beta).reshape(1, D),
        wr=f(L0_ffn_Wr), w1m=f(L0_ffn_W1), b1m=f(L0_ffn_b1),
        w2m=f(L0_ffn_W2), b2m=f(L0_ffn_b2))
    in_maps = [dict(x=np.ascontiguousarray(x[b]), **shared)
               for b in range(N_CORES)]
    nc = _get_program(mode)
    res = run_bass_kernel_spmd(nc, in_maps, core_ids=list(range(N_CORES)))
    return np.stack([res.results[b]['out'] for b in range(N_CORES)], axis=0)


# version 05
# speedup vs baseline: 1.6626x; 1.0977x over previous
import contextlib
import numpy as np

import concourse.bass as bass
import concourse.mybir as mybir
from concourse.bass_utils import run_bass_kernel_spmd

F32 = mybir.dt.float32
BF16 = mybir.dt.bfloat16
U32 = mybir.dt.uint32
AF = mybir.ActivationFunctionType
OP = mybir.AluOpType
AX = mybir.AxisListType

D, T, FF, NH, DH, E = 1024, 4096, 4096, 16, 64, 8
DC = D // 128
TBN = T // 512
NTT = T // 128
EPS = 1e-5
N_CORES = 8

ENGINES = ('sync', 'gpsimd', 'tensor', 'scalar', 'vector')


class SharedTok:

    def __init__(self):
        self.t = None

    def __getitem__(self, i):
        return self.t

    def __setitem__(self, i, v):
        self.t = v


class Prog:

    NPOOL = 6

    def __init__(self):
        self.streams = {e: [] for e in ENGINES}
        self.cnt = dict(st=0, ss=0, sv=0, sg=0)
        for q in ('dl', 'ds', 'dg'):
            for i in range(self.NPOOL):
                self.cnt[f'{q}{i}'] = 0
        self.rr = dict(dl=0, ds=0, dg=0)
        self.base = {}

    def _dma(self, eng, q, deps, fn):
        i = self.rr[q]
        self.rr[q] = (i + 1) % self.NPOOL
        sem = f'{q}{i}'
        prev = self.cnt[sem]
        dd = list(deps)
        if prev:
            dd.append((sem, prev))
        return self._op(eng, sem, True, dd, fn)

    def set_base(self, toks):
        self.base = {}
        for t in toks:
            if t is None:
                continue
            s, c = t
            if c > self.base.get(s, 0):
                self.base[s] = c

    def _op(self, eng, sem, dma, deps, fn):
        tok = None
        if sem is not None:
            self.cnt[sem] += 16 if dma else 1
            tok = (sem, self.cnt[sem])
        dd = dict(self.base)
        for d in deps:
            if d is None:
                continue
            s, c = d
            if c > dd.get(s, 0):
                dd[s] = c
        self.streams[eng].append((fn, dd, sem, 16 if dma else 1))
        return tok

    def load(self, dst, src, deps=(), slow=False):
        if slow:
            return self._dma('sync', 'dl', deps,
                             lambda h, a=dst, b=src: h.dma_start(
                                 out=a, in_=b, allow_slow_non_contiguous=True))
        return self._dma('sync', 'dl', deps,
                         lambda h, a=dst, b=src: h.dma_start(out=a, in_=b))

    def store(self, dst, src, deps=()):
        return self._dma('scalar', 'ds', deps,
                         lambda h, a=dst, b=src: h.dma_start(out=a, in_=b))

    def gstore(self, dst, src, deps=(), accum=False):
        if accum:
            return self._dma('gpsimd', 'dg', deps,
                             lambda h, a=dst, b=src:
                                 h.dma_start(out=a, in_=b, accum_op=OP.add))
        return self._dma('gpsimd', 'dg', deps,
                         lambda h, a=dst, b=src: h.dma_start(out=a, in_=b))

    def gather(self, dst, src_rows, idx, deps=()):
        return self._dma('gpsimd', 'dg', deps,
                         lambda h, a=dst, b=src_rows, i=idx: h.indirect_dma_start(
                             out=a, out_offset=None, in_=b,
                             in_offset=bass.IndirectOffsetOnAxis(ap=i, axis=0)))

    def mm(self, out, stat, mov, start, stop, deps=(), inc=False):
        return self._op('tensor', 'st' if inc else None, False, deps,
                        lambda h, o=out, s=stat, m=mov, a=start, b=stop:
                            h.matmul(o, s, m, start=a, stop=b))

    def transp(self, out, in_, ident, deps=(), inc=True):
        return self._op('tensor', 'st' if inc else None, False, deps,
                        lambda h, o=out, i=in_, d=ident: h.transpose(o, i, d))

    def act(self, out, in_, func, deps=(), bias=None, scale=1.0):
        if bias is None:
            return self._op('scalar', 'ss', False, deps,
                            lambda h, o=out, i=in_, f=func, sc=scale:
                                h.activation(o, i, f, scale=sc))
        return self._op('scalar', 'ss', False, deps,
                        lambda h, o=out, i=in_, f=func, b=bias, sc=scale:
                            h.activation(o, i, f, bias=b, scale=sc))

    def v(self, name, *args, deps=(), **kw):
        return self._op('vector', 'sv', False, deps,
                        lambda h, n=name, a=args, k=kw: getattr(h, n)(*a, **k))

    def g(self, name, *args, deps=(), **kw):
        return self._op('gpsimd', 'sg', False, deps,
                        lambda h, n=name, a=args, k=kw: getattr(h, n)(*a, **k))

    def snap(self):
        return [(s, c) for s, c in self.cnt.items() if c > 0]

    def emit(self, block, sems):
        for eng in ENGINES:
            items = self.streams[eng]
            if not items:
                continue

            def mk(items=items):
                def body(h):
                    hw = {}
                    for fn, dd, sem, amt in items:
                        for s, c in sorted(dd.items()):
                            if c > hw.get(s, 0):
                                h.wait_ge(sems[s], c)
                                hw[s] = c
                        inst = fn(h)
                        if sem is not None:
                            inst.then_inc(sems[sem], amt)
                return body
            getattr(block, eng)(mk())


def build_program(mode='full'):
    nc = bass.Bass('TRN2', target_bir_lowering=False)
    x_in = nc.dram_tensor('x', [T, D], F32, kind='ExternalInput').ap()
    w1a = nc.dram_tensor('w1a', [D, FF], F32, kind='ExternalInput').ap()
    b1a = nc.dram_tensor('b1a', [1, FF], F32, kind='ExternalInput').ap()
    w2a = nc.dram_tensor('w2a', [D, D], F32, kind='ExternalInput').ap()
    b2a = nc.dram_tensor('b2a', [1, D], F32, kind='ExternalInput').ap()
    gam = nc.dram_tensor('gam', [1, D], F32, kind='ExternalInput').ap()
    bet = nc.dram_tensor('bet', [1, D], F32, kind='ExternalInput').ap()
    wr = nc.dram_tensor('wr', [D, E], F32, kind='ExternalInput').ap()
    w1m = nc.dram_tensor('w1m', [E, D, FF], F32, kind='ExternalInput').ap()
    b1m = nc.dram_tensor('b1m', [E, FF], F32, kind='ExternalInput').ap()
    w2m = nc.dram_tensor('w2m', [E, FF, D], F32, kind='ExternalInput').ap()
    b2m = nc.dram_tensor('b2m', [E, D], F32, kind='ExternalInput').ap()
    out = nc.dram_tensor('out', [T, D], F32, kind='ExternalOutput').ap()

    x_fm = nc.dram_tensor('x_fm', [D, T], F32).ap()
    ln_fm = nc.dram_tensor('ln_fm', [D, T], F32).ap()
    lnh_fm = nc.dram_tensor('lnh_fm', [D, T], BF16).ap()
    lnl_fm = nc.dram_tensor('lnl_fm', [D, T], BF16).ap()
    y_acc = nc.dram_tensor('y_acc', [D, T], F32).ap()
    proj_fm = nc.dram_tensor('proj_fm', [3 * D, T], F32).ap()
    projh = nc.dram_tensor('projh', [2 * D, T], BF16).ap()
    projl = nc.dram_tensor('projl', [2 * D, T], BF16).ap()
    w1ah = nc.dram_tensor('w1ah', [D, FF], BF16).ap()
    w1al = nc.dram_tensor('w1al', [D, FF], BF16).ap()
    v_tm = nc.dram_tensor('v_tm', [T, D], F32).ap()
    avu_fm = nc.dram_tensor('avu_fm', [D, T], F32).ap()

    w1rows = w1m.rearrange('e d (q c) -> (e d q) c', c=1024)
    w2rows = w2m.rearrange('e f d -> (e f) d')

    def fmc(dram, t0, w):
        return dram[:, t0:t0 + w].rearrange('(c p) w -> p c w', p=128)

    def sb3(ap, w):
        return ap.rearrange('p (c w) -> p c w', w=w)

    P = Prog()
    uid = [0]

    def sbt(sc_, name, shape, dtype=F32):
        uid[0] += 1
        return sc_.enter_context(
            nc.sbuf_tensor(f'{name}_{uid[0]}', shape, dtype)).ap()

    def pst(sc_, name, shape):
        uid[0] += 1
        return sc_.enter_context(
            nc.psum_tensor(f'{name}_{uid[0]}', shape, F32)).ap()

    def mm3(out, sh, sl_, mh, ml, first, last, deps=(), inc=False):
        P.mm(out, sh, mh, first, False, deps=deps)
        P.mm(out, sh, ml, False, False, deps=deps)
        return P.mm(out, sl_, mh, False, last, deps=deps, inc=inc)

    es = contextlib.ExitStack()
    sems = {s: es.enter_context(nc.semaphore(s)) for s in P.cnt}

    ident = nc.alloc_sbuf_tensor('ident', [128, 128], F32).ap()
    iota128 = nc.alloc_sbuf_tensor('iota128', [128, 1], F32).ap()
    iota4 = nc.alloc_sbuf_tensor('iota4', [128, 1], F32).ap()
    iota8 = nc.alloc_sbuf_tensor('iota8', [1, 8], F32).ap()
    ones_row = nc.alloc_sbuf_tensor('ones_row', [1, 128], F32).ap()
    ones_col = nc.alloc_sbuf_tensor('ones_col', [128, 1], F32).ap()
    pooled = nc.alloc_sbuf_tensor('pooled', [128, 16], F32).ap()
    gcol = nc.alloc_sbuf_tensor('gcol', [128, 4], F32).ap()
    b1cols = nc.alloc_sbuf_tensor('b1cols', [128, 64], F32).ap()
    b2comb = nc.alloc_sbuf_tensor('b2comb', [128, 8], F32).ap()
    idxw1 = nc.alloc_sbuf_tensor('idxw1', [128, 64], U32).ap()
    idxw2 = nc.alloc_sbuf_tensor('idxw2', [128, 64], U32).ap()

    c0 = P.g('memset', ident[:], 0.0)
    c1 = P.g('affine_select', out=ident[:], in_=ident[:],
             compare_op=OP.not_equal, fill=1.0, base=0,
             pattern=[[-1, 128]], channel_multiplier=1, deps=[c0])
    c2 = P.g('iota', iota128[:], [[0, 1]], channel_multiplier=1,
             allow_small_or_imprecise_dtypes=True, deps=[c1])
    c3 = P.g('iota', iota8[:], [[1, 8]], channel_multiplier=0,
             allow_small_or_imprecise_dtypes=True, deps=[c2])
    c4 = P.g('memset', ones_row[:], 1.0, deps=[c3])
    c5 = P.g('memset', ones_col[:], 1.0, deps=[c4])
    c6 = P.v('tensor_scalar', iota4[:], iota128[:], 4.0, None, OP.mult,
             deps=[c5])
    P.set_base([c6])

    def phase_in_transpose():
        sc = contextlib.ExitStack()
        stage = [sbt(sc, f'p0stg{i}', [128, T])
                 for i in range(DC)]
        xin = [sbt(sc, f'p0xin{i}', [128, D])
               for i in range(2)]
        ptr = [pst(sc, f'p0ps{i}', [128, 128])
               for i in range(2)]
        xin_free = [None, None]
        ptr_free = [None, None]
        last_cp = None
        i = 0
        for tt in range(NTT):
            b = tt % 2
            ld = P.load(xin[b][:], x_in[tt * 128:(tt + 1) * 128, :],
                        deps=[xin_free[b]])
            for dc in range(DC):
                pb = i % 2
                tr = P.transp(ptr[pb][:], xin[b][:, dc * 128:(dc + 1) * 128],
                              ident[:], deps=[ld, ptr_free[pb]])
                cp = P.v('tensor_copy', stage[dc][:, tt * 128:(tt + 1) * 128],
                         ptr[pb][:], deps=[tr])
                ptr_free[pb] = cp
                last_cp = cp
                i += 1
            xin_free[b] = last_cp
        for dc in range(DC):
            P.store(x_fm[dc * 128:(dc + 1) * 128, :], stage[dc][:],
                    deps=[last_cp])
        sc.close()

    def phase_ln(src, dst, want_pooled):
        sc = contextlib.ExitStack()
        xs = [sbt(sc, f'lnxs{i}', [128, DC * 512])
              for i in range(2)]
        xq = [sbt(sc, f'lnxq{i}', [128, DC * 512])
              for i in range(2)]
        tp = [sbt(sc, f'lntp{i}', [128, DC * 512])
              for i in range(2)]
        lo = [sbt(sc, f'lnlo{i}', [128, DC * 512])
              for i in range(2)]
        lhh = [sbt(sc, f'lnhh{i}', [128, DC * 512], BF16) for i in range(2)]
        lll = [sbt(sc, f'lnll{i}', [128, DC * 512], BF16) for i in range(2)]
        mu_s = sbt(sc, 'ln_mu', [1, 512])
        ms_s = sbt(sc, 'ln_ms', [1, 512])
        m2_s = sbt(sc, 'ln_m2', [1, 512])
        va_s = sbt(sc, 'ln_va', [1, 512])
        sd_s = sbt(sc, 'ln_sd', [1, 512])
        rs_s = sbt(sc, 'ln_rs', [1, 512])
        plr = sbt(sc, 'ln_plr', [128, 1])
        p_sum = pst(sc, 'ln_psum', [1, 512])
        p_sq = pst(sc, 'ln_psq', [1, 512])
        p_mu = pst(sc, 'ln_pmu', [128, 512])
        p_rs = pst(sc, 'ln_prs', [128, 512])

        xs_free = [None, None]
        xq_free = [None, None]
        tp_free = [None, None]
        lo_free = [None, None]
        lh_free = [None, None]
        ll_free = [None, None]
        war_psum = war_psq = war_pmu = war_prs = None
        pool_par = [0] * DC
        pool_tok = [None] * DC
        plr_free = None
        if want_pooled:
            z0 = P.v('memset', pooled[:], 0.0)
            pool_tok = [z0] * DC
        for tb in range(TBN):
            b = tb % 2
            ld = P.load(sb3(xs[b][:], 512), fmc(src, tb * 512, 512),
                        deps=[xs_free[b]])
            sq = None
            for dc in range(DC):
                sq = P.act(xq[b][:, dc * 512:(dc + 1) * 512],
                           xs[b][:, dc * 512:(dc + 1) * 512], AF.Square,
                           deps=[ld, xq_free[b]])
            tsum = tsq = None
            for dc in range(DC):
                tsum = P.mm(p_sum[:], ones_col[:],
                            xs[b][:, dc * 512:(dc + 1) * 512],
                            dc == 0, dc == DC - 1,
                            deps=[ld, war_psum], inc=(dc == DC - 1))
            for dc in range(DC):
                tsq = P.mm(p_sq[:], ones_col[:],
                           xq[b][:, dc * 512:(dc + 1) * 512],
                           dc == 0, dc == DC - 1,
                           deps=[sq, war_psq], inc=(dc == DC - 1))
            u1 = P.v('tensor_scalar', mu_s[:], p_sum[:], 1.0 / D, None, OP.mult,
                     deps=[tsum])
            u2 = P.v('tensor_scalar', ms_s[:], p_sq[:], 1.0 / D, EPS,
                     OP.mult, OP.add, deps=[tsq])
            u3 = P.v('tensor_tensor', m2_s[:], mu_s[:], mu_s[:], OP.mult,
                     deps=[u1])
            u4 = P.v('tensor_tensor', va_s[:], ms_s[:], m2_s[:], OP.subtract,
                     deps=[u2, u3])
            u5 = P.act(sd_s[:], va_s[:], AF.Sqrt, deps=[u4])
            u6 = P.v('reciprocal', rs_s[:], sd_s[:], deps=[u5])
            war_psum = u1
            war_psq = u2
            bmu = P.mm(p_mu[:], ones_row[:], mu_s[:], True, True,
                       deps=[u3, war_pmu], inc=True)
            brs = P.mm(p_rs[:], ones_row[:], rs_s[:], True, True,
                       deps=[u6, war_prs], inc=True)
            last_n = None
            for dc in range(DC):
                sl = slice(dc * 512, (dc + 1) * 512)
                n1 = P.v('tensor_tensor', tp[b][:, sl], xs[b][:, sl], p_mu[:],
                         OP.subtract, deps=[bmu, ld, tp_free[b]])
                n2 = P.v('tensor_tensor', lo[b][:, sl], tp[b][:, sl], p_rs[:],
                         OP.mult, deps=[n1, brs, lo_free[b]])
                last_n = n2
                if want_pooled:
                    r1 = P.v('tensor_reduce', plr[:], lo[b][:, sl], AX.X,
                             OP.add, deps=[n2, plr_free])
                    par = pool_par[dc]
                    r2 = P.v('tensor_tensor',
                             pooled[:, dc * 2 + 1 - par:dc * 2 + 2 - par],
                             pooled[:, dc * 2 + par:dc * 2 + par + 1],
                             plr[:], OP.add, deps=[r1, pool_tok[dc]])
                    pool_par[dc] = 1 - par
                    pool_tok[dc] = r2
                    plr_free = r2
            war_pmu = last_n
            war_prs = last_n
            st = P.store(fmc(dst, tb * 512, 512), sb3(lo[b][:], 512),
                         deps=[last_n])
            hsplit = None
            for dc in range(DC):
                sl = slice(dc * 512, (dc + 1) * 512)
                s1 = P.v('tensor_copy', lhh[b][:, sl], lo[b][:, sl],
                         deps=[last_n, lh_free[b]])
                hsplit = P.v('tensor_tensor', lll[b][:, sl], lo[b][:, sl],
                             lhh[b][:, sl], OP.subtract, deps=[s1, ll_free[b]])
            sth = P.store(fmc(lnh_fm, tb * 512, 512), sb3(lhh[b][:], 512),
                          deps=[hsplit])
            stl = P.store(fmc(lnl_fm, tb * 512, 512), sb3(lll[b][:], 512),
                          deps=[hsplit])
            lh_free[b] = sth
            ll_free[b] = stl
            xs_free[b] = last_n
            xq_free[b] = tsq
            tp_free[b] = last_n
            lo_free[b] = hsplit
        sc.close()
        pool_cols = [dc * 2 + pool_par[dc] for dc in range(DC)]
        return pool_cols, pool_tok

    def phase_routing(pool_cols, pool_tok, split_w1a=False):
        sc = contextlib.ExitStack()
        if split_w1a:
            phase_w1a_split(sc)
        wr_sb = sbt(sc, 'rt_wr', [128, 64])
        lg = sbt(sc, 'rt_lg', [1, 8])
        lg2 = sbt(sc, 'rt_lg2', [1, 8])
        mk1 = sbt(sc, 'rt_mk1', [1, 8])
        mk2 = sbt(sc, 'rt_mk2', [1, 8])
        m1 = sbt(sc, 'rt_m1', [1, 1])
        m2 = sbt(sc, 'rt_m2', [1, 1])
        ddf = sbt(sc, 'rt_dd', [1, 1])
        dsc_ = sbt(sc, 'rt_dsc', [1, 1])
        e2v = sbt(sc, 'rt_e2', [1, 1])
        den = sbt(sc, 'rt_den', [1, 1])
        rec = sbt(sc, 'rt_rec', [1, 1])
        w2v = sbt(sc, 'rt_w2v', [1, 1])
        tm8a = sbt(sc, 'rt_tm8a', [1, 8])
        tm8b = sbt(sc, 'rt_tm8b', [1, 8])
        i1 = sbt(sc, 'rt_i1', [1, 1])
        i2 = sbt(sc, 'rt_i2', [1, 1])
        pack = sbt(sc, 'rt_pack', [1, 4])
        stripe = sbt(sc, 'rt_eb', [1, 2])
        ebf = sbt(sc, 'rt_ebf', [2, 1])
        ebu = sbt(sc, 'rt_ebu', [2, 1], U32)
        b1s = sbt(sc, 'rt_b1s', [2, FF])
        b2s = sbt(sc, 'rt_b2s', [2, D])
        b2c = sbt(sc, 'rt_b2c', [128, 16])
        tcol = sbt(sc, 'rt_tcol', [128, 1])
        fst = [sbt(sc, f'rt_fst{i}', [128, 1])
               for i in range(2)]
        base1 = [sbt(sc, f'rt_ba{k}', [128, 1])
                 for k in range(2)]
        base2 = [sbt(sc, f'rt_bb{k}', [128, 1])
                 for k in range(2)]
        p_lg = pst(sc, 'rt_plg', [1, 8])
        p_g = pst(sc, 'rt_pg', [128, 4])
        p_eb = pst(sc, 'rt_peb', [2, 1])
        p_t = [pst(sc, f'rt_pt{i}', [128, 2])
               for i in range(2)]

        wl = P.load(sb3(wr_sb[:], 8), wr.rearrange('(c p) e -> p c e', p=128))
        tl = None
        for dc in range(DC):
            col = pool_cols[dc]
            tl = P.mm(p_lg[:], pooled[:, col:col + 1],
                      wr_sb[:, dc * 8:(dc + 1) * 8], dc == 0, dc == DC - 1,
                      deps=[wl, pool_tok[dc]], inc=(dc == DC - 1))
        a0 = P.v('tensor_copy', lg[:], p_lg[:], deps=[tl])
        a1 = P.v('reduce_max', m1[:], lg[:], AX.X, deps=[a0])
        a2 = P.v('tensor_scalar', mk1[:], lg[:], m1[:], None, OP.is_ge,
                 deps=[a1])
        a3 = P.v('scalar_tensor_tensor', out=lg2[:], in0=mk1[:], scalar=-1e30,
                 in1=lg[:], op0=OP.mult, op1=OP.add, deps=[a2])
        a4 = P.v('reduce_max', m2[:], lg2[:], AX.X, deps=[a3])
        a5 = P.v('tensor_scalar', mk2[:], lg2[:], m2[:], None, OP.is_ge,
                 deps=[a4])
        a6 = P.v('tensor_tensor', ddf[:], m2[:], m1[:], OP.subtract,
                 deps=[a4, a1])
        a7 = P.v('tensor_scalar', dsc_[:], ddf[:], 1.0 / T, None, OP.mult,
                 deps=[a6])
        a8 = P.act(e2v[:], dsc_[:], AF.Exp, deps=[a7])
        a9 = P.v('tensor_scalar', den[:], e2v[:], 1.0, None, OP.add, deps=[a8])
        a10 = P.v('reciprocal', rec[:], den[:], deps=[a9])
        a11 = P.v('tensor_tensor', w2v[:], e2v[:], rec[:], OP.mult,
                  deps=[a8, a10])
        a12 = P.v('tensor_scalar', pack[:, 0:1], rec[:], 0.5, None, OP.mult,
                  deps=[a10])
        a13 = P.v('tensor_scalar', pack[:, 1:2], w2v[:], 0.5, None, OP.mult,
                  deps=[a11])
        a14 = P.v('tensor_tensor', tm8a[:], mk1[:], iota8[:], OP.mult,
                  deps=[a2])
        a15 = P.v('tensor_reduce', i1[:], tm8a[:], AX.X, OP.add, deps=[a14])
        a16 = P.v('tensor_tensor', tm8b[:], mk2[:], iota8[:], OP.mult,
                  deps=[a5])
        a17 = P.v('tensor_reduce', i2[:], tm8b[:], AX.X, OP.add, deps=[a16])
        a18 = P.v('tensor_copy', pack[:, 2:3], i1[:], deps=[a15])
        a19 = P.v('tensor_copy', pack[:, 3:4], i2[:], deps=[a17])
        bg = P.mm(p_g[:], ones_row[:], pack[:], True, True,
                  deps=[a12, a13, a18, a19], inc=True)
        g0 = P.v('tensor_copy', gcol[:], p_g[:], deps=[bg])
        s0 = P.v('tensor_copy', stripe[:, 0:1], i1[:], deps=[a15])
        s1 = P.v('tensor_copy', stripe[:, 1:2], i2[:], deps=[a17])
        te = P.transp(p_eb[:], stripe[:], ident[0:1, 0:1], deps=[s0, s1])
        s2 = P.v('tensor_copy', ebf[:], p_eb[:], deps=[te])
        s3 = P.v('tensor_copy', ebu[:], ebf[:], deps=[s2])
        gb1 = P.gather(b1s[:], b1m, ebu[:], deps=[s3])
        gb2 = P.gather(b2s[:], b2m, ebu[:], deps=[s3])
        ptf = [te, te]
        b1c_tok = None
        for ft in range(32):
            pb = ft % 2
            tr = P.transp(p_t[pb][:], b1s[0:2, ft * 128:(ft + 1) * 128],
                          ident[0:2, 0:2], deps=[gb1, ptf[pb]])
            cp = P.v('tensor_copy', b1cols[:, ft * 2:ft * 2 + 2], p_t[pb][:],
                     deps=[tr])
            ptf[pb] = cp
            b1c_tok = cp
        b2c_tok = None
        for dc in range(DC):
            pb = dc % 2
            tr = P.transp(p_t[pb][:], b2s[0:2, dc * 128:(dc + 1) * 128],
                          ident[0:2, 0:2], deps=[gb2, ptf[pb]])
            cp = P.v('tensor_copy', b2c[:, dc * 2:dc * 2 + 2], p_t[pb][:],
                     deps=[tr])
            ptf[pb] = cp
            b2c_tok = cp
        b2cb_tok = None
        tcol_tok = None
        for dc in range(DC):
            q1 = P.v('tensor_scalar', tcol[:], b2c[:, dc * 2:dc * 2 + 1],
                     gcol[:, 0:1], None, OP.mult, deps=[b2c_tok, g0, tcol_tok])
            q2 = P.v('scalar_tensor_tensor', out=b2comb[:, dc:dc + 1],
                     in0=b2c[:, dc * 2 + 1:dc * 2 + 2], scalar=gcol[:, 1:2],
                     in1=tcol[:], op0=OP.mult, op1=OP.add, deps=[q1])
            tcol_tok = q2
            b2cb_tok = q2
        idx_tok = None
        fs_tok = [g0, g0]
        for k in range(2):
            bb1 = P.v('scalar_tensor_tensor', out=base1[k],
                      in0=gcol[:, 2 + k:3 + k], scalar=4096.0, in1=iota4[:],
                      op0=OP.mult, op1=OP.add, deps=[g0])
            bb2 = P.v('scalar_tensor_tensor', out=base2[k],
                      in0=gcol[:, 2 + k:3 + k], scalar=4096.0, in1=iota128[:],
                      op0=OP.mult, op1=OP.add, deps=[g0])
            for fq in range(4):
                for dc in range(DC):
                    col = k * 32 + fq * 8 + dc
                    fb_ = col % 2
                    f1 = P.v('tensor_scalar', fst[fb_][:], base1[k],
                             float(512 * dc + fq), None, OP.add,
                             deps=[bb1, fs_tok[fb_]])
                    f2 = P.v('tensor_copy', idxw1[:, col:col + 1], fst[fb_][:],
                             deps=[f1])
                    fs_tok[fb_] = f2
                for fb in range(8):
                    col = k * 32 + fq * 8 + fb
                    fb_ = col % 2
                    f1 = P.v('tensor_scalar', fst[fb_][:], base2[k],
                             float(fq * 1024 + fb * 128), None, OP.add,
                             deps=[bb2, fs_tok[fb_]])
                    f2 = P.v('tensor_copy', idxw2[:, col:col + 1], fst[fb_][:],
                             deps=[f1])
                    fs_tok[fb_] = f2
                    idx_tok = f2
        sc.close()
        return dict(gcol=g0, b1c=b1c_tok, b2cb=b2cb_tok, idx=idx_tok)

    def phase_ffn(rt, first_call):
        sc = contextlib.ExitStack()
        w1h = [sbt(sc, f'ff_w1h{i}', [128, DC * 1024], BF16) for i in range(2)]
        w1l = [sbt(sc, f'ff_w1l{i}', [128, DC * 1024], BF16) for i in range(2)]
        w2h = sbt(sc, 'ff_w2h', [128, 8 * 1024], BF16)
        w2l = sbt(sc, 'ff_w2l', [128, 8 * 1024], BF16)
        wtr = sbt(sc, 'ff_wtr', [128, 4 * 1024])
        lnsh = [sbt(sc, f'ff_lnh{i}', [128, DC * 512], BF16) for i in range(2)]
        lnsl = [sbt(sc, f'ff_lnl{i}', [128, DC * 512], BF16) for i in range(2)]
        hbf = sbt(sc, 'ff_h', [128, 8 * 512])
        hh = sbt(sc, 'ff_hh', [128, 8 * 512], BF16)
        hl = sbt(sc, 'ff_hl', [128, 8 * 512], BF16)
        ysb = [sbt(sc, f'ff_ysb{i}', [128, 4 * 512]) for i in range(2)]
        ph = [pst(sc, f'ff_ph{i}', [128, 512]) for i in range(2)]
        py = [pst(sc, f'ff_py{i}', [128, 512]) for i in range(4)]

        w1_tok = [None, None]
        w1_free = [None, None]
        w2_tok = None
        w2_free = None
        wtr_free = None
        lns_free = [None, None, None, None]
        h_tok = [None] * 8
        h_free = None
        hs_tok = [None] * 8
        hs_free = None
        ph_free = [None, None]
        py_free = [None] * 4
        ysb_free = [None, None]
        y_chain = [None] * TBN

        iters = [(k, fq) for k in range(2) for fq in range(4)]

        def gather_split(rows_ap, idx_cols, dsth, dstl, nblk, blkw, dep0):
            nonlocal wtr_free
            sp = None
            half = 4096 // blkw
            for r0 in range(0, nblk, half):
                gts = []
                for j in range(half):
                    blk = r0 + j
                    gts.append(P.gather(
                        wtr[:, j * blkw:(j + 1) * blkw], rows_ap,
                        idx_cols[blk], deps=[rt['idx'], wtr_free, dep0]))
                gt = gts[-1]
                for j in range(half):
                    blk = r0 + j
                    s1 = P.v('tensor_copy',
                             dsth[:, blk * blkw:(blk + 1) * blkw],
                             wtr[:, j * blkw:(j + 1) * blkw], deps=[gt])
                    sp = P.v('tensor_tensor',
                             dstl[:, blk * blkw:(blk + 1) * blkw],
                             wtr[:, j * blkw:(j + 1) * blkw],
                             dsth[:, blk * blkw:(blk + 1) * blkw],
                             OP.subtract, deps=[s1])
                wtr_free = sp
            return sp

        def issue_w1(it_i):
            k, fq = iters[it_i]
            slot = it_i % 2
            idxs = [idxw1[:, k * 32 + fq * 8 + dc:k * 32 + fq * 8 + dc + 1]
                    for dc in range(DC)]
            w1_tok[slot] = gather_split(w1rows, idxs, w1h[slot], w1l[slot],
                                        DC, 1024, w1_free[slot])

        issue_w1(0)
        for it_i, (k, fq) in enumerate(iters):
            slot = it_i % 2
            idxs = [idxw2[:, k * 32 + fq * 8 + fb:k * 32 + fq * 8 + fb + 1]
                    for fb in range(8)]
            w2_tok = gather_split(w2rows, idxs, w2h, w2l, 8, 1024, w2_free)
            if it_i + 1 < len(iters):
                issue_w1(it_i + 1)
            for tb in range(TBN):
                lb = tb % 2
                ldh = P.load(sb3(lnsh[lb][:], 512), fmc(lnh_fm, tb * 512, 512),
                             deps=[lns_free[lb * 2]])
                ldl = P.load(sb3(lnsl[lb][:], 512), fmc(lnl_fm, tb * 512, 512),
                             deps=[lns_free[lb * 2 + 1]])
                for ft in range(8):
                    pb = ft % 2
                    g = None
                    for dc in range(DC):
                        g = mm3(ph[pb][:],
                                w1h[slot][:, dc * 1024 + ft * 128:
                                           dc * 1024 + (ft + 1) * 128],
                                w1l[slot][:, dc * 1024 + ft * 128:
                                           dc * 1024 + (ft + 1) * 128],
                                lnsh[lb][:, dc * 512:(dc + 1) * 512],
                                lnsl[lb][:, dc * 512:(dc + 1) * 512],
                                dc == 0, dc == DC - 1,
                                deps=[w1_tok[slot], ldh, ldl, ph_free[pb]],
                                inc=(dc == DC - 1))
                    sl = slice(ft * 512, (ft + 1) * 512)
                    h_tok[ft] = P.act(hbf[:, sl], ph[pb][:], AF.Gelu,
                                      bias=b1cols[:, (fq * 8 + ft) * 2 + k:
                                                  (fq * 8 + ft) * 2 + k + 1],
                                      deps=[g, rt['b1c'], h_free])
                    s1 = P.v('tensor_copy', hh[:, sl], hbf[:, sl],
                             deps=[h_tok[ft], hs_free])
                    hs_tok[ft] = P.v('tensor_tensor', hl[:, sl], hbf[:, sl],
                                     hh[:, sl], OP.subtract, deps=[s1])
                    ph_free[pb] = h_tok[ft]
                for grp in range(2):
                    gt = None
                    for fb in range(8):
                        for dtl in range(4):
                            dt = grp * 4 + dtl
                            gt = mm3(py[dtl][:],
                                     w2h[:, fb * 1024 + dt * 128:
                                          fb * 1024 + (dt + 1) * 128],
                                     w2l[:, fb * 1024 + dt * 128:
                                          fb * 1024 + (dt + 1) * 128],
                                     hh[:, fb * 512:(fb + 1) * 512],
                                     hl[:, fb * 512:(fb + 1) * 512],
                                     fb == 0, fb == 7,
                                     deps=[hs_tok[fb], w2_tok, py_free[dtl]],
                                     inc=(fb == 7 and dtl == 3))
                    yb = ysb[grp]
                    ev = None
                    for dtl in range(4):
                        ev = P.v('tensor_scalar',
                                 yb[:, dtl * 512:(dtl + 1) * 512], py[dtl][:],
                                 gcol[:, k:k + 1], None, OP.mult,
                                 deps=[gt, rt['gcol'], ysb_free[grp]])
                        py_free[dtl] = ev
                    dst = y_acc[grp * 512:(grp + 1) * 512,
                                tb * 512:(tb + 1) * 512].rearrange(
                                    '(d p) w -> d p w', p=128).transpose([1, 0, 2])
                    srcv = yb[:].rearrange('p (d w) -> p d w', w=512)
                    y_chain[tb] = P.gstore(dst, srcv,
                                           deps=[ev, y_chain[tb]],
                                           accum=(it_i != 0))
                    ysb_free[grp] = y_chain[tb]
                h_free = gt
                hs_free = gt
                lns_free[lb * 2] = gt
                lns_free[lb * 2 + 1] = gt
            w1_free[slot] = P.snap_tensor_token()
            w2_free = w1_free[slot]

        sc.close()
        P.set_base(P.snap())
        sc = contextlib.ExitStack()
        ysb2 = [sbt(sc, f'fc_y{i}', [128, DC * 512])
                for i in range(2)]
        xsb2 = [sbt(sc, f'fc_x{i}', [128, DC * 512])
                for i in range(2)]
        tsb2 = [sbt(sc, f'fc_t{i}', [128, DC * 512])
                for i in range(2)]
        y2f = [None, None]
        x2f = [None, None]
        t2f = [None, None]
        for tb in range(TBN):
            b = tb % 2
            ly = P.load(sb3(ysb2[b][:], 512), fmc(y_acc, tb * 512, 512),
                        deps=[y2f[b], y_chain[tb]])
            lx = P.load(sb3(xsb2[b][:], 512), fmc(x_fm, tb * 512, 512),
                        deps=[x2f[b]])
            last = None
            for dc in range(DC):
                sl = slice(dc * 512, (dc + 1) * 512)
                q1 = P.v('tensor_scalar', tsb2[b][:, sl], ysb2[b][:, sl],
                         b2comb[:, dc:dc + 1], None, OP.add,
                         deps=[ly, rt['b2cb'], t2f[b]])
                q2 = P.v('tensor_tensor', ysb2[b][:, sl], tsb2[b][:, sl],
                         xsb2[b][:, sl], OP.add, deps=[q1, lx])
                last = q2
            st = P.store(fmc(x_fm, tb * 512, 512), sb3(ysb2[b][:], 512),
                         deps=[last])
            y2f[b] = st
            x2f[b] = last
            t2f[b] = last
        sc.close()

    def first_call_iter(it_i):
        return it_i == 0

    def snap_tensor_token():
        c = P.cnt['st']
        return ('st', c) if c > 0 else None
    P.snap_tensor_token = snap_tensor_token

    def phase_w1a_split(sc):
        wf = [sbt(sc, f'ws_f{i}', [128, DC * 512]) for i in range(2)]
        wh = [sbt(sc, f'ws_h{i}', [128, DC * 512], BF16) for i in range(2)]
        wl = [sbt(sc, f'ws_l{i}', [128, DC * 512], BF16) for i in range(2)]
        f_free = [None, None]
        h_free = [None, None]
        for cc in range(8):
            b = cc % 2
            ld = P.load(sb3(wf[b][:], 512),
                        w1a[:, cc * 512:(cc + 1) * 512].rearrange(
                            '(c p) w -> p c w', p=128), deps=[f_free[b]])
            sp = None
            for dc in range(DC):
                sl = slice(dc * 512, (dc + 1) * 512)
                s1 = P.v('tensor_copy', wh[b][:, sl], wf[b][:, sl],
                         deps=[ld, h_free[b]])
                sp = P.v('tensor_tensor', wl[b][:, sl], wf[b][:, sl],
                         wh[b][:, sl], OP.subtract, deps=[s1])
            sth = P.store(w1ah[:, cc * 512:(cc + 1) * 512].rearrange(
                              '(c p) w -> p c w', p=128), sb3(wh[b][:], 512),
                          deps=[sp])
            stl = P.store(w1al[:, cc * 512:(cc + 1) * 512].rearrange(
                              '(c p) w -> p c w', p=128), sb3(wl[b][:], 512),
                          deps=[sp])
            f_free[b] = sp
            h_free[b] = stl

    def phase_xproj():
        sc = contextlib.ExitStack()
        lnsh = [sbt(sc, f'xp_lnh{i}', [128, DC * 512], BF16) for i in range(2)]
        lnsl = [sbt(sc, f'xp_lnl{i}', [128, DC * 512], BF16) for i in range(2)]
        wcth = [sbt(sc, f'xp_wcth{i}', [128, DC * 128], BF16) for i in range(2)]
        wctl = [sbt(sc, f'xp_wctl{i}', [128, DC * 128], BF16) for i in range(2)]
        wuvh = [sbt(sc, f'xp_wuvh{i}', [128, DC * 512], BF16) for i in range(2)]
        wuvl = [sbt(sc, f'xp_wuvl{i}', [128, DC * 512], BF16) for i in range(2)]
        qst = [sbt(sc, f'xp_qst{i}', [128, 512]) for i in range(2)]
        qsh = [sbt(sc, f'xp_qsh{i}', [128, 512], BF16) for i in range(2)]
        qsl = [sbt(sc, f'xp_qsl{i}', [128, 512], BF16) for i in range(2)]
        ust = [sbt(sc, f'xp_ust{i}', [128, 512]) for i in range(2)]
        vst = [sbt(sc, f'xp_vst{i}', [128, 512]) for i in range(2)]
        b1a_sb = sbt(sc, 'xp_b1a', [1, FF])
        b1f = sbt(sc, 'xp_b1f', [128, 24])
        pp = [pst(sc, f'xp_pp{i}', [128, 512]) for i in range(2)]
        p_t = [pst(sc, f'xp_pt{i}', [128, 1]) for i in range(2)]

        def colof(j):
            return j * 128 if j < 8 else 2048 + (j - 8) * 128

        lb1 = P.load(b1a_sb[:], b1a)
        ptf = [None, None]
        bf_tok = None
        for j in range(24):
            pb = j % 2
            c0 = colof(j)
            tr = P.transp(p_t[pb][:], b1a_sb[0:1, c0:c0 + 128],
                          ident[0:1, 0:1], deps=[lb1, ptf[pb]])
            cp = P.v('tensor_copy', b1f[:, j:j + 1], p_t[pb][:], deps=[tr])
            ptf[pb] = cp
            bf_tok = cp

        lns_free = [None, None]
        wct_tok = [None, None]
        wct_free = [None, None]
        wuv_tok = [None, None]
        wuv_free = [None, None]
        pp_free = [None, None]
        qst_free = [None, None]
        qs_free = [None, None]
        ust_free = [None, None]
        vst_free = [None, None]
        g = None
        for tb in range(TBN):
            lb = tb % 2
            ldh = P.load(sb3(lnsh[lb][:], 512), fmc(lnh_fm, tb * 512, 512),
                         deps=[lns_free[lb]])
            ldl = P.load(sb3(lnsl[lb][:], 512), fmc(lnl_fm, tb * 512, 512),
                         deps=[lns_free[lb]])
            for j in range(24):
                wb = j % 2
                c0 = colof(j)
                wct_tok[wb] = P.load(
                    sb3(wcth[wb][:], 128),
                    w1ah[:, c0:c0 + 128].rearrange('(c p) w -> p c w', p=128),
                    deps=[wct_free[wb]])
                wct_tok2 = P.load(
                    sb3(wctl[wb][:], 128),
                    w1al[:, c0:c0 + 128].rearrange('(c p) w -> p c w', p=128),
                    deps=[wct_free[wb]])
                g = None
                for dc in range(DC):
                    g = mm3(pp[wb][:],
                            wcth[wb][:, dc * 128:(dc + 1) * 128],
                            wctl[wb][:, dc * 128:(dc + 1) * 128],
                            lnsh[lb][:, dc * 512:(dc + 1) * 512],
                            lnsl[lb][:, dc * 512:(dc + 1) * 512],
                            dc == 0, dc == DC - 1,
                            deps=[wct_tok[wb], wct_tok2, ldh, ldl,
                                  pp_free[wb]],
                            inc=(dc == DC - 1))
                sl = P.act(qst[wb][:], pp[wb][:], AF.Silu,
                           bias=b1f[:, j:j + 1], deps=[g, bf_tok, qst_free[wb]])
                if j < 8:
                    st = P.store(proj_fm[j * 128:(j + 1) * 128,
                                         tb * 512:(tb + 1) * 512], qst[wb][:],
                                 deps=[sl])
                    qst_free[wb] = st
                else:
                    r0 = (j - 8) * 128
                    s1 = P.v('tensor_copy', qsh[wb][:], qst[wb][:],
                             deps=[sl, qs_free[wb]])
                    s2 = P.v('tensor_tensor', qsl[wb][:], qst[wb][:],
                             qsh[wb][:], OP.subtract, deps=[s1])
                    sth = P.store(projh[r0:r0 + 128,
                                        tb * 512:(tb + 1) * 512], qsh[wb][:],
                                  deps=[s2])
                    stl = P.store(projl[r0:r0 + 128,
                                        tb * 512:(tb + 1) * 512], qsl[wb][:],
                                  deps=[s2])
                    qst_free[wb] = s2
                    qs_free[wb] = stl
                pp_free[wb] = sl
                wct_free[wb] = g
            for ci, cc in enumerate((2, 3)):
                wb = ci % 2
                wuv_tok[wb] = P.load(
                    sb3(wuvh[wb][:], 512),
                    w1ah[:, cc * 512:(cc + 1) * 512].rearrange(
                        '(c p) w -> p c w', p=128),
                    deps=[wuv_free[wb]])
                wuv_tok2 = P.load(
                    sb3(wuvl[wb][:], 512),
                    w1al[:, cc * 512:(cc + 1) * 512].rearrange(
                        '(c p) w -> p c w', p=128),
                    deps=[wuv_free[wb]])
                for t4 in range(4):
                    pb = (ci * 4 + t4) % 2
                    g = None
                    for dc in range(DC):
                        g = mm3(pp[pb][:],
                                lnsh[lb][:, dc * 512 + t4 * 128:
                                        dc * 512 + (t4 + 1) * 128],
                                lnsl[lb][:, dc * 512 + t4 * 128:
                                        dc * 512 + (t4 + 1) * 128],
                                wuvh[wb][:, dc * 512:(dc + 1) * 512],
                                wuvl[wb][:, dc * 512:(dc + 1) * 512],
                                dc == 0, False,
                                deps=[wuv_tok[wb], wuv_tok2, ldh, ldl,
                                      pp_free[pb]])
                    g = P.mm(pp[pb][:], ones_row[:],
                             b1a_sb[0:1, cc * 512:(cc + 1) * 512],
                             False, True, deps=[lb1], inc=True)
                    row0 = tb * 512 + t4 * 128
                    sl = P.act(ust[pb][:], pp[pb][:], AF.Silu,
                               deps=[g, ust_free[pb]])
                    sc_ = P.v('tensor_scalar', vst[pb][:], ust[pb][:],
                              1.0 / T, None, OP.mult,
                              deps=[sl, vst_free[pb]])
                    st = P.store(v_tm[row0:row0 + 128,
                                      (cc - 2) * 512:(cc - 1) * 512],
                                 vst[pb][:], deps=[sc_])
                    ust_free[pb] = sc_
                    vst_free[pb] = st
                    pp_free[pb] = sl
                wuv_free[wb] = g
            lns_free[lb] = g
        sc.close()

    def phase_attn_core():
        sc = contextlib.ExitStack()
        qhh = [sbt(sc, f'at_qh{i}', [128, T], BF16) for i in range(2)]
        qhl = [sbt(sc, f'at_ql{i}', [128, T], BF16) for i in range(2)]
        khh = [sbt(sc, f'at_kh{i}', [128, T], BF16) for i in range(2)]
        khl = [sbt(sc, f'at_kl{i}', [128, T], BF16) for i in range(2)]
        vh = [sbt(sc, f'at_v{i}', [128, NTT * 64]) for i in range(2)]
        uh = [sbt(sc, f'at_u{i}', [64, T]) for i in range(2)]
        att = [sbt(sc, f'at_a{i}', [128, 512]) for i in range(2)]
        avf = [sbt(sc, f'at_avf{i}', [64, T]) for i in range(2)]
        ps = [pst(sc, f'at_ps{i}', [128, 512]) for i in range(2)]
        pav = [pst(sc, f'at_pav{i}', [64, 512]) for i in range(2)]

        zp = None
        for i in range(2):
            for tns in (qhh, qhl, khh, khl):
                zp = P.v('memset', tns[i][64:128, :], 0.0, deps=[zp])
        zp_tok = zp

        set_free = [None, None]
        ps_free = [None, None]
        pav_free = [None, None]
        att_free = [None, None]
        avf_free = [None, None]
        for h in range(NH):
            hb = h % 2
            lq1 = P.load(qhh[hb][0:64, :], projh[h * 64:(h + 1) * 64, :],
                         deps=[set_free[hb], zp_tok])
            lq2 = P.load(qhl[hb][0:64, :], projl[h * 64:(h + 1) * 64, :],
                         deps=[set_free[hb], zp_tok])
            lk1 = P.load(khh[hb][0:64, :], projh[D + h * 64:D + (h + 1) * 64, :],
                         deps=[set_free[hb], zp_tok])
            lk2 = P.load(khl[hb][0:64, :], projl[D + h * 64:D + (h + 1) * 64, :],
                         deps=[set_free[hb], zp_tok])
            lv = P.load(vh[hb][:].rearrange('p (t w) -> p t w', w=64),
                        v_tm[:, h * 64:(h + 1) * 64].rearrange(
                            '(t p) w -> p t w', p=128),
                        deps=[set_free[hb]])
            lu = P.load(uh[hb][:], proj_fm[h * 64:(h + 1) * 64, :],
                        deps=[set_free[hb]])
            mu_t = None
            for qp in range(4):
                qcs = (2 * qp, 2 * qp + 1)
                gav = [None, None]
                sl_tok = [None, None]
                for kt in range(NTT):
                    for s in range(2):
                        qc = qcs[s]
                        sm = mm3(ps[s][:],
                                 khh[hb][:, kt * 128:(kt + 1) * 128],
                                 khl[hb][:, kt * 128:(kt + 1) * 128],
                                 qhh[hb][:, qc * 512:(qc + 1) * 512],
                                 qhl[hb][:, qc * 512:(qc + 1) * 512],
                                 True, True,
                                 deps=[lq1, lq2, lk1, lk2, ps_free[s]],
                                 inc=True)
                        sl_tok[s] = P.act(att[s][:], ps[s][:], AF.Silu,
                                          deps=[sm])
                        ps_free[s] = sl_tok[s]
                    for s in range(2):
                        qc = qcs[s]
                        gav[s] = P.mm(pav[s][:],
                                      vh[hb][:, kt * 64:(kt + 1) * 64],
                                      att[s][:],
                                      kt == 0, kt == NTT - 1,
                                      deps=[sl_tok[s], lv] +
                                           ([pav_free[s]] if kt == 0 else []),
                                      inc=(kt == NTT - 1))
                for s in range(2):
                    qc = qcs[s]
                    mu_t = P.v('tensor_tensor',
                               avf[hb][:, qc * 512:(qc + 1) * 512],
                               pav[s][:],
                               uh[hb][:, qc * 512:(qc + 1) * 512], OP.mult,
                               deps=[gav[s], lu, avf_free[hb]])
                    pav_free[s] = mu_t
            stv = P.store(avu_fm[h * 64:(h + 1) * 64, :], avf[hb][:],
                          deps=[mu_t])
            avf_free[hb] = stv
            set_free[hb] = mu_t

        sc.close()
        return None

    def phase_outproj(stats_tok, dump_z=False):
        sc = contextlib.ExitStack()
        w2gh = sbt(sc, 'op_w2gh', [128, DC * 1024], BF16)
        w2gl = sbt(sc, 'op_w2gl', [128, DC * 1024], BF16)
        wgs = sbt(sc, 'op_wgs', [128, 1024])
        wch = [sbt(sc, f'op_wch{i}', [128, 1024]) for i in range(2)]
        gcols = sbt(sc, 'op_gc', [128, 8])
        bcols = sbt(sc, 'op_bc', [128, 8])
        b2cols = sbt(sc, 'op_b2c', [128, 8])
        biasc = sbt(sc, 'op_bias', [128, 8])
        avs = [sbt(sc, f'op_av{i}', [128, DC * 512]) for i in range(2)]
        avq = [sbt(sc, 'op_aq0', [128, DC * 512])] * 2
        tpz = avq
        zfm = [sbt(sc, f'op_zfm{i}', [128, DC * 512]) for i in range(2)]
        zfh = [sbt(sc, 'op_zfh0', [128, DC * 512], BF16)] * 2
        zfl = [sbt(sc, 'op_zfl0', [128, DC * 512], BF16)] * 2
        xch = [sbt(sc, 'op_xch0', [128, DC * 512])] * 2
        xnb = [sbt(sc, 'op_xnb0', [128, DC * 512])] * 2
        yst = [sbt(sc, 'op_yst0', [128, DC * 512])] * 2
        mu_s = sbt(sc, 'op_mu', [1, 512])
        ms_s = sbt(sc, 'op_ms', [1, 512])
        m2_s = sbt(sc, 'op_m2', [1, 512])
        va_s = sbt(sc, 'op_va', [1, 512])
        sd_s = sbt(sc, 'op_sd', [1, 512])
        rs_s = sbt(sc, 'op_rs', [1, 512])
        p_b = pst(sc, 'op_pb', [128, 8])
        p_y = [pst(sc, f'op_py{i}', [128, 512]) for i in range(2)]
        p_sum = pst(sc, 'op_psum', [1, 512])
        p_sq = pst(sc, 'op_psq', [1, 512])
        p_mu = pst(sc, 'op_pmu', [128, 512])
        p_rs = pst(sc, 'op_prs', [128, 512])

        lg_ = P.load(gcols[:], gam[0].rearrange('(c p) -> p c', p=128),
                     slow=True)
        lb_ = P.load(bcols[:], bet[0].rearrange('(c p) -> p c', p=128),
                     slow=True)
        lb2 = P.load(b2cols[:], b2a[0].rearrange('(c p) -> p c', p=128),
                     slow=True)
        wch_tok = [None, None]
        wch_free = [None, None]
        bias_mm = None
        w2g_tok = None
        for dc in range(DC):
            wb = dc % 2
            wch_tok[wb] = P.load(wch[wb][:], w2a[dc * 128:(dc + 1) * 128, :],
                                 deps=(wch_free[wb] or []))
            for dt in range(DC):
                bias_mm = P.mm(p_b[:, dt:dt + 1],
                               wch[wb][:, dt * 128:(dt + 1) * 128],
                               bcols[:, dc:dc + 1],
                               dc == 0 and dt == 0,
                               dc == DC - 1 and dt == DC - 1,
                               deps=[wch_tok[wb], lb_],
                               inc=(dt == DC - 1))
            wg1 = P.v('tensor_scalar', wgs[:], wch[wb][:],
                      gcols[:, dc:dc + 1], None, OP.mult,
                      deps=[wch_tok[wb], lg_, w2g_tok])
            wg2 = P.v('tensor_copy', w2gh[:, dc * 1024:(dc + 1) * 1024],
                      wgs[:], deps=[wg1])
            w2g_tok = P.v('tensor_tensor', w2gl[:, dc * 1024:(dc + 1) * 1024],
                          wgs[:], w2gh[:, dc * 1024:(dc + 1) * 1024],
                          OP.subtract, deps=[wg2])
            wch_free[wb] = [w2g_tok, bias_mm]
        bias_tok = P.v('tensor_tensor', biasc[:], p_b[:], b2cols[:], OP.add,
                       deps=[bias_mm, lb2])

        av_free = [None, None]
        zs_free = SharedTok()
        aq_free = SharedTok()
        tp_free = SharedTok()
        zfm_free = [None, None]
        py_free = [None, None]
        xch_free = SharedTok()
        xnb_free = SharedTok()
        yst_free = SharedTok()
        war_psum = war_psq = war_pmu = war_prs = None
        for tb in range(TBN):
            zb = tb % 2
            la = P.load(sb3(avs[zb][:], 512), fmc(avu_fm, tb * 512, 512),
                        deps=[av_free[zb]])
            lx = P.load(sb3(xch[zb][:], 512), fmc(x_fm, tb * 512, 512),
                        deps=[xch_free[zb]])
            sq = None
            for dc in range(DC):
                sq = P.act(avq[zb][:, dc * 512:(dc + 1) * 512],
                           avs[zb][:, dc * 512:(dc + 1) * 512], AF.Square,
                           deps=[la, aq_free[zb]])
            tsum = tsq = None
            for dc in range(DC):
                tsum = P.mm(p_sum[:], ones_col[:],
                            avs[zb][:, dc * 512:(dc + 1) * 512],
                            dc == 0, dc == DC - 1,
                            deps=[la, war_psum], inc=(dc == DC - 1))
            for dc in range(DC):
                tsq = P.mm(p_sq[:], ones_col[:],
                           avq[zb][:, dc * 512:(dc + 1) * 512],
                           dc == 0, dc == DC - 1,
                           deps=[sq, war_psq], inc=(dc == DC - 1))
            u1 = P.v('tensor_scalar', mu_s[:], p_sum[:], 1.0 / D, None,
                     OP.mult, deps=[tsum])
            u2 = P.v('tensor_scalar', ms_s[:], p_sq[:], 1.0 / D, EPS,
                     OP.mult, OP.add, deps=[tsq])
            u3 = P.v('tensor_tensor', m2_s[:], mu_s[:], mu_s[:], OP.mult,
                     deps=[u1])
            u4 = P.v('tensor_tensor', va_s[:], ms_s[:], m2_s[:], OP.subtract,
                     deps=[u2, u3])
            u5 = P.act(sd_s[:], va_s[:], AF.Sqrt, deps=[u4])
            u6 = P.v('reciprocal', rs_s[:], sd_s[:], deps=[u5])
            war_psum = u1
            war_psq = u2
            bmu = P.mm(p_mu[:], ones_row[:], mu_s[:], True, True,
                       deps=[u3, war_pmu], inc=True)
            brs = P.mm(p_rs[:], ones_row[:], rs_s[:], True, True,
                       deps=[u6, war_prs], inc=True)
            last_n = None
            for dc in range(DC):
                sl = slice(dc * 512, (dc + 1) * 512)
                n1 = P.v('tensor_tensor', tpz[zb][:, sl], avs[zb][:, sl],
                         p_mu[:], OP.subtract,
                         deps=[bmu, la, tsq, tp_free[zb]])
                n2 = P.v('tensor_tensor', zfm[zb][:, sl], tpz[zb][:, sl],
                         p_rs[:], OP.mult, deps=[n1, brs, zfm_free[zb]])
                z1 = P.v('tensor_copy', zfh[zb][:, sl], zfm[zb][:, sl],
                         deps=[n2, zs_free[0]])
                last_n = P.v('tensor_tensor', zfl[zb][:, sl], zfm[zb][:, sl],
                             zfh[zb][:, sl], OP.subtract,
                             deps=[z1, zs_free[0]])
            war_pmu = last_n
            war_prs = last_n
            if dump_z:
                P.store(fmc(y_acc, tb * 512, 512), sb3(zfm[zb][:], 512),
                        deps=[last_n])
            last_q = None
            for dt in range(DC):
                pb = dt % 2
                g = None
                for dc in range(DC):
                    g = mm3(p_y[pb][:],
                            w2gh[:, dc * 1024 + dt * 128:
                                 dc * 1024 + (dt + 1) * 128],
                            w2gl[:, dc * 1024 + dt * 128:
                                 dc * 1024 + (dt + 1) * 128],
                            zfh[zb][:, dc * 512:(dc + 1) * 512],
                            zfl[zb][:, dc * 512:(dc + 1) * 512],
                            dc == 0, dc == DC - 1,
                            deps=[w2g_tok, last_n, py_free[pb]],
                            inc=(dc == DC - 1))
                sl = slice(dt * 512, (dt + 1) * 512)
                q1 = P.v('tensor_scalar', yst[zb][:, sl], p_y[pb][:],
                         biasc[:, dt:dt + 1], None, OP.add,
                         deps=[g, bias_tok, yst_free[zb]])
                q2 = P.v('tensor_tensor', xnb[zb][:, sl], yst[zb][:, sl],
                         xch[zb][:, sl], OP.add, deps=[q1, lx, xnb_free[zb]])
                py_free[pb] = q1
                last_q = q2
            st = P.store(fmc(x_fm, tb * 512, 512), sb3(xnb[zb][:], 512),
                         deps=[last_q])
            av_free[zb] = last_n
            aq_free[zb] = last_n
            tp_free[zb] = last_n
            zfm_free[zb] = last_n
            zs_free[0] = last_q
            xch_free[zb] = last_q
            xnb_free[zb] = st
            yst_free[zb] = last_q
        sc.close()

    def phase_out_transpose(src):
        sc = contextlib.ExitStack()
        chunk = [sbt(sc, f'pf_ch{i}', [128, T])
                 for i in range(DC)]
        ostg = [sbt(sc, f'pf_os{i}', [128, D])
                for i in range(2)]
        ptr = [pst(sc, f'pf_ps{i}', [128, 128])
               for i in range(2)]
        lds = [P.load(chunk[dc][:], src[dc * 128:(dc + 1) * 128, :])
               for dc in range(DC)]
        ptr_free = [None, None]
        ostg_free = [None, None]
        i = 0
        for tt in range(NTT):
            ob = tt % 2
            last_cp = None
            for dc in range(DC):
                pb = i % 2
                tr = P.transp(ptr[pb][:],
                              chunk[dc][:, tt * 128:(tt + 1) * 128],
                              ident[:], deps=[lds[dc], ptr_free[pb]])
                cp = P.v('tensor_copy', ostg[ob][:, dc * 128:(dc + 1) * 128],
                         ptr[pb][:], deps=[tr, ostg_free[ob]])
                ptr_free[pb] = cp
                last_cp = cp
                i += 1
            st = P.store(out[tt * 128:(tt + 1) * 128, :], ostg[ob][:],
                         deps=[last_cp])
            ostg_free[ob] = st
        sc.close()

    def phase_dump_bf16_fm(srcd):
        sc = contextlib.ExitStack()
        bb = [sbt(sc, f'db_b{i}', [128, DC * 512], BF16) for i in range(2)]
        ff = [sbt(sc, f'db_f{i}', [128, DC * 512]) for i in range(2)]
        bfree = [None, None]
        ffree = [None, None]
        for tb in range(TBN):
            b = tb % 2
            ld = P.load(sb3(bb[b][:], 512), fmc(srcd, tb * 512, 512),
                        deps=[bfree[b]])
            cp = P.v('tensor_copy', ff[b][:], bb[b][:], deps=[ld, ffree[b]])
            st = P.store(fmc(y_acc, tb * 512, 512), sb3(ff[b][:], 512),
                         deps=[cp])
            bfree[b] = cp
            ffree[b] = st
        sc.close()

    def phase_copy_tm(srcd, col0):
        sc = contextlib.ExitStack()
        stg = [sbt(sc, f'cp_st{i}', [128, D]) for i in range(2)]
        free = [None, None]
        for tt in range(NTT):
            b = tt % 2
            ld = P.load(stg[b][:], srcd[tt * 128:(tt + 1) * 128, col0:col0 + D],
                        deps=[free[b]])
            st = P.store(out[tt * 128:(tt + 1) * 128, :], stg[b][:], deps=[ld])
            free[b] = st
        sc.close()

    phase_in_transpose()
    P.set_base(P.snap())
    if mode == 'transpose':
        phase_out_transpose(x_fm)
    else:
        pc, pt = phase_ln(x_fm, ln_fm, True)
        P.set_base(P.snap())
        if mode == 'ln':
            phase_out_transpose(ln_fm)
        else:
            rt = phase_routing(pc, pt, split_w1a=True)
            P.set_base(P.snap())
            if mode in ('w1ahdump', 'w1aldump'):
                phase_dump_bf16_fm(w1ah if mode == 'w1ahdump' else w1al)
                P.set_base(P.snap())
                phase_out_transpose(y_acc)
                P.set_base([])
                P._op('sync', None, False, P.snap(), lambda h: h.nop())
                with nc.Block() as block:
                    P.emit(block, sems)
                es.close()
                return nc
            phase_ffn(rt, True)
            P.set_base(P.snap())
            if mode == 'ffn1':
                phase_out_transpose(x_fm)
            else:
                phase_ln(x_fm, ln_fm, False)
                P.set_base(P.snap())
                phase_xproj()
                P.set_base(P.snap())
                if mode in ('qpart', 'kpart', 'upart', 'vpart'):
                    if mode == 'qpart':
                        phase_out_transpose(proj_fm[D:2 * D, :])
                    elif mode == 'kpart':
                        phase_out_transpose(proj_fm[2 * D:3 * D, :])
                    elif mode == 'upart':
                        phase_out_transpose(proj_fm[0:D, :])
                    else:
                        phase_copy_tm(v_tm, 0)
                    P.set_base([])
                    P._op('sync', None, False, P.snap(), lambda h: h.nop())
                    with nc.Block() as block:
                        P.emit(block, sems)
                    es.close()
                    return nc
                stats_tok = phase_attn_core()
                P.set_base(P.snap())
                if mode == 'avu':
                    phase_out_transpose(avu_fm)
                    P.set_base([])
                    P._op('sync', None, False, P.snap(), lambda h: h.nop())
                    with nc.Block() as block:
                        P.emit(block, sems)
                    es.close()
                    return nc
                phase_outproj(stats_tok, dump_z=(mode == 'zdump'))
                P.set_base(P.snap())
                if mode == 'zdump':
                    phase_out_transpose(y_acc)
                elif mode == 'attn':
                    phase_out_transpose(x_fm)
                else:
                    pc, pt = phase_ln(x_fm, ln_fm, True)
                    P.set_base(P.snap())
                    rt = phase_routing(pc, pt)
                    P.set_base(P.snap())
                    phase_ffn(rt, True)
                    P.set_base(P.snap())
                    phase_out_transpose(x_fm)

    P.set_base([])
    P._op('sync', None, False, P.snap(), lambda h: h.nop())

    with nc.Block() as block:
        P.emit(block, sems)
    es.close()
    return nc


_NC_CACHE = {}


def _get_program(mode):
    if mode not in _NC_CACHE:
        _NC_CACHE[mode] = build_program(mode)
    return _NC_CACHE[mode]


def kernel(L0_attn_x, L0_attn_W1, L0_attn_b1, L0_attn_W2, L0_attn_b2,
           L0_attn_gamma, L0_attn_beta, L0_ffn_Wr, L0_ffn_W1, L0_ffn_b1,
           L0_ffn_W2, L0_ffn_b2, mode='full'):
    f = lambda a: np.ascontiguousarray(np.asarray(a, dtype=np.float32))
    x = f(L0_attn_x)
    shared = dict(
        w1a=f(L0_attn_W1), b1a=f(L0_attn_b1).reshape(1, FF),
        w2a=f(L0_attn_W2), b2a=f(L0_attn_b2).reshape(1, D),
        gam=f(L0_attn_gamma).reshape(1, D), bet=f(L0_attn_beta).reshape(1, D),
        wr=f(L0_ffn_Wr), w1m=f(L0_ffn_W1), b1m=f(L0_ffn_b1),
        w2m=f(L0_ffn_W2), b2m=f(L0_ffn_b2))
    in_maps = [dict(x=np.ascontiguousarray(x[b]), **shared)
               for b in range(N_CORES)]
    nc = _get_program(mode)
    res = run_bass_kernel_spmd(nc, in_maps, core_ids=list(range(N_CORES)))
    return np.stack([res.results[b]['out'] for b in range(N_CORES)], axis=0)


# version 06
# speedup vs baseline: 1.8564x; 1.1165x over previous
import contextlib
import numpy as np

import concourse.bass as bass
import concourse.mybir as mybir
from concourse.bass_utils import run_bass_kernel_spmd

F32 = mybir.dt.float32
BF16 = mybir.dt.bfloat16
U32 = mybir.dt.uint32
AF = mybir.ActivationFunctionType
OP = mybir.AluOpType
AX = mybir.AxisListType

D, T, FF, NH, DH, E = 1024, 4096, 4096, 16, 64, 8
DC = D // 128
TBN = T // 512
NTT = T // 128
EPS = 1e-5
N_CORES = 8

ENGINES = ('sync', 'gpsimd', 'tensor', 'scalar', 'vector')


class SharedTok:

    def __init__(self):
        self.t = None

    def __getitem__(self, i):
        return self.t

    def __setitem__(self, i, v):
        self.t = v


class Prog:

    NPOOL = 6

    def __init__(self):
        self.streams = {e: [] for e in ENGINES}
        self.cnt = dict(st=0, ss=0, sv=0, sg=0)
        for q in ('dl', 'ds', 'dg'):
            for i in range(self.NPOOL):
                self.cnt[f'{q}{i}'] = 0
        self.rr = dict(dl=0, ds=0, dg=0)
        self.base = {}

    def _dma(self, eng, q, deps, fn):
        i = self.rr[q]
        self.rr[q] = (i + 1) % self.NPOOL
        sem = f'{q}{i}'
        prev = self.cnt[sem]
        dd = list(deps)
        if prev:
            dd.append((sem, prev))
        return self._op(eng, sem, True, dd, fn)

    def set_base(self, toks):
        self.base = {}
        for t in toks:
            if t is None:
                continue
            s, c = t
            if c > self.base.get(s, 0):
                self.base[s] = c

    def _op(self, eng, sem, dma, deps, fn):
        tok = None
        if sem is not None:
            self.cnt[sem] += 16 if dma else 1
            tok = (sem, self.cnt[sem])
        dd = dict(self.base)
        for d in deps:
            if d is None:
                continue
            s, c = d
            if c > dd.get(s, 0):
                dd[s] = c
        self.streams[eng].append((fn, dd, sem, 16 if dma else 1))
        return tok

    def load(self, dst, src, deps=(), slow=False):
        if slow:
            return self._dma('sync', 'dl', deps,
                             lambda h, a=dst, b=src: h.dma_start(
                                 out=a, in_=b, allow_slow_non_contiguous=True))
        return self._dma('sync', 'dl', deps,
                         lambda h, a=dst, b=src: h.dma_start(out=a, in_=b))

    def store(self, dst, src, deps=()):
        return self._dma('scalar', 'ds', deps,
                         lambda h, a=dst, b=src: h.dma_start(out=a, in_=b))

    def gstore(self, dst, src, deps=(), accum=False):
        if accum:
            return self._dma('gpsimd', 'dg', deps,
                             lambda h, a=dst, b=src:
                                 h.dma_start(out=a, in_=b, accum_op=OP.add))
        return self._dma('gpsimd', 'dg', deps,
                         lambda h, a=dst, b=src: h.dma_start(out=a, in_=b))

    def gather(self, dst, src_rows, idx, deps=()):
        return self._dma('gpsimd', 'dg', deps,
                         lambda h, a=dst, b=src_rows, i=idx: h.indirect_dma_start(
                             out=a, out_offset=None, in_=b,
                             in_offset=bass.IndirectOffsetOnAxis(ap=i, axis=0)))

    def mm(self, out, stat, mov, start, stop, deps=(), inc=False):
        return self._op('tensor', 'st' if inc else None, False, deps,
                        lambda h, o=out, s=stat, m=mov, a=start, b=stop:
                            h.matmul(o, s, m, start=a, stop=b))

    def transp(self, out, in_, ident, deps=(), inc=True):
        return self._op('tensor', 'st' if inc else None, False, deps,
                        lambda h, o=out, i=in_, d=ident: h.transpose(o, i, d))

    def act(self, out, in_, func, deps=(), bias=None, scale=1.0):
        if bias is None:
            return self._op('scalar', 'ss', False, deps,
                            lambda h, o=out, i=in_, f=func, sc=scale:
                                h.activation(o, i, f, scale=sc))
        return self._op('scalar', 'ss', False, deps,
                        lambda h, o=out, i=in_, f=func, b=bias, sc=scale:
                            h.activation(o, i, f, bias=b, scale=sc))

    def v(self, name, *args, deps=(), **kw):
        return self._op('vector', 'sv', False, deps,
                        lambda h, n=name, a=args, k=kw: getattr(h, n)(*a, **k))

    def g(self, name, *args, deps=(), **kw):
        return self._op('gpsimd', 'sg', False, deps,
                        lambda h, n=name, a=args, k=kw: getattr(h, n)(*a, **k))

    def snap(self):
        return [(s, c) for s, c in self.cnt.items() if c > 0]

    def emit(self, block, sems):
        for eng in ENGINES:
            items = self.streams[eng]
            if not items:
                continue

            def mk(items=items):
                def body(h):
                    hw = {}
                    for fn, dd, sem, amt in items:
                        for s, c in sorted(dd.items()):
                            if c > hw.get(s, 0):
                                h.wait_ge(sems[s], c)
                                hw[s] = c
                        inst = fn(h)
                        if sem is not None:
                            inst.then_inc(sems[sem], amt)
                return body
            getattr(block, eng)(mk())


def build_program(mode='full'):
    nc = bass.Bass('TRN2', target_bir_lowering=False)
    x_in = nc.dram_tensor('x', [T, D], F32, kind='ExternalInput').ap()
    w1a = nc.dram_tensor('w1a', [D, FF], F32, kind='ExternalInput').ap()
    b1a = nc.dram_tensor('b1a', [1, FF], F32, kind='ExternalInput').ap()
    w2a = nc.dram_tensor('w2a', [D, D], F32, kind='ExternalInput').ap()
    b2a = nc.dram_tensor('b2a', [1, D], F32, kind='ExternalInput').ap()
    gam = nc.dram_tensor('gam', [1, D], F32, kind='ExternalInput').ap()
    bet = nc.dram_tensor('bet', [1, D], F32, kind='ExternalInput').ap()
    wr = nc.dram_tensor('wr', [D, E], F32, kind='ExternalInput').ap()
    w1m = nc.dram_tensor('w1m', [E, D, FF], F32, kind='ExternalInput').ap()
    b1m = nc.dram_tensor('b1m', [E, FF], F32, kind='ExternalInput').ap()
    w2m = nc.dram_tensor('w2m', [E, FF, D], F32, kind='ExternalInput').ap()
    b2m = nc.dram_tensor('b2m', [E, D], F32, kind='ExternalInput').ap()
    out = nc.dram_tensor('out', [T, D], F32, kind='ExternalOutput').ap()

    x_fm = nc.dram_tensor('x_fm', [D, T], F32).ap()
    ln_fm = nc.dram_tensor('ln_fm', [D, T], F32).ap()
    lnh_fm = nc.dram_tensor('lnh_fm', [D, T], BF16).ap()
    lnl_fm = nc.dram_tensor('lnl_fm', [D, T], BF16).ap()
    y_acc = nc.dram_tensor('y_acc', [D, T], F32).ap()
    proj_fm = nc.dram_tensor('proj_fm', [3 * D, T], F32).ap()
    projh = nc.dram_tensor('projh', [2 * D, T], BF16).ap()
    projl = nc.dram_tensor('projl', [2 * D, T], BF16).ap()
    w1ah = nc.dram_tensor('w1ah', [D, FF], BF16).ap()
    w1al = nc.dram_tensor('w1al', [D, FF], BF16).ap()
    vh_tm = nc.dram_tensor('vh_tm', [T, D], BF16).ap()
    vl_tm = nc.dram_tensor('vl_tm', [T, D], BF16).ap()
    avu_fm = nc.dram_tensor('avu_fm', [D, T], F32).ap()

    w1rows = w1m.rearrange('e d (q c) -> (e d q) c', c=1024)
    w2rows = w2m.rearrange('e f d -> (e f) d')

    def fmc(dram, t0, w):
        return dram[:, t0:t0 + w].rearrange('(c p) w -> p c w', p=128)

    def sb3(ap, w):
        return ap.rearrange('p (c w) -> p c w', w=w)

    P = Prog()
    uid = [0]

    def sbt(sc_, name, shape, dtype=F32):
        uid[0] += 1
        return sc_.enter_context(
            nc.sbuf_tensor(f'{name}_{uid[0]}', shape, dtype)).ap()

    def pst(sc_, name, shape):
        uid[0] += 1
        return sc_.enter_context(
            nc.psum_tensor(f'{name}_{uid[0]}', shape, F32)).ap()

    def mm3(out, sh, sl_, mh, ml, first, last, deps=(), inc=False):
        P.mm(out, sh, mh, first, False, deps=deps)
        P.mm(out, sh, ml, False, False, deps=deps)
        return P.mm(out, sl_, mh, False, last, deps=deps, inc=inc)

    es = contextlib.ExitStack()
    sems = {s: es.enter_context(nc.semaphore(s)) for s in P.cnt}

    ident = nc.alloc_sbuf_tensor('ident', [128, 128], F32).ap()
    iota128 = nc.alloc_sbuf_tensor('iota128', [128, 1], F32).ap()
    iota4 = nc.alloc_sbuf_tensor('iota4', [128, 1], F32).ap()
    iota8 = nc.alloc_sbuf_tensor('iota8', [1, 8], F32).ap()
    ones_row = nc.alloc_sbuf_tensor('ones_row', [1, 128], F32).ap()
    ones_col = nc.alloc_sbuf_tensor('ones_col', [128, 1], F32).ap()
    pooled = nc.alloc_sbuf_tensor('pooled', [128, 16], F32).ap()
    gcol = nc.alloc_sbuf_tensor('gcol', [128, 4], F32).ap()
    b1cols = nc.alloc_sbuf_tensor('b1cols', [128, 64], F32).ap()
    b2comb = nc.alloc_sbuf_tensor('b2comb', [128, 8], F32).ap()
    idxw1 = nc.alloc_sbuf_tensor('idxw1', [128, 64], U32).ap()
    idxw2 = nc.alloc_sbuf_tensor('idxw2', [128, 64], U32).ap()

    c0 = P.g('memset', ident[:], 0.0)
    c1 = P.g('affine_select', out=ident[:], in_=ident[:],
             compare_op=OP.not_equal, fill=1.0, base=0,
             pattern=[[-1, 128]], channel_multiplier=1, deps=[c0])
    c2 = P.g('iota', iota128[:], [[0, 1]], channel_multiplier=1,
             allow_small_or_imprecise_dtypes=True, deps=[c1])
    c3 = P.g('iota', iota8[:], [[1, 8]], channel_multiplier=0,
             allow_small_or_imprecise_dtypes=True, deps=[c2])
    c4 = P.g('memset', ones_row[:], 1.0, deps=[c3])
    c5 = P.g('memset', ones_col[:], 1.0, deps=[c4])
    c6 = P.v('tensor_scalar', iota4[:], iota128[:], 4.0, None, OP.mult,
             deps=[c5])
    P.set_base([c6])

    def phase_in_transpose():
        sc = contextlib.ExitStack()
        stage = [sbt(sc, f'p0stg{i}', [128, T])
                 for i in range(DC)]
        xin = [sbt(sc, f'p0xin{i}', [128, D])
               for i in range(2)]
        ptr = [pst(sc, f'p0ps{i}', [128, 128])
               for i in range(2)]
        xin_free = [None, None]
        ptr_free = [None, None]
        last_cp = None
        i = 0
        for tt in range(NTT):
            b = tt % 2
            ld = P.load(xin[b][:], x_in[tt * 128:(tt + 1) * 128, :],
                        deps=[xin_free[b]])
            for dc in range(DC):
                pb = i % 2
                tr = P.transp(ptr[pb][:], xin[b][:, dc * 128:(dc + 1) * 128],
                              ident[:], deps=[ld, ptr_free[pb]])
                cp = P.v('tensor_copy', stage[dc][:, tt * 128:(tt + 1) * 128],
                         ptr[pb][:], deps=[tr])
                ptr_free[pb] = cp
                last_cp = cp
                i += 1
            xin_free[b] = last_cp
        for dc in range(DC):
            P.store(x_fm[dc * 128:(dc + 1) * 128, :], stage[dc][:],
                    deps=[last_cp])
        sc.close()

    def phase_ln(src, dst, want_pooled):
        sc = contextlib.ExitStack()
        xs = [sbt(sc, f'lnxs{i}', [128, DC * 512])
              for i in range(2)]
        xq = [sbt(sc, f'lnxq{i}', [128, DC * 512])
              for i in range(2)]
        tp = [sbt(sc, f'lntp{i}', [128, DC * 512])
              for i in range(2)]
        lo = [sbt(sc, f'lnlo{i}', [128, DC * 512])
              for i in range(2)]
        lhh = [sbt(sc, f'lnhh{i}', [128, DC * 512], BF16) for i in range(2)]
        lll = [sbt(sc, f'lnll{i}', [128, DC * 512], BF16) for i in range(2)]
        mu_s = sbt(sc, 'ln_mu', [1, 512])
        ms_s = sbt(sc, 'ln_ms', [1, 512])
        m2_s = sbt(sc, 'ln_m2', [1, 512])
        va_s = sbt(sc, 'ln_va', [1, 512])
        sd_s = sbt(sc, 'ln_sd', [1, 512])
        rs_s = sbt(sc, 'ln_rs', [1, 512])
        plr = sbt(sc, 'ln_plr', [128, 1])
        p_sum = pst(sc, 'ln_psum', [1, 512])
        p_sq = pst(sc, 'ln_psq', [1, 512])
        p_mu = pst(sc, 'ln_pmu', [128, 512])
        p_rs = pst(sc, 'ln_prs', [128, 512])

        xs_free = [None, None]
        xq_free = [None, None]
        tp_free = [None, None]
        lo_free = [None, None]
        lh_free = [None, None]
        ll_free = [None, None]
        war_psum = war_psq = war_pmu = war_prs = None
        pool_par = [0] * DC
        pool_tok = [None] * DC
        plr_free = None
        if want_pooled:
            z0 = P.v('memset', pooled[:], 0.0)
            pool_tok = [z0] * DC
        for tb in range(TBN):
            b = tb % 2
            ld = P.load(sb3(xs[b][:], 512), fmc(src, tb * 512, 512),
                        deps=[xs_free[b]])
            sq = None
            for dc in range(DC):
                sq = P.act(xq[b][:, dc * 512:(dc + 1) * 512],
                           xs[b][:, dc * 512:(dc + 1) * 512], AF.Square,
                           deps=[ld, xq_free[b]])
            tsum = tsq = None
            for dc in range(DC):
                tsum = P.mm(p_sum[:], ones_col[:],
                            xs[b][:, dc * 512:(dc + 1) * 512],
                            dc == 0, dc == DC - 1,
                            deps=[ld, war_psum], inc=(dc == DC - 1))
            for dc in range(DC):
                tsq = P.mm(p_sq[:], ones_col[:],
                           xq[b][:, dc * 512:(dc + 1) * 512],
                           dc == 0, dc == DC - 1,
                           deps=[sq, war_psq], inc=(dc == DC - 1))
            u1 = P.v('tensor_scalar', mu_s[:], p_sum[:], 1.0 / D, None, OP.mult,
                     deps=[tsum])
            u2 = P.v('tensor_scalar', ms_s[:], p_sq[:], 1.0 / D, EPS,
                     OP.mult, OP.add, deps=[tsq])
            u3 = P.v('tensor_tensor', m2_s[:], mu_s[:], mu_s[:], OP.mult,
                     deps=[u1])
            u4 = P.v('tensor_tensor', va_s[:], ms_s[:], m2_s[:], OP.subtract,
                     deps=[u2, u3])
            u5 = P.act(sd_s[:], va_s[:], AF.Sqrt, deps=[u4])
            u6 = P.v('reciprocal', rs_s[:], sd_s[:], deps=[u5])
            war_psum = u1
            war_psq = u2
            bmu = P.mm(p_mu[:], ones_row[:], mu_s[:], True, True,
                       deps=[u3, war_pmu], inc=True)
            brs = P.mm(p_rs[:], ones_row[:], rs_s[:], True, True,
                       deps=[u6, war_prs], inc=True)
            last_n = None
            for dc in range(DC):
                sl = slice(dc * 512, (dc + 1) * 512)
                n1 = P.v('tensor_tensor', tp[b][:, sl], xs[b][:, sl], p_mu[:],
                         OP.subtract, deps=[bmu, ld, tp_free[b]])
                n2 = P.v('tensor_tensor', lo[b][:, sl], tp[b][:, sl], p_rs[:],
                         OP.mult, deps=[n1, brs, lo_free[b]])
                last_n = n2
                if want_pooled:
                    r1 = P.v('tensor_reduce', plr[:], lo[b][:, sl], AX.X,
                             OP.add, deps=[n2, plr_free])
                    par = pool_par[dc]
                    r2 = P.v('tensor_tensor',
                             pooled[:, dc * 2 + 1 - par:dc * 2 + 2 - par],
                             pooled[:, dc * 2 + par:dc * 2 + par + 1],
                             plr[:], OP.add, deps=[r1, pool_tok[dc]])
                    pool_par[dc] = 1 - par
                    pool_tok[dc] = r2
                    plr_free = r2
            war_pmu = last_n
            war_prs = last_n
            st = P.store(fmc(dst, tb * 512, 512), sb3(lo[b][:], 512),
                         deps=[last_n])
            hsplit = None
            for dc in range(DC):
                sl = slice(dc * 512, (dc + 1) * 512)
                s1 = P.v('tensor_copy', lhh[b][:, sl], lo[b][:, sl],
                         deps=[last_n, lh_free[b]])
                hsplit = P.v('tensor_tensor', lll[b][:, sl], lo[b][:, sl],
                             lhh[b][:, sl], OP.subtract, deps=[s1, ll_free[b]])
            sth = P.store(fmc(lnh_fm, tb * 512, 512), sb3(lhh[b][:], 512),
                          deps=[hsplit])
            stl = P.store(fmc(lnl_fm, tb * 512, 512), sb3(lll[b][:], 512),
                          deps=[hsplit])
            lh_free[b] = sth
            ll_free[b] = stl
            xs_free[b] = last_n
            xq_free[b] = tsq
            tp_free[b] = last_n
            lo_free[b] = hsplit
        sc.close()
        pool_cols = [dc * 2 + pool_par[dc] for dc in range(DC)]
        return pool_cols, pool_tok

    def phase_routing(pool_cols, pool_tok, split_w1a=False):
        sc = contextlib.ExitStack()
        if split_w1a:
            phase_w1a_split(sc)
        wr_sb = sbt(sc, 'rt_wr', [128, 64])
        lg = sbt(sc, 'rt_lg', [1, 8])
        lg2 = sbt(sc, 'rt_lg2', [1, 8])
        mk1 = sbt(sc, 'rt_mk1', [1, 8])
        mk2 = sbt(sc, 'rt_mk2', [1, 8])
        m1 = sbt(sc, 'rt_m1', [1, 1])
        m2 = sbt(sc, 'rt_m2', [1, 1])
        ddf = sbt(sc, 'rt_dd', [1, 1])
        dsc_ = sbt(sc, 'rt_dsc', [1, 1])
        e2v = sbt(sc, 'rt_e2', [1, 1])
        den = sbt(sc, 'rt_den', [1, 1])
        rec = sbt(sc, 'rt_rec', [1, 1])
        w2v = sbt(sc, 'rt_w2v', [1, 1])
        tm8a = sbt(sc, 'rt_tm8a', [1, 8])
        tm8b = sbt(sc, 'rt_tm8b', [1, 8])
        i1 = sbt(sc, 'rt_i1', [1, 1])
        i2 = sbt(sc, 'rt_i2', [1, 1])
        pack = sbt(sc, 'rt_pack', [1, 4])
        stripe = sbt(sc, 'rt_eb', [1, 2])
        ebf = sbt(sc, 'rt_ebf', [2, 1])
        ebu = sbt(sc, 'rt_ebu', [2, 1], U32)
        b1s = sbt(sc, 'rt_b1s', [2, FF])
        b2s = sbt(sc, 'rt_b2s', [2, D])
        b2c = sbt(sc, 'rt_b2c', [128, 16])
        tcol = sbt(sc, 'rt_tcol', [128, 1])
        fst = [sbt(sc, f'rt_fst{i}', [128, 1])
               for i in range(2)]
        base1 = [sbt(sc, f'rt_ba{k}', [128, 1])
                 for k in range(2)]
        base2 = [sbt(sc, f'rt_bb{k}', [128, 1])
                 for k in range(2)]
        p_lg = pst(sc, 'rt_plg', [1, 8])
        p_g = pst(sc, 'rt_pg', [128, 4])
        p_eb = pst(sc, 'rt_peb', [2, 1])
        p_t = [pst(sc, f'rt_pt{i}', [128, 2])
               for i in range(2)]

        wl = P.load(sb3(wr_sb[:], 8), wr.rearrange('(c p) e -> p c e', p=128))
        tl = None
        for dc in range(DC):
            col = pool_cols[dc]
            tl = P.mm(p_lg[:], pooled[:, col:col + 1],
                      wr_sb[:, dc * 8:(dc + 1) * 8], dc == 0, dc == DC - 1,
                      deps=[wl, pool_tok[dc]], inc=(dc == DC - 1))
        a0 = P.v('tensor_copy', lg[:], p_lg[:], deps=[tl])
        a1 = P.v('reduce_max', m1[:], lg[:], AX.X, deps=[a0])
        a2 = P.v('tensor_scalar', mk1[:], lg[:], m1[:], None, OP.is_ge,
                 deps=[a1])
        a3 = P.v('scalar_tensor_tensor', out=lg2[:], in0=mk1[:], scalar=-1e30,
                 in1=lg[:], op0=OP.mult, op1=OP.add, deps=[a2])
        a4 = P.v('reduce_max', m2[:], lg2[:], AX.X, deps=[a3])
        a5 = P.v('tensor_scalar', mk2[:], lg2[:], m2[:], None, OP.is_ge,
                 deps=[a4])
        a6 = P.v('tensor_tensor', ddf[:], m2[:], m1[:], OP.subtract,
                 deps=[a4, a1])
        a7 = P.v('tensor_scalar', dsc_[:], ddf[:], 1.0 / T, None, OP.mult,
                 deps=[a6])
        a8 = P.act(e2v[:], dsc_[:], AF.Exp, deps=[a7])
        a9 = P.v('tensor_scalar', den[:], e2v[:], 1.0, None, OP.add, deps=[a8])
        a10 = P.v('reciprocal', rec[:], den[:], deps=[a9])
        a11 = P.v('tensor_tensor', w2v[:], e2v[:], rec[:], OP.mult,
                  deps=[a8, a10])
        a12 = P.v('tensor_scalar', pack[:, 0:1], rec[:], 0.5, None, OP.mult,
                  deps=[a10])
        a13 = P.v('tensor_scalar', pack[:, 1:2], w2v[:], 0.5, None, OP.mult,
                  deps=[a11])
        a14 = P.v('tensor_tensor', tm8a[:], mk1[:], iota8[:], OP.mult,
                  deps=[a2])
        a15 = P.v('tensor_reduce', i1[:], tm8a[:], AX.X, OP.add, deps=[a14])
        a16 = P.v('tensor_tensor', tm8b[:], mk2[:], iota8[:], OP.mult,
                  deps=[a5])
        a17 = P.v('tensor_reduce', i2[:], tm8b[:], AX.X, OP.add, deps=[a16])
        a18 = P.v('tensor_copy', pack[:, 2:3], i1[:], deps=[a15])
        a19 = P.v('tensor_copy', pack[:, 3:4], i2[:], deps=[a17])
        bg = P.mm(p_g[:], ones_row[:], pack[:], True, True,
                  deps=[a12, a13, a18, a19], inc=True)
        g0 = P.v('tensor_copy', gcol[:], p_g[:], deps=[bg])
        s0 = P.v('tensor_copy', stripe[:, 0:1], i1[:], deps=[a15])
        s1 = P.v('tensor_copy', stripe[:, 1:2], i2[:], deps=[a17])
        te = P.transp(p_eb[:], stripe[:], ident[0:1, 0:1], deps=[s0, s1])
        s2 = P.v('tensor_copy', ebf[:], p_eb[:], deps=[te])
        s3 = P.v('tensor_copy', ebu[:], ebf[:], deps=[s2])
        gb1 = P.gather(b1s[:], b1m, ebu[:], deps=[s3])
        gb2 = P.gather(b2s[:], b2m, ebu[:], deps=[s3])
        ptf = [te, te]
        b1c_tok = None
        for ft in range(32):
            pb = ft % 2
            tr = P.transp(p_t[pb][:], b1s[0:2, ft * 128:(ft + 1) * 128],
                          ident[0:2, 0:2], deps=[gb1, ptf[pb]])
            cp = P.v('tensor_copy', b1cols[:, ft * 2:ft * 2 + 2], p_t[pb][:],
                     deps=[tr])
            ptf[pb] = cp
            b1c_tok = cp
        b2c_tok = None
        for dc in range(DC):
            pb = dc % 2
            tr = P.transp(p_t[pb][:], b2s[0:2, dc * 128:(dc + 1) * 128],
                          ident[0:2, 0:2], deps=[gb2, ptf[pb]])
            cp = P.v('tensor_copy', b2c[:, dc * 2:dc * 2 + 2], p_t[pb][:],
                     deps=[tr])
            ptf[pb] = cp
            b2c_tok = cp
        b2cb_tok = None
        tcol_tok = None
        for dc in range(DC):
            q1 = P.v('tensor_scalar', tcol[:], b2c[:, dc * 2:dc * 2 + 1],
                     gcol[:, 0:1], None, OP.mult, deps=[b2c_tok, g0, tcol_tok])
            q2 = P.v('scalar_tensor_tensor', out=b2comb[:, dc:dc + 1],
                     in0=b2c[:, dc * 2 + 1:dc * 2 + 2], scalar=gcol[:, 1:2],
                     in1=tcol[:], op0=OP.mult, op1=OP.add, deps=[q1])
            tcol_tok = q2
            b2cb_tok = q2
        idx_tok = None
        fs_tok = [g0, g0]
        for k in range(2):
            bb1 = P.v('scalar_tensor_tensor', out=base1[k],
                      in0=gcol[:, 2 + k:3 + k], scalar=4096.0, in1=iota4[:],
                      op0=OP.mult, op1=OP.add, deps=[g0])
            bb2 = P.v('scalar_tensor_tensor', out=base2[k],
                      in0=gcol[:, 2 + k:3 + k], scalar=4096.0, in1=iota128[:],
                      op0=OP.mult, op1=OP.add, deps=[g0])
            for fq in range(4):
                for dc in range(DC):
                    col = k * 32 + fq * 8 + dc
                    fb_ = col % 2
                    f1 = P.v('tensor_scalar', fst[fb_][:], base1[k],
                             float(512 * dc + fq), None, OP.add,
                             deps=[bb1, fs_tok[fb_]])
                    f2 = P.v('tensor_copy', idxw1[:, col:col + 1], fst[fb_][:],
                             deps=[f1])
                    fs_tok[fb_] = f2
                for fb in range(8):
                    col = k * 32 + fq * 8 + fb
                    fb_ = col % 2
                    f1 = P.v('tensor_scalar', fst[fb_][:], base2[k],
                             float(fq * 1024 + fb * 128), None, OP.add,
                             deps=[bb2, fs_tok[fb_]])
                    f2 = P.v('tensor_copy', idxw2[:, col:col + 1], fst[fb_][:],
                             deps=[f1])
                    fs_tok[fb_] = f2
                    idx_tok = f2
        sc.close()
        return dict(gcol=g0, b1c=b1c_tok, b2cb=b2cb_tok, idx=idx_tok)

    def phase_ffn(rt, first_call):
        sc = contextlib.ExitStack()
        w1h = [sbt(sc, f'ff_w1h{i}', [128, DC * 1024], BF16) for i in range(2)]
        w1l = [sbt(sc, f'ff_w1l{i}', [128, DC * 1024], BF16) for i in range(2)]
        w2h = sbt(sc, 'ff_w2h', [128, 8 * 1024], BF16)
        w2l = sbt(sc, 'ff_w2l', [128, 8 * 1024], BF16)
        wtr = sbt(sc, 'ff_wtr', [128, 4 * 1024])
        lnsh = [sbt(sc, f'ff_lnh{i}', [128, DC * 512], BF16) for i in range(2)]
        lnsl = [sbt(sc, f'ff_lnl{i}', [128, DC * 512], BF16) for i in range(2)]
        hbf = sbt(sc, 'ff_h', [128, 8 * 512])
        hh = sbt(sc, 'ff_hh', [128, 8 * 512], BF16)
        hl = sbt(sc, 'ff_hl', [128, 8 * 512], BF16)
        ysb = [sbt(sc, f'ff_ysb{i}', [128, 4 * 512]) for i in range(2)]
        ph = [pst(sc, f'ff_ph{i}', [128, 512]) for i in range(2)]
        py = [pst(sc, f'ff_py{i}', [128, 512]) for i in range(4)]

        w1_tok = [None, None]
        w1_free = [None, None]
        w2_tok = None
        w2_free = None
        wtr_free = None
        lns_free = [None, None, None, None]
        h_tok = [None] * 8
        h_free = None
        hs_tok = [None] * 8
        hs_free = None
        ph_free = [None, None]
        py_free = [None] * 4
        ysb_free = [None, None]
        y_chain = [None] * TBN

        iters = [(k, fq) for k in range(2) for fq in range(4)]

        def gather_split(rows_ap, idx_cols, dsth, dstl, nblk, blkw, dep0):
            nonlocal wtr_free
            sp = None
            half = 4096 // blkw
            for r0 in range(0, nblk, half):
                gts = []
                for j in range(half):
                    blk = r0 + j
                    gts.append(P.gather(
                        wtr[:, j * blkw:(j + 1) * blkw], rows_ap,
                        idx_cols[blk], deps=[rt['idx'], wtr_free, dep0]))
                gt = gts[-1]
                for j in range(half):
                    blk = r0 + j
                    s1 = P.v('tensor_copy',
                             dsth[:, blk * blkw:(blk + 1) * blkw],
                             wtr[:, j * blkw:(j + 1) * blkw], deps=[gt])
                    sp = P.v('tensor_tensor',
                             dstl[:, blk * blkw:(blk + 1) * blkw],
                             wtr[:, j * blkw:(j + 1) * blkw],
                             dsth[:, blk * blkw:(blk + 1) * blkw],
                             OP.subtract, deps=[s1])
                wtr_free = sp
            return sp

        def issue_w1(it_i):
            k, fq = iters[it_i]
            slot = it_i % 2
            idxs = [idxw1[:, k * 32 + fq * 8 + dc:k * 32 + fq * 8 + dc + 1]
                    for dc in range(DC)]
            w1_tok[slot] = gather_split(w1rows, idxs, w1h[slot], w1l[slot],
                                        DC, 1024, w1_free[slot])

        issue_w1(0)
        for it_i, (k, fq) in enumerate(iters):
            slot = it_i % 2
            idxs = [idxw2[:, k * 32 + fq * 8 + fb:k * 32 + fq * 8 + fb + 1]
                    for fb in range(8)]
            w2_tok = gather_split(w2rows, idxs, w2h, w2l, 8, 1024, w2_free)
            if it_i + 1 < len(iters):
                issue_w1(it_i + 1)
            for tb in range(TBN):
                lb = tb % 2
                ldh = P.load(sb3(lnsh[lb][:], 512), fmc(lnh_fm, tb * 512, 512),
                             deps=[lns_free[lb * 2]])
                ldl = P.load(sb3(lnsl[lb][:], 512), fmc(lnl_fm, tb * 512, 512),
                             deps=[lns_free[lb * 2 + 1]])
                for ft in range(8):
                    pb = ft % 2
                    g = None
                    for dc in range(DC):
                        g = mm3(ph[pb][:],
                                w1h[slot][:, dc * 1024 + ft * 128:
                                           dc * 1024 + (ft + 1) * 128],
                                w1l[slot][:, dc * 1024 + ft * 128:
                                           dc * 1024 + (ft + 1) * 128],
                                lnsh[lb][:, dc * 512:(dc + 1) * 512],
                                lnsl[lb][:, dc * 512:(dc + 1) * 512],
                                dc == 0, dc == DC - 1,
                                deps=[w1_tok[slot], ldh, ldl, ph_free[pb]],
                                inc=(dc == DC - 1))
                    sl = slice(ft * 512, (ft + 1) * 512)
                    h_tok[ft] = P.act(hbf[:, sl], ph[pb][:], AF.Gelu,
                                      bias=b1cols[:, (fq * 8 + ft) * 2 + k:
                                                  (fq * 8 + ft) * 2 + k + 1],
                                      deps=[g, rt['b1c'], h_free])
                    s1 = P.v('tensor_copy', hh[:, sl], hbf[:, sl],
                             deps=[h_tok[ft], hs_free])
                    hs_tok[ft] = P.v('tensor_tensor', hl[:, sl], hbf[:, sl],
                                     hh[:, sl], OP.subtract, deps=[s1])
                    ph_free[pb] = h_tok[ft]
                for grp in range(2):
                    gt = None
                    for fb in range(8):
                        for dtl in range(4):
                            dt = grp * 4 + dtl
                            gt = mm3(py[dtl][:],
                                     w2h[:, fb * 1024 + dt * 128:
                                          fb * 1024 + (dt + 1) * 128],
                                     w2l[:, fb * 1024 + dt * 128:
                                          fb * 1024 + (dt + 1) * 128],
                                     hh[:, fb * 512:(fb + 1) * 512],
                                     hl[:, fb * 512:(fb + 1) * 512],
                                     fb == 0, fb == 7,
                                     deps=[hs_tok[fb], w2_tok, py_free[dtl]],
                                     inc=(fb == 7 and dtl == 3))
                    yb = ysb[grp]
                    ev = None
                    for dtl in range(4):
                        ev = P.v('tensor_scalar',
                                 yb[:, dtl * 512:(dtl + 1) * 512], py[dtl][:],
                                 gcol[:, k:k + 1], None, OP.mult,
                                 deps=[gt, rt['gcol'], ysb_free[grp]])
                        py_free[dtl] = ev
                    dst = y_acc[grp * 512:(grp + 1) * 512,
                                tb * 512:(tb + 1) * 512].rearrange(
                                    '(d p) w -> d p w', p=128).transpose([1, 0, 2])
                    srcv = yb[:].rearrange('p (d w) -> p d w', w=512)
                    y_chain[tb] = P.gstore(dst, srcv,
                                           deps=[ev, y_chain[tb]],
                                           accum=(it_i != 0))
                    ysb_free[grp] = y_chain[tb]
                h_free = gt
                hs_free = gt
                lns_free[lb * 2] = gt
                lns_free[lb * 2 + 1] = gt
            w1_free[slot] = P.snap_tensor_token()
            w2_free = w1_free[slot]

        sc.close()
        P.set_base(P.snap())
        sc = contextlib.ExitStack()
        ysb2 = [sbt(sc, f'fc_y{i}', [128, DC * 512])
                for i in range(2)]
        xsb2 = [sbt(sc, f'fc_x{i}', [128, DC * 512])
                for i in range(2)]
        tsb2 = [sbt(sc, f'fc_t{i}', [128, DC * 512])
                for i in range(2)]
        y2f = [None, None]
        x2f = [None, None]
        t2f = [None, None]
        for tb in range(TBN):
            b = tb % 2
            ly = P.load(sb3(ysb2[b][:], 512), fmc(y_acc, tb * 512, 512),
                        deps=[y2f[b], y_chain[tb]])
            lx = P.load(sb3(xsb2[b][:], 512), fmc(x_fm, tb * 512, 512),
                        deps=[x2f[b]])
            last = None
            for dc in range(DC):
                sl = slice(dc * 512, (dc + 1) * 512)
                q1 = P.v('tensor_scalar', tsb2[b][:, sl], ysb2[b][:, sl],
                         b2comb[:, dc:dc + 1], None, OP.add,
                         deps=[ly, rt['b2cb'], t2f[b]])
                q2 = P.v('tensor_tensor', ysb2[b][:, sl], tsb2[b][:, sl],
                         xsb2[b][:, sl], OP.add, deps=[q1, lx])
                last = q2
            st = P.store(fmc(x_fm, tb * 512, 512), sb3(ysb2[b][:], 512),
                         deps=[last])
            y2f[b] = st
            x2f[b] = last
            t2f[b] = last
        sc.close()

    def first_call_iter(it_i):
        return it_i == 0

    def snap_tensor_token():
        c = P.cnt['st']
        return ('st', c) if c > 0 else None
    P.snap_tensor_token = snap_tensor_token

    def phase_w1a_split(sc):
        wf = [sbt(sc, f'ws_f{i}', [128, DC * 512]) for i in range(2)]
        wh = [sbt(sc, f'ws_h{i}', [128, DC * 512], BF16) for i in range(2)]
        wl = [sbt(sc, f'ws_l{i}', [128, DC * 512], BF16) for i in range(2)]
        f_free = [None, None]
        h_free = [None, None]
        for cc in range(8):
            b = cc % 2
            ld = P.load(sb3(wf[b][:], 512),
                        w1a[:, cc * 512:(cc + 1) * 512].rearrange(
                            '(c p) w -> p c w', p=128), deps=[f_free[b]])
            sp = None
            for dc in range(DC):
                sl = slice(dc * 512, (dc + 1) * 512)
                s1 = P.v('tensor_copy', wh[b][:, sl], wf[b][:, sl],
                         deps=[ld, h_free[b]])
                sp = P.v('tensor_tensor', wl[b][:, sl], wf[b][:, sl],
                         wh[b][:, sl], OP.subtract, deps=[s1])
            sth = P.store(w1ah[:, cc * 512:(cc + 1) * 512].rearrange(
                              '(c p) w -> p c w', p=128), sb3(wh[b][:], 512),
                          deps=[sp])
            stl = P.store(w1al[:, cc * 512:(cc + 1) * 512].rearrange(
                              '(c p) w -> p c w', p=128), sb3(wl[b][:], 512),
                          deps=[sp])
            f_free[b] = sp
            h_free[b] = stl

    def phase_xproj():
        sc = contextlib.ExitStack()
        lnsh = [sbt(sc, f'xp_lnh{i}', [128, DC * 512], BF16) for i in range(2)]
        lnsl = [sbt(sc, f'xp_lnl{i}', [128, DC * 512], BF16) for i in range(2)]
        wcth = [sbt(sc, f'xp_wcth{i}', [128, DC * 128], BF16) for i in range(2)]
        wctl = [sbt(sc, f'xp_wctl{i}', [128, DC * 128], BF16) for i in range(2)]
        wuvh = [sbt(sc, f'xp_wuvh{i}', [128, DC * 512], BF16) for i in range(2)]
        wuvl = [sbt(sc, f'xp_wuvl{i}', [128, DC * 512], BF16) for i in range(2)]
        qst = [sbt(sc, f'xp_qst{i}', [128, 512]) for i in range(2)]
        vsh = [sbt(sc, f'xp_vsh{i}', [128, 512], BF16) for i in range(2)]
        vsl = [sbt(sc, f'xp_vsl{i}', [128, 512], BF16) for i in range(2)]
        qsh = [sbt(sc, f'xp_qsh{i}', [128, 512], BF16) for i in range(2)]
        qsl = [sbt(sc, f'xp_qsl{i}', [128, 512], BF16) for i in range(2)]
        ust = [sbt(sc, f'xp_ust{i}', [128, 512]) for i in range(2)]
        vst = [sbt(sc, f'xp_vst{i}', [128, 512]) for i in range(2)]
        b1a_sb = sbt(sc, 'xp_b1a', [1, FF])
        b1f = sbt(sc, 'xp_b1f', [128, 24])
        pp = [pst(sc, f'xp_pp{i}', [128, 512]) for i in range(2)]
        p_t = [pst(sc, f'xp_pt{i}', [128, 1]) for i in range(2)]

        def colof(j):
            return j * 128 if j < 8 else 2048 + (j - 8) * 128

        lb1 = P.load(b1a_sb[:], b1a)
        ptf = [None, None]
        bf_tok = None
        for j in range(24):
            pb = j % 2
            c0 = colof(j)
            tr = P.transp(p_t[pb][:], b1a_sb[0:1, c0:c0 + 128],
                          ident[0:1, 0:1], deps=[lb1, ptf[pb]])
            cp = P.v('tensor_copy', b1f[:, j:j + 1], p_t[pb][:], deps=[tr])
            ptf[pb] = cp
            bf_tok = cp

        lns_free = [None, None]
        wct_tok = [None, None]
        wct_free = [None, None]
        wuv_tok = [None, None]
        wuv_free = [None, None]
        pp_free = [None, None]
        qst_free = [None, None]
        qs_free = [None, None]
        ust_free = [None, None]
        vst_free = [None, None]
        vs_free = [None, None]
        g = None
        for tb in range(TBN):
            lb = tb % 2
            ldh = P.load(sb3(lnsh[lb][:], 512), fmc(lnh_fm, tb * 512, 512),
                         deps=[lns_free[lb]])
            ldl = P.load(sb3(lnsl[lb][:], 512), fmc(lnl_fm, tb * 512, 512),
                         deps=[lns_free[lb]])
            for j in range(24):
                wb = j % 2
                c0 = colof(j)
                wct_tok[wb] = P.load(
                    sb3(wcth[wb][:], 128),
                    w1ah[:, c0:c0 + 128].rearrange('(c p) w -> p c w', p=128),
                    deps=[wct_free[wb]])
                wct_tok2 = P.load(
                    sb3(wctl[wb][:], 128),
                    w1al[:, c0:c0 + 128].rearrange('(c p) w -> p c w', p=128),
                    deps=[wct_free[wb]])
                g = None
                for dc in range(DC):
                    g = mm3(pp[wb][:],
                            wcth[wb][:, dc * 128:(dc + 1) * 128],
                            wctl[wb][:, dc * 128:(dc + 1) * 128],
                            lnsh[lb][:, dc * 512:(dc + 1) * 512],
                            lnsl[lb][:, dc * 512:(dc + 1) * 512],
                            dc == 0, dc == DC - 1,
                            deps=[wct_tok[wb], wct_tok2, ldh, ldl,
                                  pp_free[wb]],
                            inc=(dc == DC - 1))
                sl = P.act(qst[wb][:], pp[wb][:], AF.Silu,
                           bias=b1f[:, j:j + 1], deps=[g, bf_tok, qst_free[wb]])
                if j < 8:
                    st = P.store(proj_fm[j * 128:(j + 1) * 128,
                                         tb * 512:(tb + 1) * 512], qst[wb][:],
                                 deps=[sl])
                    qst_free[wb] = st
                else:
                    r0 = (j - 8) * 128
                    s1 = P.v('tensor_copy', qsh[wb][:], qst[wb][:],
                             deps=[sl, qs_free[wb]])
                    s2 = P.v('tensor_tensor', qsl[wb][:], qst[wb][:],
                             qsh[wb][:], OP.subtract, deps=[s1])
                    sth = P.store(projh[r0:r0 + 128,
                                        tb * 512:(tb + 1) * 512], qsh[wb][:],
                                  deps=[s2])
                    stl = P.store(projl[r0:r0 + 128,
                                        tb * 512:(tb + 1) * 512], qsl[wb][:],
                                  deps=[s2])
                    qst_free[wb] = s2
                    qs_free[wb] = stl
                pp_free[wb] = sl
                wct_free[wb] = g
            for ci, cc in enumerate((2, 3)):
                wb = ci % 2
                wuv_tok[wb] = P.load(
                    sb3(wuvh[wb][:], 512),
                    w1ah[:, cc * 512:(cc + 1) * 512].rearrange(
                        '(c p) w -> p c w', p=128),
                    deps=[wuv_free[wb]])
                wuv_tok2 = P.load(
                    sb3(wuvl[wb][:], 512),
                    w1al[:, cc * 512:(cc + 1) * 512].rearrange(
                        '(c p) w -> p c w', p=128),
                    deps=[wuv_free[wb]])
                for t4 in range(4):
                    pb = (ci * 4 + t4) % 2
                    g = None
                    for dc in range(DC):
                        g = mm3(pp[pb][:],
                                lnsh[lb][:, dc * 512 + t4 * 128:
                                        dc * 512 + (t4 + 1) * 128],
                                lnsl[lb][:, dc * 512 + t4 * 128:
                                        dc * 512 + (t4 + 1) * 128],
                                wuvh[wb][:, dc * 512:(dc + 1) * 512],
                                wuvl[wb][:, dc * 512:(dc + 1) * 512],
                                dc == 0, False,
                                deps=[wuv_tok[wb], wuv_tok2, ldh, ldl,
                                      pp_free[pb]])
                    g = P.mm(pp[pb][:], ones_row[:],
                             b1a_sb[0:1, cc * 512:(cc + 1) * 512],
                             False, True, deps=[lb1], inc=True)
                    row0 = tb * 512 + t4 * 128
                    sl = P.act(ust[pb][:], pp[pb][:], AF.Silu,
                               deps=[g, ust_free[pb]])
                    sc_ = P.v('tensor_scalar', vst[pb][:], ust[pb][:],
                              1.0 / T, None, OP.mult,
                              deps=[sl, vst_free[pb]])
                    v1 = P.v('tensor_copy', vsh[pb][:], vst[pb][:],
                             deps=[sc_, vs_free[pb]])
                    v2 = P.v('tensor_tensor', vsl[pb][:], vst[pb][:],
                             vsh[pb][:], OP.subtract, deps=[v1])
                    c0v = (cc - 2) * 512
                    sth = P.store(vh_tm[row0:row0 + 128, c0v:c0v + 512],
                                  vsh[pb][:], deps=[v2])
                    stl = P.store(vl_tm[row0:row0 + 128, c0v:c0v + 512],
                                  vsl[pb][:], deps=[v2])
                    ust_free[pb] = sc_
                    vst_free[pb] = v2
                    vs_free[pb] = stl
                    pp_free[pb] = sl
                wuv_free[wb] = g
            lns_free[lb] = g
        sc.close()

    def phase_attn_core():
        sc = contextlib.ExitStack()
        qhh = [sbt(sc, f'at_qh{i}', [128, T], BF16) for i in range(2)]
        qhl = [sbt(sc, f'at_ql{i}', [128, T], BF16) for i in range(2)]
        khh = [sbt(sc, f'at_kh{i}', [128, T], BF16) for i in range(2)]
        khl = [sbt(sc, f'at_kl{i}', [128, T], BF16) for i in range(2)]
        vhh = [sbt(sc, f'at_vh{i}', [128, NTT * 64], BF16) for i in range(2)]
        vhl = [sbt(sc, f'at_vl{i}', [128, NTT * 64], BF16) for i in range(2)]
        uh = [sbt(sc, f'at_u{i}', [64, T]) for i in range(2)]
        att = [[sbt(sc, f'at_a{i}_{j}', [128, 512]) for j in range(2)]
               for i in range(2)]
        atth = [[sbt(sc, f'at_ah{i}_{j}', [128, 512], BF16) for j in range(2)]
                for i in range(2)]
        attl = [[sbt(sc, f'at_al{i}_{j}', [128, 512], BF16) for j in range(2)]
                for i in range(2)]
        avf = [sbt(sc, f'at_avf{i}', [64, T]) for i in range(2)]
        ps = [[pst(sc, f'at_ps{i}_{j}', [128, 512]) for j in range(2)]
              for i in range(2)]
        pav = [pst(sc, f'at_pav{i}', [64, 512]) for i in range(2)]

        zp = None
        for i in range(2):
            for tns in (qhh, qhl, khh, khl):
                zp = P.v('memset', tns[i][64:128, :], 0.0, deps=[zp])
        zp_tok = zp

        set_free = [None, None]
        ps_free = [[None, None], [None, None]]
        pav_free = [None, None]
        att_w = [[None, None], [None, None]]
        avf_free = [None, None]
        for h in range(NH):
            hb = h % 2
            lq1 = P.load(qhh[hb][0:64, :], projh[h * 64:(h + 1) * 64, :],
                         deps=[set_free[hb], zp_tok])
            lq2 = P.load(qhl[hb][0:64, :], projl[h * 64:(h + 1) * 64, :],
                         deps=[set_free[hb], zp_tok])
            lk1 = P.load(khh[hb][0:64, :], projh[D + h * 64:D + (h + 1) * 64, :],
                         deps=[set_free[hb], zp_tok])
            lk2 = P.load(khl[hb][0:64, :], projl[D + h * 64:D + (h + 1) * 64, :],
                         deps=[set_free[hb], zp_tok])
            lv1 = P.load(vhh[hb][:].rearrange('p (t w) -> p t w', w=64),
                         vh_tm[:, h * 64:(h + 1) * 64].rearrange(
                             '(t p) w -> p t w', p=128),
                         deps=[set_free[hb]])
            lv2 = P.load(vhl[hb][:].rearrange('p (t w) -> p t w', w=64),
                         vl_tm[:, h * 64:(h + 1) * 64].rearrange(
                             '(t p) w -> p t w', p=128),
                         deps=[set_free[hb]])
            lu = P.load(uh[hb][:], proj_fm[h * 64:(h + 1) * 64, :],
                        deps=[set_free[hb]])
            mu_t = None
            for qp in range(4):
                qcs = (2 * qp, 2 * qp + 1)
                gav = [None, None]
                spl = {}
                for kt in range(NTT + 1):
                    if kt < NTT:
                        kb = kt % 2
                        for s in range(2):
                            qc = qcs[s]
                            sm = mm3(ps[s][kb][:],
                                     khh[hb][:, kt * 128:(kt + 1) * 128],
                                     khl[hb][:, kt * 128:(kt + 1) * 128],
                                     qhh[hb][:, qc * 512:(qc + 1) * 512],
                                     qhl[hb][:, qc * 512:(qc + 1) * 512],
                                     True, True,
                                     deps=[lq1, lq2, lk1, lk2,
                                           ps_free[s][kb]],
                                     inc=True)
                            sl_ = P.act(att[s][kb][:], ps[s][kb][:], AF.Silu,
                                        deps=[sm, att_w[s][kb]])
                            ps_free[s][kb] = sl_
                            a1 = P.v('tensor_copy', atth[s][kb][:],
                                     att[s][kb][:], deps=[sl_])
                            a2 = P.v('tensor_tensor', attl[s][kb][:],
                                     att[s][kb][:], atth[s][kb][:],
                                     OP.subtract, deps=[a1])
                            spl[(s, kt)] = a2
                            att_w[s][kb] = a2
                    if kt > 0:
                        kp = kt - 1
                        kb = kp % 2
                        for s in range(2):
                            gav[s] = mm3(pav[s][:],
                                         vhh[hb][:, kp * 64:(kp + 1) * 64],
                                         vhl[hb][:, kp * 64:(kp + 1) * 64],
                                         atth[s][kb][:], attl[s][kb][:],
                                         kp == 0, kp == NTT - 1,
                                         deps=[spl[(s, kp)], lv1, lv2] +
                                              ([pav_free[s]] if kp == 0
                                               else []),
                                         inc=(kp == NTT - 1))
                            pass
                for s in range(2):
                    qc = qcs[s]
                    mu_t = P.v('tensor_tensor',
                               avf[hb][:, qc * 512:(qc + 1) * 512],
                               pav[s][:],
                               uh[hb][:, qc * 512:(qc + 1) * 512], OP.mult,
                               deps=[gav[s], lu, avf_free[hb]])
                    pav_free[s] = mu_t
            stv = P.store(avu_fm[h * 64:(h + 1) * 64, :], avf[hb][:],
                          deps=[mu_t])
            avf_free[hb] = stv
            set_free[hb] = mu_t

        sc.close()
        return None

    def phase_outproj(stats_tok, dump_z=False):
        sc = contextlib.ExitStack()
        w2gh = sbt(sc, 'op_w2gh', [128, DC * 1024], BF16)
        w2gl = sbt(sc, 'op_w2gl', [128, DC * 1024], BF16)
        wgs = sbt(sc, 'op_wgs', [128, 1024])
        wch = [sbt(sc, f'op_wch{i}', [128, 1024]) for i in range(2)]
        gcols = sbt(sc, 'op_gc', [128, 8])
        bcols = sbt(sc, 'op_bc', [128, 8])
        b2cols = sbt(sc, 'op_b2c', [128, 8])
        biasc = sbt(sc, 'op_bias', [128, 8])
        avs = [sbt(sc, f'op_av{i}', [128, DC * 512]) for i in range(2)]
        avq = [sbt(sc, 'op_aq0', [128, DC * 512])] * 2
        tpz = avq
        zfm = [sbt(sc, f'op_zfm{i}', [128, DC * 512]) for i in range(2)]
        zfh = [sbt(sc, 'op_zfh0', [128, DC * 512], BF16)] * 2
        zfl = [sbt(sc, 'op_zfl0', [128, DC * 512], BF16)] * 2
        xch = [sbt(sc, 'op_xch0', [128, DC * 512])] * 2
        xnb = [sbt(sc, 'op_xnb0', [128, DC * 512])] * 2
        yst = [sbt(sc, 'op_yst0', [128, DC * 512])] * 2
        mu_s = sbt(sc, 'op_mu', [1, 512])
        ms_s = sbt(sc, 'op_ms', [1, 512])
        m2_s = sbt(sc, 'op_m2', [1, 512])
        va_s = sbt(sc, 'op_va', [1, 512])
        sd_s = sbt(sc, 'op_sd', [1, 512])
        rs_s = sbt(sc, 'op_rs', [1, 512])
        p_b = pst(sc, 'op_pb', [128, 8])
        p_y = [pst(sc, f'op_py{i}', [128, 512]) for i in range(2)]
        p_sum = pst(sc, 'op_psum', [1, 512])
        p_sq = pst(sc, 'op_psq', [1, 512])
        p_mu = pst(sc, 'op_pmu', [128, 512])
        p_rs = pst(sc, 'op_prs', [128, 512])

        lg_ = P.load(gcols[:], gam[0].rearrange('(c p) -> p c', p=128),
                     slow=True)
        lb_ = P.load(bcols[:], bet[0].rearrange('(c p) -> p c', p=128),
                     slow=True)
        lb2 = P.load(b2cols[:], b2a[0].rearrange('(c p) -> p c', p=128),
                     slow=True)
        wch_tok = [None, None]
        wch_free = [None, None]
        bias_mm = None
        w2g_tok = None
        for dc in range(DC):
            wb = dc % 2
            wch_tok[wb] = P.load(wch[wb][:], w2a[dc * 128:(dc + 1) * 128, :],
                                 deps=(wch_free[wb] or []))
            for dt in range(DC):
                bias_mm = P.mm(p_b[:, dt:dt + 1],
                               wch[wb][:, dt * 128:(dt + 1) * 128],
                               bcols[:, dc:dc + 1],
                               dc == 0 and dt == 0,
                               dc == DC - 1 and dt == DC - 1,
                               deps=[wch_tok[wb], lb_],
                               inc=(dt == DC - 1))
            wg1 = P.v('tensor_scalar', wgs[:], wch[wb][:],
                      gcols[:, dc:dc + 1], None, OP.mult,
                      deps=[wch_tok[wb], lg_, w2g_tok])
            wg2 = P.v('tensor_copy', w2gh[:, dc * 1024:(dc + 1) * 1024],
                      wgs[:], deps=[wg1])
            w2g_tok = P.v('tensor_tensor', w2gl[:, dc * 1024:(dc + 1) * 1024],
                          wgs[:], w2gh[:, dc * 1024:(dc + 1) * 1024],
                          OP.subtract, deps=[wg2])
            wch_free[wb] = [w2g_tok, bias_mm]
        bias_tok = P.v('tensor_tensor', biasc[:], p_b[:], b2cols[:], OP.add,
                       deps=[bias_mm, lb2])

        av_free = [None, None]
        zs_free = SharedTok()
        aq_free = SharedTok()
        tp_free = SharedTok()
        zfm_free = [None, None]
        py_free = [None, None]
        xch_free = SharedTok()
        xnb_free = SharedTok()
        yst_free = SharedTok()
        war_psum = war_psq = war_pmu = war_prs = None
        for tb in range(TBN):
            zb = tb % 2
            la = P.load(sb3(avs[zb][:], 512), fmc(avu_fm, tb * 512, 512),
                        deps=[av_free[zb]])
            lx = P.load(sb3(xch[zb][:], 512), fmc(x_fm, tb * 512, 512),
                        deps=[xch_free[zb]])
            sq = None
            for dc in range(DC):
                sq = P.act(avq[zb][:, dc * 512:(dc + 1) * 512],
                           avs[zb][:, dc * 512:(dc + 1) * 512], AF.Square,
                           deps=[la, aq_free[zb]])
            tsum = tsq = None
            for dc in range(DC):
                tsum = P.mm(p_sum[:], ones_col[:],
                            avs[zb][:, dc * 512:(dc + 1) * 512],
                            dc == 0, dc == DC - 1,
                            deps=[la, war_psum], inc=(dc == DC - 1))
            for dc in range(DC):
                tsq = P.mm(p_sq[:], ones_col[:],
                           avq[zb][:, dc * 512:(dc + 1) * 512],
                           dc == 0, dc == DC - 1,
                           deps=[sq, war_psq], inc=(dc == DC - 1))
            u1 = P.v('tensor_scalar', mu_s[:], p_sum[:], 1.0 / D, None,
                     OP.mult, deps=[tsum])
            u2 = P.v('tensor_scalar', ms_s[:], p_sq[:], 1.0 / D, EPS,
                     OP.mult, OP.add, deps=[tsq])
            u3 = P.v('tensor_tensor', m2_s[:], mu_s[:], mu_s[:], OP.mult,
                     deps=[u1])
            u4 = P.v('tensor_tensor', va_s[:], ms_s[:], m2_s[:], OP.subtract,
                     deps=[u2, u3])
            u5 = P.act(sd_s[:], va_s[:], AF.Sqrt, deps=[u4])
            u6 = P.v('reciprocal', rs_s[:], sd_s[:], deps=[u5])
            war_psum = u1
            war_psq = u2
            bmu = P.mm(p_mu[:], ones_row[:], mu_s[:], True, True,
                       deps=[u3, war_pmu], inc=True)
            brs = P.mm(p_rs[:], ones_row[:], rs_s[:], True, True,
                       deps=[u6, war_prs], inc=True)
            last_n = None
            for dc in range(DC):
                sl = slice(dc * 512, (dc + 1) * 512)
                n1 = P.v('tensor_tensor', tpz[zb][:, sl], avs[zb][:, sl],
                         p_mu[:], OP.subtract,
                         deps=[bmu, la, tsq, tp_free[zb]])
                n2 = P.v('tensor_tensor', zfm[zb][:, sl], tpz[zb][:, sl],
                         p_rs[:], OP.mult, deps=[n1, brs, zfm_free[zb]])
                z1 = P.v('tensor_copy', zfh[zb][:, sl], zfm[zb][:, sl],
                         deps=[n2, zs_free[0]])
                last_n = P.v('tensor_tensor', zfl[zb][:, sl], zfm[zb][:, sl],
                             zfh[zb][:, sl], OP.subtract,
                             deps=[z1, zs_free[0]])
            war_pmu = last_n
            war_prs = last_n
            if dump_z:
                P.store(fmc(y_acc, tb * 512, 512), sb3(zfm[zb][:], 512),
                        deps=[last_n])
            last_q = None
            for dt in range(DC):
                pb = dt % 2
                g = None
                for dc in range(DC):
                    g = mm3(p_y[pb][:],
                            w2gh[:, dc * 1024 + dt * 128:
                                 dc * 1024 + (dt + 1) * 128],
                            w2gl[:, dc * 1024 + dt * 128:
                                 dc * 1024 + (dt + 1) * 128],
                            zfh[zb][:, dc * 512:(dc + 1) * 512],
                            zfl[zb][:, dc * 512:(dc + 1) * 512],
                            dc == 0, dc == DC - 1,
                            deps=[w2g_tok, last_n, py_free[pb]],
                            inc=(dc == DC - 1))
                sl = slice(dt * 512, (dt + 1) * 512)
                q1 = P.v('tensor_scalar', yst[zb][:, sl], p_y[pb][:],
                         biasc[:, dt:dt + 1], None, OP.add,
                         deps=[g, bias_tok, yst_free[zb]])
                q2 = P.v('tensor_tensor', xnb[zb][:, sl], yst[zb][:, sl],
                         xch[zb][:, sl], OP.add, deps=[q1, lx, xnb_free[zb]])
                py_free[pb] = q1
                last_q = q2
            st = P.store(fmc(x_fm, tb * 512, 512), sb3(xnb[zb][:], 512),
                         deps=[last_q])
            av_free[zb] = last_n
            aq_free[zb] = last_n
            tp_free[zb] = last_n
            zfm_free[zb] = last_n
            zs_free[0] = last_q
            xch_free[zb] = last_q
            xnb_free[zb] = st
            yst_free[zb] = last_q
        sc.close()

    def phase_out_transpose(src):
        sc = contextlib.ExitStack()
        chunk = [sbt(sc, f'pf_ch{i}', [128, T])
                 for i in range(DC)]
        ostg = [sbt(sc, f'pf_os{i}', [128, D])
                for i in range(2)]
        ptr = [pst(sc, f'pf_ps{i}', [128, 128])
               for i in range(2)]
        lds = [P.load(chunk[dc][:], src[dc * 128:(dc + 1) * 128, :])
               for dc in range(DC)]
        ptr_free = [None, None]
        ostg_free = [None, None]
        i = 0
        for tt in range(NTT):
            ob = tt % 2
            last_cp = None
            for dc in range(DC):
                pb = i % 2
                tr = P.transp(ptr[pb][:],
                              chunk[dc][:, tt * 128:(tt + 1) * 128],
                              ident[:], deps=[lds[dc], ptr_free[pb]])
                cp = P.v('tensor_copy', ostg[ob][:, dc * 128:(dc + 1) * 128],
                         ptr[pb][:], deps=[tr, ostg_free[ob]])
                ptr_free[pb] = cp
                last_cp = cp
                i += 1
            st = P.store(out[tt * 128:(tt + 1) * 128, :], ostg[ob][:],
                         deps=[last_cp])
            ostg_free[ob] = st
        sc.close()

    def phase_dump_bf16_fm(srcd):
        sc = contextlib.ExitStack()
        bb = [sbt(sc, f'db_b{i}', [128, DC * 512], BF16) for i in range(2)]
        ff = [sbt(sc, f'db_f{i}', [128, DC * 512]) for i in range(2)]
        bfree = [None, None]
        ffree = [None, None]
        for tb in range(TBN):
            b = tb % 2
            ld = P.load(sb3(bb[b][:], 512), fmc(srcd, tb * 512, 512),
                        deps=[bfree[b]])
            cp = P.v('tensor_copy', ff[b][:], bb[b][:], deps=[ld, ffree[b]])
            st = P.store(fmc(y_acc, tb * 512, 512), sb3(ff[b][:], 512),
                         deps=[cp])
            bfree[b] = cp
            ffree[b] = st
        sc.close()

    def phase_copy_tm(srcd, col0):
        sc = contextlib.ExitStack()
        stg = [sbt(sc, f'cp_st{i}', [128, D]) for i in range(2)]
        free = [None, None]
        for tt in range(NTT):
            b = tt % 2
            ld = P.load(stg[b][:], srcd[tt * 128:(tt + 1) * 128, col0:col0 + D],
                        deps=[free[b]])
            st = P.store(out[tt * 128:(tt + 1) * 128, :], stg[b][:], deps=[ld])
            free[b] = st
        sc.close()

    phase_in_transpose()
    P.set_base(P.snap())
    if mode == 'transpose':
        phase_out_transpose(x_fm)
    else:
        pc, pt = phase_ln(x_fm, ln_fm, True)
        P.set_base(P.snap())
        if mode == 'ln':
            phase_out_transpose(ln_fm)
        else:
            rt = phase_routing(pc, pt, split_w1a=True)
            P.set_base(P.snap())
            if mode in ('w1ahdump', 'w1aldump'):
                phase_dump_bf16_fm(w1ah if mode == 'w1ahdump' else w1al)
                P.set_base(P.snap())
                phase_out_transpose(y_acc)
                P.set_base([])
                P._op('sync', None, False, P.snap(), lambda h: h.nop())
                with nc.Block() as block:
                    P.emit(block, sems)
                es.close()
                return nc
            phase_ffn(rt, True)
            P.set_base(P.snap())
            if mode == 'ffn1':
                phase_out_transpose(x_fm)
            else:
                phase_ln(x_fm, ln_fm, False)
                P.set_base(P.snap())
                phase_xproj()
                P.set_base(P.snap())
                if mode in ('qpart', 'kpart', 'upart', 'vpart'):
                    if mode == 'qpart':
                        phase_out_transpose(proj_fm[D:2 * D, :])
                    elif mode == 'kpart':
                        phase_out_transpose(proj_fm[2 * D:3 * D, :])
                    elif mode == 'upart':
                        phase_out_transpose(proj_fm[0:D, :])
                    else:
                        phase_copy_tm(v_tm, 0)
                    P.set_base([])
                    P._op('sync', None, False, P.snap(), lambda h: h.nop())
                    with nc.Block() as block:
                        P.emit(block, sems)
                    es.close()
                    return nc
                stats_tok = phase_attn_core()
                P.set_base(P.snap())
                if mode == 'avu':
                    phase_out_transpose(avu_fm)
                    P.set_base([])
                    P._op('sync', None, False, P.snap(), lambda h: h.nop())
                    with nc.Block() as block:
                        P.emit(block, sems)
                    es.close()
                    return nc
                phase_outproj(stats_tok, dump_z=(mode == 'zdump'))
                P.set_base(P.snap())
                if mode == 'zdump':
                    phase_out_transpose(y_acc)
                elif mode == 'attn':
                    phase_out_transpose(x_fm)
                else:
                    pc, pt = phase_ln(x_fm, ln_fm, True)
                    P.set_base(P.snap())
                    rt = phase_routing(pc, pt)
                    P.set_base(P.snap())
                    phase_ffn(rt, True)
                    P.set_base(P.snap())
                    phase_out_transpose(x_fm)

    P.set_base([])
    P._op('sync', None, False, P.snap(), lambda h: h.nop())

    with nc.Block() as block:
        P.emit(block, sems)
    es.close()
    return nc


_NC_CACHE = {}


def _get_program(mode):
    if mode not in _NC_CACHE:
        _NC_CACHE[mode] = build_program(mode)
    return _NC_CACHE[mode]


def kernel(L0_attn_x, L0_attn_W1, L0_attn_b1, L0_attn_W2, L0_attn_b2,
           L0_attn_gamma, L0_attn_beta, L0_ffn_Wr, L0_ffn_W1, L0_ffn_b1,
           L0_ffn_W2, L0_ffn_b2, mode='full'):
    f = lambda a: np.ascontiguousarray(np.asarray(a, dtype=np.float32))
    x = f(L0_attn_x)
    shared = dict(
        w1a=f(L0_attn_W1), b1a=f(L0_attn_b1).reshape(1, FF),
        w2a=f(L0_attn_W2), b2a=f(L0_attn_b2).reshape(1, D),
        gam=f(L0_attn_gamma).reshape(1, D), bet=f(L0_attn_beta).reshape(1, D),
        wr=f(L0_ffn_Wr), w1m=f(L0_ffn_W1), b1m=f(L0_ffn_b1),
        w2m=f(L0_ffn_W2), b2m=f(L0_ffn_b2))
    in_maps = [dict(x=np.ascontiguousarray(x[b]), **shared)
               for b in range(N_CORES)]
    nc = _get_program(mode)
    res = run_bass_kernel_spmd(nc, in_maps, core_ids=list(range(N_CORES)))
    return np.stack([res.results[b]['out'] for b in range(N_CORES)], axis=0)
